# version 05
# baseline (speedup 1.0000x reference)
import numpy as np
from contextlib import ExitStack

import concourse.bass as bass
import concourse.mybir as mybir
from concourse.bass_utils import run_bass_kernel_spmd

F32 = mybir.dt.float32
BF16 = mybir.dt.bfloat16
AF = mybir.ActivationFunctionType
ALU = mybir.AluOpType
AX = mybir.AxisListType

S = 2048
D = 1024
NT = S // 128
DT = D // 128


class Ctx:

    def __init__(self):
        self.nc = bass.Bass("TRN2", target_bir_lowering=False)
        self.stack = ExitStack()
        self.sem = {}
        self.cnt = {}
        self.inputs = {}
        self.consts = {}
        self.block = None
        self.uid = 0

    def nid(self):
        self.uid += 1
        return self.uid

    def sget(self, name):
        if name not in self.sem:
            self.sem[name] = self.stack.enter_context(self.nc.semaphore(name))
            self.cnt[name] = 0
        return self.sem[name]

    def inc(self, ins, name, k=1):
        ins.then_inc(self.sget(name), k)
        self.cnt[name] += k
        return self.cnt[name]

    def tok(self, name):
        self.sget(name)
        return self.cnt[name]

    def wait(self, eng, name, val=None):
        eng.wait_ge(self.sget(name), self.cnt[name] if val is None else val)

    def din(self, name, arr, dtype=F32):
        t = self.nc.dram_tensor(name, list(arr.shape), dtype, kind="ExternalInput")
        self.inputs[name] = np.ascontiguousarray(arr)
        return t.ap()

    def dout(self, name, shape, dtype=F32):
        return self.nc.dram_tensor(name, list(shape), dtype, kind="ExternalOutput").ap()

    def sbuf(self, name, shape, dtype, scope=None):
        return (scope or self.stack).enter_context(
            self.nc.sbuf_tensor(f"{name}_{self.nid()}", list(shape), dtype))

    def psum(self, shape, dtype=F32):
        return self.stack.enter_context(
            self.nc.psum_tensor(f"ps_{self.nid()}", list(shape), dtype))



def emit_load_consts(ctx, need_identf=True):
    block = ctx.block
    ident = ctx.sbuf("IDENT", [128, 128], BF16)
    ctx.consts["IDENT"] = ident
    idd = ctx.din("c_ident", np.eye(128, dtype=np.float32))
    if need_identf:
        identf = ctx.sbuf("IDENTF", [128, 128], F32)
        ctx.consts["IDENTF"] = identf
    ones = ctx.sbuf("ONES", [1, 128], BF16)
    ctx.consts["ONES"] = ones
    onesf128 = ctx.sbuf("ONESF128", [1, 128], F32)
    ctx.consts["ONESF128"] = onesf128
    oned = ctx.din("c_ones", np.ones((1, 512), np.float32))

    @block.gpsimd
    def _(g):
        ctx.inc(g.dma_start(out=ident[:], in_=idd[:]), "wdma", 16)
        if need_identf:
            ctx.inc(g.dma_start(out=identf[:], in_=idd[:]), "wdma", 16)
        ctx.inc(g.dma_start(out=ones[:], in_=oned[0:1, 0:128]), "wdma", 16)
        ctx.inc(g.dma_start(out=onesf128[:], in_=oned[0:1, 0:128]), "wdma", 16)
    ctx.consts["CONST_TOK"] = ctx.tok("wdma")


def emit_load_x(ctx, x_dram, x_sb):
    block = ctx.block

    @block.sync
    def _(sync):
        src = x_dram.rearrange("(t p) d -> p t d", p=128)
        dst = x_sb[:].rearrange("p (t d) -> p t d", d=D)
        ctx.inc(sync.dma_start(out=dst, in_=src), "xin", 16)


def emit_store_x(ctx, x_sb, y_dram):
    block = ctx.block

    @block.sync
    def _(sync):
        for name in ("pe", "vec", "act", "gp"):
            if name in ctx.cnt and ctx.cnt[name]:
                ctx.wait(sync, name)
        dst = y_dram.rearrange("(t p) d -> p t d", p=128)
        src2 = x_sb[:].rearrange("p (t d) -> p t d", d=D)
        ctx.inc(sync.dma_start(out=dst, in_=src2), "sdma", 16)
        ctx.wait(sync, "sdma")


def emit_ln_transpose(ctx, x_sb, out_T, ps_tr, gb_ap=None, out_also=None, scope=None,
                      lnrow_buf=None):
    block = ctx.block
    raw = ctx.sbuf("ln_raw", [128, 12], F32, scope)
    mv = ctx.sbuf("ln_mv", [128, 2 * NT], F32, scope)
    rstd = ctx.sbuf("ln_rstd", [128, NT], F32, scope)
    nmr = ctx.sbuf("ln_nmr", [128, NT], F32, scope)
    lnrow = lnrow_buf if lnrow_buf is not None else ctx.sbuf("ln_row", [128, D], BF16, scope)

    @block.vector
    def _(v):
        ctx.wait(v, "xin")
        if ctx.cnt.get("vec"):
            ctx.wait(v, "vec")
        for t in range(NT):
            xt = x_sb[:, t * D:(t + 1) * D]
            v.bn_stats(raw[:, 0:6], xt[:, 0:512])
            ctx.inc(v.bn_stats(raw[:, 6:12], xt[:, 512:1024]), "vec")
            ctx.wait(v, "vec")
            ctx.inc(v.bn_aggr(mv[:, 2 * t:2 * t + 2], raw[:]), "vec")
            ctx.wait(v, "vec")
        var_v = mv[:].rearrange("p (t two) -> p t two", two=2)[:, :, 1:2]
        ctx.inc(v.tensor_scalar_add(rstd[:].rearrange("p t -> p t ()"), var_v, 1e-5), "vec")

    @block.scalar
    def _(sc):
        ctx.wait(sc, "vec")
        ctx.inc(sc.activation(rstd[:], rstd[:], AF.Sqrt), "act")

    @block.vector
    def _(v):
        ctx.wait(v, "act")
        ctx.inc(v.reciprocal(rstd[:], rstd[:]), "vec")
        ctx.wait(v, "vec")
        mean_v = mv[:].rearrange("p (t two) -> p t two", two=2)[:, :, 0:1]
        i = v.tensor_tensor(nmr[:].rearrange("p t -> p t ()"), mean_v,
                            rstd[:].rearrange("p t -> p t ()"), ALU.mult)
        ctx.inc(i, "vec")
        ctx.wait(v, "vec")
        ctx.inc(v.tensor_scalar_mul(nmr[:], nmr[:], -1.0), "vec")

    stats_tok = ctx.tok("vec")
    for t in range(NT):
        @block.scalar
        def _(sc, t=t):
            if t == 0:
                sc.wait_ge(ctx.sget("vec"), stats_tok)
            else:
                ctx.wait(sc, "pe")
            xt = x_sb[:, t * D:(t + 1) * D]
            i = sc.activation(lnrow[:], xt, AF.Identity,
                              bias=nmr[:, t:t + 1], scale=rstd[:, t:t + 1])
            ctx.inc(i, "act")

        @block.tensor
        def _(tens, t=t):
            ctx.wait(tens, "act")
            if t == 0:
                ctx.wait(tens, "wdma")
            if t > 0:
                ctx.wait(tens, "vec")
            for half in range(2):
                for j4 in range(4):
                    j = half * 4 + j4
                    i = tens.matmul(ps_tr[:, 256 * half + 0:][:, 128 * j4:128 * j4 + 128]
                                    if False else
                                    ps_tr[:, 512 * half + 128 * j4: 512 * half + 128 * j4 + 128],
                                    lnrow[:, 128 * j:128 * (j + 1)],
                                    ctx.consts["IDENT"][:], start=True, stop=True)
            ctx.inc(i, "pe")

        @block.vector
        def _(v, t=t):
            ctx.wait(v, "pe")
            src = ps_tr[:].rearrange("p (j c) -> p j c", c=128)
            dst = out_T[:].rearrange("p (j s) -> p j s", s=S)[:, :, 128 * t:128 * t + 128]
            if gb_ap is None:
                i = v.tensor_copy(dst, src)
            else:
                g_t, b_t = gb_ap
                i = v.scalar_tensor_tensor(dst, src, g_t[:].rearrange("p j -> p j ()"),
                                           b_t[:].rearrange("p j -> p j ()"),
                                           ALU.mult, ALU.add)
            ctx.inc(i, "vec")
            if out_also is not None:
                ctx.wait(v, "vec")
                ctx.inc(v.tensor_copy(
                    out_also[:].rearrange("p (j s) -> p j s", s=S)[:, :, 128 * t:128 * t + 128],
                    src), "vec")


def load_weight_bf16(ctx, name, arr2d, tile_rows=128):
    R, C = arr2d.shape
    assert R % tile_rows == 0
    d = ctx.din(f"w_{name}_{ctx.nid()}", np.ascontiguousarray(arr2d, np.float32))
    sb = ctx.sbuf("ws_" + name, [tile_rows, (R // tile_rows) * C], BF16)
    block = ctx.block

    @block.gpsimd
    def _(g):
        src = d.rearrange("(j p) c -> p j c", p=tile_rows)
        dst = sb[:].rearrange("p (j c) -> p j c", j=R // tile_rows)
        ctx.inc(g.dma_start(out=dst, in_=src), "wdma", 16)
    return sb, ctx.tok("wdma")


def load_row_const(ctx, name, arr, dtype=F32, guards=(), scope=None):
    a = np.ascontiguousarray(arr, np.float32)
    if a.ndim == 1:
        a = a[None, :]
    d = ctx.din(f"r_{name}_{ctx.nid()}", a)
    sb = ctx.sbuf("rs_" + name, list(a.shape), dtype, scope)
    block = ctx.block

    @block.gpsimd
    def _(g):
        for nm in ("pe", "vec", "act"):
            if ctx.cnt.get(nm):
                ctx.wait(g, nm)
        ctx.inc(g.dma_start(out=sb[:], in_=d[:]), "wdma", 16)
    return sb, ctx.tok("wdma")




WSLOT_COLS = 2048


def wload(ctx, dram_ap, R, c0, c1, slot, slot_idx, pe_guard=None):
    import os as _os
    serial = bool(_os.environ.get("KSERIAL"))
    """DMA-cast fp32 dram weight rows[all] cols[c0:c1] into bf16 slot.
    Slot layout: row-tile j at cols [j*(c1-c0) : (j+1)*(c1-c0)].
    Each slot has its own completion semaphore ("wd0"/"wd1"); waits must use
    the slot's full cumulative count (DMA completions are unordered).
    pe_guard: wait for PE count before overwriting the slot.
    Returns (sem_name, token)."""
    block = ctx.block
    width = c1 - c0
    assert (R // 128) * width <= ctx.consts.get("WSLOT_COLS", WSLOT_COLS)
    semname = f"wd{slot_idx}"

    @block.gpsimd
    def _(g):
        if serial:
            for nm in ("pe", "vec", "act"):
                if ctx.cnt.get(nm):
                    ctx.wait(g, nm)
        elif pe_guard is not None:
            g.wait_ge(ctx.sget("pe"), pe_guard)
        src = dram_ap.rearrange("(j p) c -> p j c", p=128)[:, :, c0:c1]
        dst = slot[:].rearrange("p (j c) -> p j c", c=width)[:, 0:R // 128, :]
        ctx.inc(g.dma_start(out=dst, in_=src), semname, 16)
    return semname, ctx.tok(semname)


def gemm_T(ctx, dram_w, din, dout, actT, act_S, dstT, dst_S, ps,
           bias_tile=None, pre_wait=None, s_lo=0, s_hi=None, dst_off=0,
           act_off=0):
    block = ctx.block
    WS = ctx.consts["WSLOTS"]
    nj = din // 128
    w_cols = ctx.consts.get("WSLOT_COLS", WSLOT_COLS) // nj
    w_cols -= w_cols % 128
    s_hi = act_S if s_hi is None else s_hi
    nsc = (s_hi - s_lo) // 512
    for li, c0 in enumerate(range(0, dout, w_cols)):
        slot = WS[li % 2]
        wsem, tok = wload(ctx, dram_w, din, c0, c0 + w_cols, slot, li % 2,
                          pe_guard=ctx.tok("pe"))
        for jo_i in range(w_cols // 128):
            jo = c0 // 128 + jo_i
            for scn in range(nsc):
                s0 = s_lo + 512 * scn

                @block.tensor
                def _(t, jo_i=jo_i, jo=jo, s0=s0, slot=slot, tok=tok, wsem=wsem):
                    t.wait_ge(ctx.sget(wsem), tok)
                    if pre_wait is not None:
                        for nm, tv in pre_wait:
                            t.wait_ge(ctx.sget(nm), tv)
                    ctx.wait(t, "act")
                    ctx.wait(t, "vec")
                    ctx.wait(t, "wdma")
                    for j in range(nj):
                        i = t.matmul(ps,
                                     slot[:, j * w_cols + 128 * jo_i: j * w_cols + 128 * jo_i + 128],
                                     actT[:, j * act_S + act_off + s0: j * act_S + act_off + s0 + 512],
                                     start=(j == 0), stop=(j == nj - 1))
                    ctx.inc(i, "pe")

                @block.scalar
                def _(sc, jo=jo, s0=s0):
                    ctx.wait(sc, "pe")
                    o0 = jo * dst_S + dst_off + (s0 - s_lo)
                    if bias_tile is not None:
                        i = sc.activation(dstT[:, o0: o0 + 512],
                                          ps, AF.Identity, bias=bias_tile[:, jo:jo + 1])
                    else:
                        i = sc.activation(dstT[:, o0: o0 + 512],
                                          ps, AF.Identity)
                    ctx.inc(i, "act")


def gemm_N(ctx, dram_w, din, dout, actT, act_S, ps, evac,
           bias_row=None, pre_wait=None, evac_chan="vec"):
    block = ctx.block
    WS = ctx.consts["WSLOTS"]
    nj = din // 128
    w_cols = min(512, ctx.consts.get("WSLOT_COLS", WSLOT_COLS) // nj)
    ones_sb = ctx.consts["ONES"]
    for li, c0 in enumerate(range(0, dout, w_cols)):
        slot = WS[li % 2]
        wsem, tok = wload(ctx, dram_w, din, c0, c0 + w_cols, slot, li % 2,
                          pe_guard=ctx.tok("pe"))
        for t_i in range(NT):
            @block.tensor
            def _(t, t_i=t_i, c0=c0, slot=slot, tok=tok, wsem=wsem):
                t.wait_ge(ctx.sget(wsem), tok)
                if pre_wait is not None:
                    for nm, tv in pre_wait:
                        t.wait_ge(ctx.sget(nm), tv)
                ctx.wait(t, evac_chan)
                ctx.wait(t, "vec")
                ctx.wait(t, "wdma")
                for j in range(nj):
                    i = t.matmul(ps[:, 0:w_cols],
                                 actT[:, j * act_S + 128 * t_i: j * act_S + 128 * t_i + 128],
                                 slot[:, j * w_cols: (j + 1) * w_cols],
                                 start=(j == 0), stop=(j == nj - 1 and bias_row is None))
                if bias_row is not None:
                    i = t.matmul(ps[:, 0:w_cols], ones_sb[0:1, 0:128],
                                 bias_row[0:1, c0:c0 + w_cols],
                                 start=False, stop=True)
                ctx.inc(i, "pe")

            if evac_chan == "vec":
                @block.vector
                def _(v, t_i=t_i, c0=c0):
                    ctx.wait(v, "pe")
                    ctx.inc(evac(v, ps[:, 0:w_cols], t_i, c0, w_cols), "vec")
            else:
                @block.scalar
                def _(sc, t_i=t_i, c0=c0):
                    ctx.wait(sc, "pe")
                    ctx.inc(evac(sc, ps[:, 0:w_cols], t_i, c0, w_cols), "act")



def emit_l0aa(ctx, x_sb, w, PS_A, PS_B):
    block = ctx.block
    H, P = 16, 64

    Wq = np.transpose(w["L0_attn_a_q_w"], (2, 0, 1)).reshape(D, H * P)
    Wk = np.transpose(w["L0_attn_a_k_w"], (2, 0, 1)).reshape(D, H * P)
    Wv = np.transpose(w["L0_attn_a_v_w"], (2, 0, 1)).reshape(D, H * P)
    Wl = np.ascontiguousarray(np.asarray(w["L0_attn_a_l_w"]).T)
    bq = np.asarray(w["L0_attn_a_q_b"])[:, 0, :].reshape(H * P)
    bk = np.asarray(w["L0_attn_a_k_b"])[:, 0, :].reshape(H * P)
    bv = np.asarray(w["L0_attn_a_v_b"])[:, 0, :].reshape(H * P)
    bl = np.asarray(w["L0_attn_a_l_b"]).reshape(D)

    bq_sb, _ = load_row_const(ctx, "l0aa_bq", bq.reshape(8, 128).T)
    bk_sb, _ = load_row_const(ctx, "l0aa_bk", bk.reshape(8, 128).T)
    QT = ctx.sbuf("aa_QT", [128, DT * S], BF16)
    KT = ctx.sbuf("aa_KT", [128, DT * S], BF16)
    V = ctx.sbuf("aa_V", [128, NT * (H * P)], BF16)
    Eb = ctx.sbuf("aa_E", [128, S], BF16)
    ln_scope = ExitStack()
    lnT = ctx.sbuf("aa_lnT", [128, DT * S], BF16, ln_scope)
    bv_sb = ctx.sbuf("aa_bv", [1, D], BF16, ln_scope)
    dbv = ctx.din("r_l0aa_bv", bv[None, :].astype(np.float32))

    @block.gpsimd
    def _(g):
        for nm in ("pe", "vec", "act"):
            if ctx.cnt.get(nm):
                ctx.wait(g, nm)
        ctx.inc(g.dma_start(out=bv_sb[:], in_=dbv[:]), "wdma", 16)

    emit_ln_transpose(ctx, x_sb, lnT, PS_A[:, 0:1024], scope=ln_scope,
                      lnrow_buf=Eb[:, 0:D])

    dq = ctx.din("w_l0aa_q", Wq)
    dk = ctx.din("w_l0aa_k", Wk)
    dv = ctx.din("w_l0aa_v", Wv)
    dl = ctx.din("w_l0aa_l", Wl)

    ps_q = PS_A[:, 0:512]
    ps_v = PS_A[:, 1024:1536]

    gemm_T(ctx, dq, D, D, lnT, S, QT, S, ps_q, bias_tile=bq_sb)
    gemm_T(ctx, dk, D, D, lnT, S, KT, S, ps_q, bias_tile=bk_sb)

    def v_evac(v, psap, t_i, c0, w):
        return v.tensor_copy(V[:, t_i * 1024 + c0: t_i * 1024 + c0 + w], psap)
    gemm_N(ctx, dv, D, D, lnT, S, ps_v, v_evac, bias_row=bv_sb)

    ln_scope.close()
    bl_sb, _ = load_row_const(ctx, "l0aa_bl", bl[None, :], BF16,
                              guards=("pe", "vec", "act"))

    ps_sc = PS_A
    ps_cx = PS_B[0:64, :]
    negm = ctx.sbuf("aa_negm", [128, 1], F32)
    sums = ctx.sbuf("aa_sums", [128, 1], F32)
    rsum = ctx.sbuf("aa_rsum", [128, 1], F32)
    Vh = ctx.sbuf("aa_Vh", [128, P], BF16)

    proj_act_tok = ctx.tok("act")
    proj_vec_tok = ctx.tok("vec")

    for h in range(H):
        jt = h // 2
        ro = 64 * (h % 2)
        for kt in range(NT):
            @block.tensor
            def _(t, h=h, kt=kt, jt=jt, ro=ro):
                if h == 0 and kt == 0:
                    t.wait_ge(ctx.sget("act"), proj_act_tok)
                    t.wait_ge(ctx.sget("vec"), proj_vec_tok)
                else:
                    ctx.wait(t, "act")
                for qc in range(4):
                    i = t.matmul(ps_sc[:, 512 * qc:512 * qc + 512],
                                 KT[ro:ro + 64, jt * S + 128 * kt: jt * S + 128 * kt + 128],
                                 QT[ro:ro + 64, jt * S + 512 * qc: jt * S + 512 * qc + 512],
                                 start=True, stop=True)
                    ctx.inc(i, "pe")

            @block.vector
            def _(v, h=h, kt=kt):
                ctx.wait(v, "pe")
                i = v.reduce_max(negm[:], ps_sc[:], axis=AX.X, negate=True)
                ctx.inc(i, "vec")
                ctx.wait(v, "vec")
                ctx.inc(v.tensor_scalar_mul(negm[:], negm[:], 0.125), "vec")

            @block.scalar
            def _(sc, h=h, kt=kt):
                ctx.wait(sc, "vec")
                ctx.wait(sc, "pe")
                i = sc.activation(Eb[:], ps_sc[:], AF.Exp,
                                  bias=negm[:], scale=0.125, accum_out=sums[:])
                ctx.inc(i, "act")

            @block.vector
            def _(v, h=h, kt=kt):
                ctx.wait(v, "act")
                ctx.inc(v.reciprocal(rsum[:], sums[:]), "vec")
                ctx.wait(v, "vec")
                i = v.tensor_scalar(Vh[:], V[:, kt * 1024 + 64 * h: kt * 1024 + 64 * h + 64],
                                    rsum[:], None, ALU.mult)
                ctx.inc(i, "vec")

            @block.tensor
            def _(t, h=h, kt=kt):
                ctx.wait(t, "vec")
                for qc in range(4):
                    i = t.matmul(ps_cx[:, 512 * qc:512 * qc + 512],
                                 Vh[:], Eb[:, 512 * qc:512 * qc + 512],
                                 start=(kt == 0), stop=(kt == NT - 1))
                ctx.inc(i, "pe")

        @block.scalar
        def _(sc, h=h, jt=jt, ro=ro):
            ctx.wait(sc, "pe")
            i = sc.activation(KT[ro:ro + 64, jt * S: jt * S + S], ps_cx, AF.Identity)
            ctx.inc(i, "act")

    attn_act_tok = ctx.tok("act")
    ps_o = PS_B[:, 0:512]

    def o_evac(v, psap, t_i, c0, w):
        dst = x_sb[:, t_i * D + c0: t_i * D + c0 + w]
        return v.tensor_add(dst, dst, psap)
    gemm_N(ctx, dl, D, D, KT, S, ps_o, o_evac, bias_row=bl_sb,
           pre_wait=[("act", attn_act_tok)])





def emit_l0ab(ctx, x_sb, w, PS_A, PS_B):
    block = ctx.block
    H, DH = 16, 64

    w_q = np.asarray(w["L0_attn_b_w_q"], np.float32)
    w_kv = np.asarray(w["L0_attn_b_w_kv"], np.float32)
    wk = np.ascontiguousarray(w_kv[:, 0:D])
    wv = np.ascontiguousarray(w_kv[:, D:2 * D])
    w_o = np.asarray(w["L0_attn_b_w_out"], np.float32)
    b_o = np.asarray(w["L0_attn_b_b_out"]).reshape(D)
    scale_in = np.asarray(w["L0_attn_b_scale"]).reshape(H, 1)

    dq = ctx.din("w_l0ab_q", w_q)
    dk = ctx.din("w_l0ab_k", wk)
    dv = ctx.din("w_l0ab_v", wv)
    do = ctx.din("w_l0ab_o", w_o)

    sel = np.zeros((128, 8 * 16), np.float32)
    selT = np.zeros((16, 8 * 128), np.float32)
    for j in range(8):
        for p in range(128):
            h = 2 * j + (1 if p >= 64 else 0)
            sel[p, j * 16 + h] = 1.0
            selT[h, j * 128 + p] = 1.0
    sel_sb, _ = load_row_const(ctx, "l0ab_sel", sel, BF16)
    selT_sb, _ = load_row_const(ctx, "l0ab_selT", selT, F32)
    esc_d = ctx.din("r_l0ab_scale", scale_in)
    esc = ctx.sbuf("ab_esc", [H, 1], F32)

    @block.gpsimd
    def _(g):
        for nm in ("pe", "vec", "act"):
            if ctx.cnt.get(nm):
                ctx.wait(g, nm)
        ctx.inc(g.dma_start(out=esc[:], in_=esc_d[:]), "wdma", 16)

    @block.scalar
    def _(sc):
        ctx.wait(sc, "wdma")
        ctx.inc(sc.activation(esc[:], esc[:], AF.Exp), "act")

    QT = ctx.sbuf("ab_QT", [128, DT * S], BF16)
    KT = ctx.sbuf("ab_KT", [128, DT * S], BF16)
    V = ctx.sbuf("ab_V", [128, NT * D], BF16)
    ln_scope = ExitStack()
    lnT = ctx.sbuf("ab_lnT", [128, DT * S], BF16, ln_scope)
    emit_ln_transpose(ctx, x_sb, lnT, PS_A[:, 0:1024], scope=ln_scope,
                      lnrow_buf=QT[:, 0:D])

    gemm_T(ctx, dq, D, D, lnT, S, QT, S, PS_A[:, 0:512])
    gemm_T(ctx, dk, D, D, lnT, S, KT, S, PS_A[:, 512:1024])

    def v_evac(v, psap, t_i, c0, wd):
        return v.tensor_copy(V[:, t_i * D + c0: t_i * D + c0 + wd], psap)
    gemm_N(ctx, dv, D, D, lnT, S, PS_A[:, 1024:1536], v_evac)
    ln_scope.close()

    sqb = ctx.sbuf("ab_sq", [128, 512], BF16)
    nrm = ctx.sbuf("ab_nrm", [16, 512], F32)
    ps_ss = PS_B[0:16, 0:512]
    ps_rb = PS_B[:, 512:1024]
    for which, src in (("q", QT), ("k", KT)):
        for scn in range(4):
            for j in range(DT):
                @block.scalar
                def _(sc, j=j, scn=scn, src=src):
                    ctx.wait(sc, "act")
                    ctx.wait(sc, "vec")
                    ctx.wait(sc, "pe")
                    i = sc.activation(sqb[:], src[:, j * S + 512 * scn: j * S + 512 * scn + 512],
                                      AF.Square)
                    ctx.inc(i, "act")

                @block.tensor
                def _(t, j=j, scn=scn):
                    ctx.wait(t, "act")
                    if j == 0:
                        ctx.wait(t, "vec")
                    i = t.matmul(ps_ss, sel_sb[:, j * 16:(j + 1) * 16], sqb[:],
                                 start=(j == 0), stop=(j == DT - 1))
                    ctx.inc(i, "pe")

            @block.scalar
            def _(sc, which=which, scn=scn):
                ctx.wait(sc, "pe")
                ctx.inc(sc.activation(nrm[:], ps_ss, AF.Sqrt), "act")

            @block.vector
            def _(v, which=which, scn=scn):
                ctx.wait(v, "act")
                ctx.inc(v.tensor_scalar_max(nrm[:], nrm[:], 1e-12), "vec")
                ctx.wait(v, "vec")
                ctx.inc(v.reciprocal(nrm[:], nrm[:]), "vec")
                if which == "q":
                    ctx.wait(v, "vec")
                    ctx.inc(v.tensor_scalar(nrm[:], nrm[:], esc[:], None, ALU.mult), "vec")

            for j in range(DT):
                @block.tensor
                def _(t, j=j, scn=scn):
                    ctx.wait(t, "vec")
                    i = t.matmul(ps_rb, selT_sb[0:16, j * 128:(j + 1) * 128], nrm[:],
                                 start=True, stop=True)
                    ctx.inc(i, "pe")

                @block.vector
                def _(v, j=j, scn=scn, src=src):
                    ctx.wait(v, "pe")
                    sl = src[:, j * S + 512 * scn: j * S + 512 * scn + 512]
                    ctx.inc(v.tensor_mul(sl, sl, ps_rb), "vec")

    QA = ctx.sbuf("ab_QA", [66, 2 * S], BF16)
    KA = ctx.sbuf("ab_KA", [66, 2 * S], BF16)
    ET = ctx.sbuf("ab_ET", [128, 2 * S], BF16)
    scr = ctx.sbuf("ab_scr", [128, S], BF16)
    negm = ctx.sbuf("ab_negm", [128, NT], F32)
    sums = ctx.sbuf("ab_sums", [128, NT], F32)
    mls = ctx.sbuf("ab_mls", [128, NT], F32)
    mhb = ctx.sbuf("ab_mhb", [128, NT], BF16)
    mhl = ctx.sbuf("ab_mhl", [128, 2 * NT], F32)
    norm_vec_tok = ctx.tok("vec")

    @block.vector
    def _(v):
        ctx.wait(v, "vec")
        ctx.inc(v.memset(KA[64:66, :], -1.0), "vec")

    for pair in range(8):
        for hh in range(2):
            h = 2 * pair + hh
            jt = h // 2
            ro = 64 * (h % 2)
            for qt in range(NT):
                @block.tensor
                def _(t, qt=qt, jt=jt, ro=ro, pair=pair, hh=hh):
                    if pair == 0 and hh == 0 and qt == 0:
                        t.wait_ge(ctx.sget("vec"), norm_vec_tok)
                    ctx.wait(t, "act")
                    for kc in range(4):
                        i = t.matmul(PS_A[:, 512 * kc:512 * kc + 512],
                                     QT[ro:ro + 64, jt * S + 128 * qt: jt * S + 128 * qt + 128],
                                     KT[ro:ro + 64, jt * S + 512 * kc: jt * S + 512 * kc + 512],
                                     start=True, stop=True)
                    ctx.inc(i, "pe")

                @block.vector
                def _(v, qt=qt):
                    ctx.wait(v, "pe")
                    ctx.inc(v.reduce_max(negm[:, qt:qt + 1], PS_A[:], axis=AX.X,
                                         negate=True), "vec")

                @block.scalar
                def _(sc, qt=qt):
                    ctx.wait(sc, "vec")
                    ctx.wait(sc, "act")
                    i = sc.activation(scr[:], PS_A[:], AF.Exp,
                                      bias=negm[:, qt:qt + 1],
                                      accum_out=sums[:, qt:qt + 1])
                    ctx.inc(i, "act")

            @block.scalar
            def _(sc):
                ctx.wait(sc, "act")
                ctx.inc(sc.activation(mls[:], sums[:], AF.Ln), "act")

            @block.vector
            def _(v):
                ctx.wait(v, "act")
                ctx.inc(v.tensor_sub(mls[:], mls[:], negm[:]), "vec")
                ctx.wait(v, "vec")
                ctx.inc(v.tensor_copy(mhb[:], mls[:]), "vec")
                ctx.wait(v, "vec")
                ctx.inc(v.tensor_copy(mhl[:].rearrange("p (t two) -> p two t", two=2)[:, 0, :],
                                      mhb[:]), "vec")
                ctx.wait(v, "vec")
                ctx.inc(v.tensor_sub(mhl[:].rearrange("p (t two) -> p two t", two=2)[:, 1, :],
                                     mls[:],
                                     mhl[:].rearrange("p (t two) -> p two t", two=2)[:, 0, :]),
                        "vec")

            @block.vector
            def _(v, hh=hh, jt=jt, ro=ro):
                ctx.wait(v, "vec")
                ctx.inc(v.tensor_copy(QA[0:64, hh * S:(hh + 1) * S],
                                      QT[ro:ro + 64, jt * S: jt * S + S]), "vec")
                ctx.wait(v, "vec")
                ctx.inc(v.tensor_copy(KA[0:64, hh * S:(hh + 1) * S],
                                      KT[ro:ro + 64, jt * S: jt * S + S]), "vec")

            for qt in range(NT):
                @block.tensor
                def _(t, qt=qt):
                    ctx.wait(t, "vec")
                    i = t.matmul(PS_B[0:2, 0:128], mhl[:, 2 * qt:2 * qt + 2],
                                 ctx.consts["IDENTF"][:], start=True, stop=True)
                    ctx.inc(i, "pe")

                @block.vector
                def _(v, qt=qt, hh=hh):
                    ctx.wait(v, "pe")
                    dst = QA[64:66, hh * S + 128 * qt: hh * S + 128 * qt + 128]
                    ctx.inc(v.tensor_copy(dst, PS_B[0:2, 0:128]), "vec")

        pair_vec_tok = ctx.tok("vec")
        for kt in range(NT):
            for hh in range(2):
                h = 2 * pair + hh

                @block.tensor
                def _(t, kt=kt, hh=hh, pair_vec_tok=pair_vec_tok):
                    t.wait_ge(ctx.sget("vec"), pair_vec_tok)
                    ctx.wait(t, "act")
                    for qc in range(4):
                        i = t.matmul(PS_A[:, 512 * qc:512 * qc + 512],
                                     KA[0:66, hh * S + 128 * kt: hh * S + 128 * kt + 128],
                                     QA[0:66, hh * S + 512 * qc: hh * S + 512 * qc + 512],
                                     start=True, stop=True)
                    ctx.inc(i, "pe")

                @block.scalar
                def _(sc, kt=kt, hh=hh):
                    ctx.wait(sc, "pe")
                    ctx.wait(sc, "act")
                    i = sc.activation(ET[:, hh * S:(hh + 1) * S], PS_A[:], AF.Exp)
                    ctx.inc(i, "act")

                @block.tensor
                def _(t, kt=kt, hh=hh, h=h):
                    ctx.wait(t, "act")
                    for qc in range(4):
                        i = t.matmul(PS_B[64 * hh:64 * hh + 64, 512 * qc:512 * qc + 512],
                                     V[:, kt * D + 64 * h: kt * D + 64 * h + 64],
                                     ET[:, hh * S + 512 * qc: hh * S + 512 * qc + 512],
                                     start=(kt == 0), stop=(kt == NT - 1),
                                     tile_position=(0, 64 * hh),
                                     skip_group_check=True)
                    ctx.inc(i, "pe")

        @block.scalar
        def _(sc, pair=pair):
            ctx.wait(sc, "pe")
            i = sc.activation(QT[:, pair * S:(pair + 1) * S], PS_B[:, 0:S], AF.Identity)
            ctx.inc(i, "act")

    bo_sb, _ = load_row_const(ctx, "l0ab_bo", b_o[None, :], BF16,
                              guards=("pe", "vec", "act"))
    att_act_tok = ctx.tok("act")

    def o_evac(v, psap, t_i, c0, wd):
        dst = x_sb[:, t_i * D + c0: t_i * D + c0 + wd]
        return v.tensor_add(dst, dst, psap)
    gemm_N(ctx, do, D, D, QT, S, PS_A[:, 0:512], o_evac, bias_row=bo_sb,
           pre_wait=[("act", att_act_tok)])





def emit_l1aa(ctx, x_sb, w, PS_A, PS_B):
    block = ctx.block
    H, DH, RD = 16, 64, 32
    EPS = 1e-6
    SCL = float(1.0 / np.sqrt(np.float32(96.0)))

    W_kv = np.asarray(w["L1_attn_a_W_kv"], np.float32)
    g_kv = np.asarray(w["L1_attn_a_g_kv"], np.float32)
    W_kdec = np.asarray(w["L1_attn_a_W_kdec"], np.float32)
    W_vdec = np.asarray(w["L1_attn_a_W_vdec"], np.float32)
    W_q = np.asarray(w["L1_attn_a_W_q"], np.float32)
    W_qdec = np.asarray(w["L1_attn_a_W_qdec"], np.float32)
    W_krope = np.asarray(w["L1_attn_a_W_krope"], np.float32)
    W_qrope = np.asarray(w["L1_attn_a_W_qrope"], np.float32)
    W_o = np.asarray(w["L1_attn_a_W_o"], np.float32)

    def perm_rope(W):
        Wr = np.zeros_like(W)
        for h in range(H):
            blk = W[:, 32 * h:32 * h + 32]
            Wr[:, 32 * h:32 * h + 16] = blk[:, 16:32]
            Wr[:, 32 * h + 16:32 * h + 32] = blk[:, 0:16]
        return np.ascontiguousarray(Wr)

    d_kv = ctx.din("w_l1aa_kv", W_kv)
    d_kdec = ctx.din("w_l1aa_kdec", W_kdec)
    d_vdec = ctx.din("w_l1aa_vdec", W_vdec)
    d_q = ctx.din("w_l1aa_q", W_q)
    d_qdec = ctx.din("w_l1aa_qdec", W_qdec)
    d_krp = ctx.din("w_l1aa_krp", W_krope)
    d_krr = ctx.din("w_l1aa_krr", perm_rope(W_krope))
    d_qrp = ctx.din("w_l1aa_qrp", W_qrope)
    d_qrr = ctx.din("w_l1aa_qrr", perm_rope(W_qrope))
    d_o = ctx.din("w_l1aa_o", W_o)

    inv_freq = 1.0 / 10000.0 ** (np.arange(0, RD, 2, dtype=np.float64) / RD)
    tpos = np.arange(S, dtype=np.float64)
    fr = np.outer(tpos, inv_freq)
    emb = np.concatenate([fr, fr], -1)
    cos32 = np.cos(emb).T
    sin32 = np.sin(emb).T
    cosT = np.zeros((128, S), np.float32)
    sinT = np.zeros((128, S), np.float32)
    for r in range(128):
        d = r % 32
        cosT[r] = cos32[d]
        sinT[r] = sin32[d] * (-1.0 if d < 16 else 1.0)
    cos_sb, _ = load_row_const(ctx, "l1aa_cos", cosT, BF16)
    sin_sb, _ = load_row_const(ctx, "l1aa_sin", sinT, BF16)
    gk_tile, _ = load_row_const(ctx, "l1aa_gkv", g_kv.reshape(2, 128).T)
    onescolb, _ = load_row_const(ctx, "l1aa_onescol", np.ones((128, 1), np.float32), BF16)
    maskTs = []
    for r in range(4):
        m = np.zeros((128, 512), np.float32)
        for qb in range(4):
            if qb < r:
                m[:, qb * 128:(qb + 1) * 128] = -1e30
            elif qb == r:
                for p in range(128):
                    m[p, qb * 128: qb * 128 + p] = -1e30
        maskTs.append(m)
    maskT_sb, _ = load_row_const(ctx, "l1aa_maskT", np.concatenate(maskTs, 1), BF16)
    maskD0 = np.zeros((128, 128), np.float32)
    for p in range(128):
        maskD0[p, p + 1:] = -1e30
    maskD_sb, _ = load_row_const(ctx, "l1aa_maskD", maskD0, BF16)
    eps_sb, _ = load_row_const(ctx, "l1aa_eps", np.array([[EPS]], np.float32))
    onesrowf, _ = load_row_const(ctx, "l1aa_onesrowf", np.ones((1, 128), np.float32), F32)

    QC = ctx.sbuf("ma_QC", [128, 4 * S], BF16)
    KVT = ctx.sbuf("ma_KVT", [128, 2 * S], BF16)
    AOT = ctx.sbuf("ma_AOT", [128, DT * S], BF16)
    KRF = ctx.sbuf("ma_KRF", [128, 4 * S], BF16)
    scr512 = ctx.sbuf("ma_scr", [128, 512], BF16)
    ph1 = ExitStack()
    lnT = ctx.sbuf("ma_lnT", [128, DT * S], BF16, ph1)
    KRR = AOT[:, 4 * S:8 * S]
    rms = ctx.sbuf("ma_rms", [1, S], F32, ph1)
    import os as _os9
    if _os9.environ.get("KLNZ"):
        @block.vector
        def _(v):
            ctx.wait(v, "xin")
            ctx.inc(v.memset(lnT[:], 0.0), "vec")
            ctx.wait(v, "vec")
            ctx.inc(v.memset(QC[:], 0.0), "vec")
            ctx.wait(v, "vec")
            ctx.inc(v.memset(KVT[:], 0.0), "vec")
    else:
        emit_ln_transpose(ctx, x_sb, lnT, PS_A[:, 0:1024], lnrow_buf=AOT[:, 0:D],
                          scope=ph1)

    _sk2 = set((_os9.environ.get("KSKIP2") or "").split(","))
    if "kv" not in _sk2:
        gemm_T(ctx, d_kv, D, 256, lnT, S, KVT, S, PS_A[:, 0:512])
    if "q" not in _sk2:
        gemm_T(ctx, d_q, D, 512, lnT, S, QC, S, PS_A[:, 512:1024])
    if "krg" not in _sk2:
        gemm_T(ctx, d_krp, D, 512, lnT, S, KRF, S, PS_A[:, 0:512])
        gemm_T(ctx, d_krr, D, 512, lnT, S, KRR, S, PS_A[:, 512:1024])
    for j in (() if "krft" in _sk2 else range(4)):
        for scn in range(4):
            @block.vector
            def _(v, j=j, scn=scn):
                ctx.wait(v, "act")
                ctx.wait(v, "vec")
                sl = slice(j * S + 512 * scn, j * S + 512 * scn + 512)
                ssl = slice(512 * scn, 512 * scn + 512)
                i = v.tensor_mul(scr512[:], KRF[:, sl], cos_sb[:, ssl])
                ctx.inc(i, "vec")
                ctx.wait(v, "vec")
                i = v.tensor_mul(KRR[:, sl], KRR[:, sl], sin_sb[:, ssl])
                ctx.inc(i, "vec")
                ctx.wait(v, "vec")
                i = v.tensor_add(KRF[:, sl], scr512[:], KRR[:, sl])
                ctx.inc(i, "vec")

    if "rms" in _sk2:
        _rms_skipped = True
    for scn in (() if "rms" in _sk2 else range(4)):
        for j in range(2):
            @block.scalar
            def _(sc, j=j, scn=scn):
                ctx.wait(sc, "act")
                ctx.wait(sc, "vec")
                ctx.wait(sc, "pe")
                i = sc.activation(scr512[:], KVT[:, j * S + 512 * scn: j * S + 512 * scn + 512],
                                  AF.Square)
                ctx.inc(i, "act")

            @block.tensor
            def _(t, j=j, scn=scn):
                ctx.wait(t, "act")
                i = t.matmul(PS_B[0:1, 0:512], onescolb[:], scr512[:],
                             start=(j == 0), stop=(j == 1))
                ctx.inc(i, "pe")

        @block.scalar
        def _(sc, scn=scn):
            ctx.wait(sc, "pe")
            i = sc.activation(rms[0:1, 512 * scn:512 * scn + 512], PS_B[0:1, 0:512],
                              AF.Sqrt, scale=1.0 / 256.0, bias=eps_sb[0:1, 0:1])
            ctx.inc(i, "act")

        @block.vector
        def _(v, scn=scn):
            ctx.wait(v, "act")
            sl = rms[0:1, 512 * scn:512 * scn + 512]
            ctx.inc(v.reciprocal(sl, sl), "vec")

    sq_act_tok = ctx.tok("act")
    for j in (() if "rms" in _sk2 else range(2)):
        @block.vector
        def _(v, j=j):
            v.wait_ge(ctx.sget("act"), sq_act_tok)
            ctx.wait(v, "vec")
            i = v.tensor_scalar(KVT[:, j * S:(j + 1) * S], KVT[:, j * S:(j + 1) * S],
                                gk_tile[:, j:j + 1], None, ALU.mult)
            ctx.inc(i, "vec")
    for j in range(2):
        for scn in range(4):
            @block.tensor
            def _(t, j=j, scn=scn):
                ctx.wait(t, "vec")
                i = t.matmul(PS_B[:, 512:1024], onesrowf[0:1, :],
                             rms[0:1, 512 * scn:512 * scn + 512],
                             start=True, stop=True)
                ctx.inc(i, "pe")

            @block.vector
            def _(v, j=j, scn=scn):
                ctx.wait(v, "pe")
                sl = KVT[:, j * S + 512 * scn: j * S + 512 * scn + 512]
                ctx.inc(v.tensor_mul(sl, sl, PS_B[:, 512:1024]), "vec")

    if ctx.consts.get("KTR") == "A":
        ph1.close()
        return
    ph1.close()
    QCp = ctx.sbuf("ma_QCp", [128, S], BF16)
    KCp = ctx.sbuf("ma_KCp", [128, S], BF16)
    QRp = ctx.sbuf("ma_QRp", [64, S], BF16)
    QRr = ctx.sbuf("ma_QRr", [64, S], BF16)
    KRp = ctx.sbuf("ma_KRp", [64, S], BF16)
    Vp = ctx.sbuf("ma_Vp", [128, S], BF16)
    KRX = ctx.sbuf("ma_KRX", [128, 2 * S], BF16)
    QRX = ctx.sbuf("ma_QRX", [128, 2 * S], BF16)
    ET = ctx.sbuf("ma_ET", [128, 2 * S], BF16)
    scrE = ET[:, 0:S]
    negm = ctx.sbuf("ma_negm", [128, NT], F32)
    sums = ctx.sbuf("ma_sums", [128, NT], F32)
    mls = ctx.sbuf("ma_mls", [128, NT], F32)
    mhb = ctx.sbuf("ma_mhb", [128, NT], BF16)
    mhl = ctx.sbuf("ma_mhl", [128, 2 * NT], F32)

    @block.vector
    def _(v):
        ctx.wait(v, "vec")
        ctx.inc(v.memset(KRX[32:64, :], 0.0), "vec")
        ctx.wait(v, "vec")
        ctx.inc(v.memset(KRX[32:34, :], -1.0), "vec")
        ctx.wait(v, "vec")
        ctx.inc(v.memset(QRX[32:64, :], 0.0), "vec")

    def pair_gemm(dram, din, actT, c0, dst, dst_c0, ncols, ps):
        nj = din // 128
        slot_i = pair_gemm.flip = (getattr(pair_gemm, "flip", 0) + 1) % 2
        slot = ctx.consts["WSLOTS"][slot_i]
        wide = 2 * ncols
        cb = (c0 // wide) * wide
        off = c0 - cb
        wsem, tok = wload(ctx, dram, din, cb, cb + wide, slot, slot_i,
                          pe_guard=ctx.tok("pe"))
        for scn in range(4):
            @block.tensor
            def _(t, scn=scn, slot=slot, tok=tok, wsem=wsem, nj=nj, ncols=ncols,
                  off=off, wide=wide):
                t.wait_ge(ctx.sget(wsem), tok)
                ctx.wait(t, "act")
                ctx.wait(t, "vec")
                for j in range(nj):
                    i = t.matmul(ps[0:ncols, 0:512],
                                 slot[:, j * wide + off: j * wide + off + ncols],
                                 actT[:, j * S + 512 * scn: j * S + 512 * scn + 512],
                                 start=(j == 0), stop=(j == nj - 1))
                ctx.inc(i, "pe")

            @block.scalar
            def _(sc, scn=scn, ncols=ncols):
                ctx.wait(sc, "pe")
                i = sc.activation(dst[dst_c0:dst_c0 + ncols, 512 * scn:512 * scn + 512],
                                  ps[0:ncols, 0:512], AF.Identity)
                ctx.inc(i, "act")

    def pair_gemm_n(dram, din, actT, c0, ps):
        nj = din // 128
        slot_i = pair_gemm.flip = (getattr(pair_gemm, "flip", 0) + 1) % 2
        slot = ctx.consts["WSLOTS"][slot_i]
        cb = (c0 // 256) * 256
        off = c0 - cb
        wsem, tok = wload(ctx, dram, din, cb, cb + 256, slot, slot_i,
                          pe_guard=ctx.tok("pe"))
        for t_i in range(NT):
            @block.tensor
            def _(t, t_i=t_i, slot=slot, tok=tok, wsem=wsem, nj=nj, off=off):
                t.wait_ge(ctx.sget(wsem), tok)
                ctx.wait(t, "act")
                ctx.wait(t, "vec")
                for j in range(nj):
                    i = t.matmul(ps[:, 0:128],
                                 actT[:, j * S + 128 * t_i: j * S + 128 * t_i + 128],
                                 slot[:, j * 256 + off: j * 256 + off + 128],
                                 start=(j == 0), stop=(j == nj - 1))
                ctx.inc(i, "pe")

            @block.vector
            def _(v, t_i=t_i):
                ctx.wait(v, "pe")
                ctx.inc(v.tensor_copy(Vp[:, t_i * 128: t_i * 128 + 128], ps[:, 0:128]),
                        "vec")

    def rope_apply(dst, rot, n_rows):
        for scn in range(4):
            @block.vector
            def _(v, scn=scn):
                ctx.wait(v, "act")
                ctx.wait(v, "vec")
                sl = slice(512 * scn, 512 * scn + 512)
                i = v.tensor_mul(scr512[0:n_rows, 0:512], dst[:, sl], cos_sb[0:n_rows, sl])
                ctx.inc(i, "vec")
                ctx.wait(v, "vec")
                i = v.tensor_mul(rot[:, sl], rot[:, sl], sin_sb[0:n_rows, sl])
                ctx.inc(i, "vec")
                ctx.wait(v, "vec")
                i = v.tensor_add(dst[:, sl], scr512[0:n_rows, 0:512], rot[:, sl])
                ctx.inc(i, "vec")

    import os as _os
    _skips = set((_os.environ.get("KSKIP") or "").split(","))
    _np_ = int(_os.environ.get("KNPAIR") or 8)
    for pair in range(_np_):
        pb = PS_A[:, 0:512] if pair % 2 == 0 else PS_A[:, 512:1024]
        if "qk" not in _skips:
            pair_gemm(d_qdec, 512, QC, 128 * pair, QCp, 0, 128, pb)
            if "kd" not in _skips:
                pair_gemm(d_kdec, 256, KVT, 128 * pair, KCp, 0, 128, pb)
        if "qr" not in _skips:
            pair_gemm(d_qrp, 512, QC, 64 * pair, QRp, 0, 64, PS_A[:, 512:1024])
            pair_gemm(d_qrr, 512, QC, 64 * pair, QRr, 0, 64, PS_A[:, 512:1024])
            if "rope" not in _skips:
                rope_apply(QRp, QRr, 64)
        if "v" not in _skips:
            pair_gemm_n(d_vdec, 256, KVT, 128 * pair, PS_A[:, 1024:1536])

        @block.vector
        def _(v, pair=pair):
            import os as _os
            if _os.environ.get("KNOKRCOPY"):
                return
            ctx.wait(v, "vec")
            ctx.wait(v, "pe")
            jsrc = pair // 2
            r0 = 64 * (pair % 2)
            i = v.tensor_copy(KRp[:, :], KRF[r0:r0 + 64, jsrc * S:(jsrc + 1) * S])
            ctx.inc(i, "vec")
            ctx.wait(v, "act")
            for hh in range(2):
                ctx.wait(v, "vec")
                i = v.tensor_copy(KRX[0:32, hh * S:(hh + 1) * S],
                                  KRp[32 * hh:32 * hh + 32, :])
                ctx.inc(i, "vec")
                ctx.wait(v, "vec")
                i = v.tensor_copy(QRX[0:32, hh * S:(hh + 1) * S],
                                  QRp[32 * hh:32 * hh + 32, :])
                ctx.inc(i, "vec")
                ctx.wait(v, "vec")
                i = v.tensor_copy(KRX[64:128, hh * S:(hh + 1) * S],
                                  KCp[64 * hh:64 * hh + 64, :])
                ctx.inc(i, "vec")
                ctx.wait(v, "vec")
                i = v.tensor_copy(QRX[64:128, hh * S:(hh + 1) * S],
                                  QCp[64 * hh:64 * hh + 64, :])
                ctx.inc(i, "vec")

        if _os.environ.get("KSTOP") == "proj":
            continue
        for hh in range(2):
            ro = 64 * hh
            rr = 32 * hh
            for qt in range(NT):
                import os as _os2
                nkc = qt // 4 + 1
                if _os2.environ.get("KFULLK"):
                    nkc = 4
                @block.tensor
                def _(t, qt=qt, ro=ro, rr=rr, nkc=nkc):
                    import os as _os
                    ctx.wait(t, "act")
                    ctx.wait(t, "vec")
                    for kc in range(nkc):
                        klen = min(512, 128 * (qt + 1) - 512 * kc)
                        if _os.environ.get("KFULLK"):
                            klen = 512
                        ps = PS_A[:, 512 * kc:512 * kc + klen]
                        if _os.environ.get("KNOROPE"):
                            i = t.matmul(ps, QCp[ro:ro + 64, 128 * qt:128 * qt + 128],
                                         KCp[ro:ro + 64, 512 * kc:512 * kc + klen],
                                         start=True, stop=True)
                        else:
                            t.matmul(ps, QCp[ro:ro + 64, 128 * qt:128 * qt + 128],
                                     KCp[ro:ro + 64, 512 * kc:512 * kc + klen],
                                     start=True, stop=False)
                            i = t.matmul(ps, QRp[rr:rr + 32, 128 * qt:128 * qt + 128],
                                         KRp[rr:rr + 32, 512 * kc:512 * kc + klen],
                                         start=False, stop=True)
                    ctx.inc(i, "pe")

                @block.vector
                def _(v, qt=qt):
                    import os as _os
                    ctx.wait(v, "pe")
                    if not _os.environ.get("KNOMASK"):
                        db = PS_A[:, 128 * qt:128 * qt + 128]
                        ctx.inc(v.tensor_add(db, db, maskD_sb[:]), "vec")
                        ctx.wait(v, "vec")
                    if _os.environ.get("KNORED"):
                        ctx.inc(v.nop(), "vec")
                    else:
                        ctx.inc(v.reduce_max(negm[:, qt:qt + 1], PS_A[:, 0:128 * (qt + 1)],
                                             axis=AX.X, negate=True), "vec")
                        ctx.wait(v, "vec")
                        ctx.inc(v.tensor_scalar_mul(negm[:, qt:qt + 1], negm[:, qt:qt + 1],
                                                    SCL), "vec")

                @block.scalar
                def _(sc, qt=qt):
                    import os as _os
                    ctx.wait(sc, "vec")
                    ctx.wait(sc, "act")
                    if _os.environ.get("KNOEXP"):
                        fd = 2048 if _os.environ.get("KFULLK") else 128 * (qt + 1)
                        i = sc.activation(scrE[:, 0:fd], PS_A[:, 0:fd], AF.Identity)
                    else:
                        i = sc.activation(scrE[:, 0:128 * (qt + 1)], PS_A[:, 0:128 * (qt + 1)],
                                          AF.Exp, bias=negm[:, qt:qt + 1], scale=SCL,
                                          accum_out=sums[:, qt:qt + 1])
                    ctx.inc(i, "act")


            @block.scalar
            def _(sc):
                ctx.wait(sc, "act")
                ctx.inc(sc.activation(mls[:], sums[:], AF.Ln), "act")

            @block.vector
            def _(v):
                ctx.wait(v, "act")
                ctx.inc(v.tensor_sub(mls[:], mls[:], negm[:]), "vec")
                ctx.wait(v, "vec")
                ctx.inc(v.tensor_scalar_mul(mls[:], mls[:], 1.0 / SCL), "vec")
                ctx.wait(v, "vec")
                ctx.inc(v.tensor_copy(mhb[:], mls[:]), "vec")
                ctx.wait(v, "vec")
                ctx.inc(v.tensor_copy(mhl[:].rearrange("p (t two) -> p two t", two=2)[:, 0, :],
                                      mhb[:]), "vec")
                ctx.wait(v, "vec")
                ctx.inc(v.tensor_sub(mhl[:].rearrange("p (t two) -> p two t", two=2)[:, 1, :],
                                     mls[:],
                                     mhl[:].rearrange("p (t two) -> p two t", two=2)[:, 0, :]),
                        "vec")

            for qt in range(NT):
                @block.tensor
                def _(t, qt=qt):
                    ctx.wait(t, "vec")
                    i = t.matmul(PS_B[0:2, 0:128], mhl[:, 2 * qt:2 * qt + 2],
                                 ctx.consts["IDENTF"][:], start=True, stop=True)
                    ctx.inc(i, "pe")

                @block.vector
                def _(v, qt=qt, hh=hh):
                    ctx.wait(v, "pe")
                    dst = QRX[32:34, hh * S + 128 * qt: hh * S + 128 * qt + 128]
                    ctx.inc(v.tensor_copy(dst, PS_B[0:2, 0:128]), "vec")

        if _os.environ.get("KSTOP") == "pass1":
            continue
        _p2s = set((_os.environ.get("KP2SKIP") or "").split(","))
        pair_vec_tok = ctx.tok("vec")
        for kt in range(NT):
            qc0 = kt // 4
            for hh in range(2):
                ro = 64 * hh
                rr = 32 * hh

                @block.tensor
                def _(t, kt=kt, hh=hh, ro=ro, rr=rr, qc0=qc0, pair_vec_tok=pair_vec_tok):
                    t.wait_ge(ctx.sget("vec"), pair_vec_tok)
                    ctx.wait(t, "act")
                    ctx.wait(t, "vec")
                    for qc in range(qc0, 4):
                        ps = PS_A[:, 512 * qc:512 * qc + 512]
                        i = t.matmul(ps,
                                     KRX[0:128, hh * S + 128 * kt: hh * S + 128 * kt + 128],
                                     QRX[0:128, hh * S + 512 * qc: hh * S + 512 * qc + 512],
                                     start=True, stop=True)
                    ctx.inc(i, "pe")

                if "mask" not in _p2s:
                    @block.vector
                    def _(v, kt=kt, qc0=qc0):
                        ctx.wait(v, "pe")
                        ps = PS_A[:, 512 * qc0:512 * qc0 + 512]
                        i = v.tensor_add(ps, ps, maskT_sb[:, 512 * (kt % 4):512 * (kt % 4) + 512])
                        ctx.inc(i, "vec")

                if "exp" not in _p2s:
                    @block.scalar
                    def _(sc, kt=kt, hh=hh, qc0=qc0):
                        ctx.wait(sc, "vec")
                        ctx.wait(sc, "act")
                        i = sc.activation(ET[:, hh * S + 512 * qc0: (hh + 1) * S],
                                          PS_A[:, 512 * qc0: 2048], AF.Exp, scale=SCL)
                        ctx.inc(i, "act")

                if "av" not in _p2s and not (hh == 1 and "avh1" in _p2s):
                    @block.tensor
                    def _(t, kt=kt, hh=hh, qc0=qc0):
                        ctx.wait(t, "act")
                        for qc in range(qc0, 4):
                            i = t.matmul(PS_B[64 * hh:64 * hh + 64, 512 * qc:512 * qc + 512],
                                         Vp[:, kt * 128 + 64 * hh: kt * 128 + 64 * hh + 64],
                                         ET[:, hh * S + 512 * qc: hh * S + 512 * qc + 512],
                                         start=(kt == 0), stop=(kt == NT - 1),
                                         tile_position=(0, 64 * hh),
                                         skip_group_check=True)
                        ctx.inc(i, "pe")

        if _os.environ.get("KSTOP") == "pass2":
            continue
        @block.scalar
        def _(sc, pair=pair):
            ctx.wait(sc, "pe")
            i = sc.activation(AOT[:, pair * S:(pair + 1) * S], PS_B[:, 0:S], AF.Identity)
            ctx.inc(i, "act")

    if _os.environ.get("KNOOUT"):
        return
    att_act_tok = ctx.tok("act")

    def o_evac(v, psap, t_i, c0, wd):
        dst = x_sb[:, t_i * D + c0: t_i * D + c0 + wd]
        return v.tensor_add(dst, dst, psap)
    gemm_N(ctx, d_o, D, D, AOT, S, PS_A[:, 0:512], o_evac,
           pre_wait=[("act", att_act_tok)])





XS = S + 1


def emit_transpose_full(ctx, x_sb, out_T, ps_tr, rowbuf):
    block = ctx.block

    @block.vector
    def _(v):
        ctx.wait(v, "vec")
        ctx.wait(v, "pe")
        z = out_T[:].rearrange("p (j s) -> p j s", s=XS)[:, :, 0:1]
        ctx.inc(v.memset(z, 0.0), "vec")
    for t in range(NT):
        @block.scalar
        def _(sc, t=t):
            ctx.wait(sc, "pe")
            ctx.wait(sc, "vec")
            ctx.wait(sc, "act")
            i = sc.activation(rowbuf[:], x_sb[:, t * D:(t + 1) * D], AF.Identity)
            ctx.inc(i, "act")

        @block.tensor
        def _(tens, t=t):
            ctx.wait(tens, "act")
            ctx.wait(tens, "vec")
            if t == 0:
                ctx.wait(tens, "wdma")
            for half in range(2):
                for j4 in range(4):
                    j = half * 4 + j4
                    i = tens.matmul(ps_tr[:, 512 * half + 128 * j4: 512 * half + 128 * j4 + 128],
                                    rowbuf[:, 128 * j:128 * (j + 1)],
                                    ctx.consts["IDENT"][:], start=True, stop=True)
            ctx.inc(i, "pe")

        @block.vector
        def _(v, t=t):
            ctx.wait(v, "pe")
            srcv = ps_tr[:].rearrange("p (j c) -> p j c", c=128)
            dst = out_T[:].rearrange("p (j s) -> p j s", s=XS)[:, :, 1 + 128 * t:1 + 128 * t + 128]
            ctx.inc(v.tensor_copy(dst, srcv), "vec")


def emit_ln_inplace(ctx, x_sb, PS_unused):
    block = ctx.block
    raw = ctx.sbuf("lni_raw", [128, 12], F32)
    mv = ctx.sbuf("lni_mv", [128, 2 * NT], F32)
    rstd = ctx.sbuf("lni_rstd", [128, NT], F32)
    nmr = ctx.sbuf("lni_nmr", [128, NT], F32)

    @block.vector
    def _(v):
        ctx.wait(v, "xin")
        ctx.wait(v, "vec")
        for t in range(NT):
            xt = x_sb[:, t * D:(t + 1) * D]
            v.bn_stats(raw[:, 0:6], xt[:, 0:512])
            ctx.inc(v.bn_stats(raw[:, 6:12], xt[:, 512:1024]), "vec")
            ctx.wait(v, "vec")
            ctx.inc(v.bn_aggr(mv[:, 2 * t:2 * t + 2], raw[:]), "vec")
            ctx.wait(v, "vec")
        var_v = mv[:].rearrange("p (t two) -> p t two", two=2)[:, :, 1:2]
        ctx.inc(v.tensor_scalar_add(rstd[:].rearrange("p t -> p t ()"), var_v, 1e-5), "vec")

    @block.scalar
    def _(sc):
        ctx.wait(sc, "vec")
        ctx.inc(sc.activation(rstd[:], rstd[:], AF.Sqrt), "act")

    @block.vector
    def _(v):
        ctx.wait(v, "act")
        ctx.inc(v.reciprocal(rstd[:], rstd[:]), "vec")
        ctx.wait(v, "vec")
        mean_v = mv[:].rearrange("p (t two) -> p t two", two=2)[:, :, 0:1]
        i = v.tensor_tensor(nmr[:].rearrange("p t -> p t ()"), mean_v,
                            rstd[:].rearrange("p t -> p t ()"), ALU.mult)
        ctx.inc(i, "vec")
        ctx.wait(v, "vec")
        ctx.inc(v.tensor_scalar_mul(nmr[:], nmr[:], -1.0), "vec")

    for t in range(NT):
        @block.scalar
        def _(sc, t=t):
            ctx.wait(sc, "vec")
            xt = x_sb[:, t * D:(t + 1) * D]
            i = sc.activation(xt, xt, AF.Identity,
                              bias=nmr[:, t:t + 1], scale=rstd[:, t:t + 1])
            ctx.inc(i, "act")


def emit_ffn(ctx, x_sb, w, pfx, PS_A, PS_B, x_in_dram):
    block = ctx.block
    HEADS, DH, W = 8, 64, 512
    HID = 2730
    HIDP = 2816
    NC = 11

    wqkv = np.asarray(w[pfx + "_wqkv"], np.float32)
    q_scale = np.asarray(w[pfx + "_q_scale"], np.float32)
    k_scale = np.asarray(w[pfx + "_k_scale"], np.float32)
    wo = np.asarray(w[pfx + "_wo"], np.float32)
    ln_g = np.asarray(w[pfx + "_ln_g"], np.float32)
    ln_b = np.asarray(w[pfx + "_ln_b"], np.float32)
    w1 = np.asarray(w[pfx + "_w1"], np.float32)
    b1 = np.asarray(w[pfx + "_b1"], np.float32)
    w2 = np.asarray(w[pfx + "_w2"], np.float32)
    b2 = np.asarray(w[pfx + "_b2"], np.float32)

    def perm64(Wm):
        Wr = np.zeros_like(Wm)
        nh = Wm.shape[1] // 64
        for h in range(nh):
            blk = Wm[:, 64 * h:64 * h + 64]
            Wr[:, 64 * h:64 * h + 32] = blk[:, 32:64]
            Wr[:, 64 * h + 32:64 * h + 64] = blk[:, 0:32]
        return np.ascontiguousarray(Wr)

    dL = []
    for l in range(2):
        Wq = np.ascontiguousarray(wqkv[l][:, 0:512])
        Wk = np.ascontiguousarray(wqkv[l][:, 512:1024])
        Wv = np.ascontiguousarray(wqkv[l][:, 1024:1536])
        W1p = np.zeros((D, 2 * HIDP), np.float32)
        b1p = np.zeros((1, 2 * HIDP), np.float32)
        a_part = np.zeros((D, HIDP), np.float32)
        g_part = np.zeros((D, HIDP), np.float32)
        a_part[:, 0:HID] = w1[l][:, 0:HID]
        g_part[:, 0:HID] = w1[l][:, HID:2 * HID]
        ab = np.zeros((1, HIDP), np.float32)
        gb = np.zeros((1, HIDP), np.float32)
        ab[0, 0:HID] = b1[l][0:HID]
        gb[0, 0:HID] = b1[l][HID:2 * HID]
        for c in range(NC):
            W1p[:, 512 * c:512 * c + 256] = a_part[:, 256 * c:256 * c + 256]
            W1p[:, 512 * c + 256:512 * c + 512] = g_part[:, 256 * c:256 * c + 256]
            b1p[0, 512 * c:512 * c + 256] = ab[0, 256 * c:256 * c + 256]
            b1p[0, 512 * c + 256:512 * c + 512] = gb[0, 256 * c:256 * c + 256]
        W2p = np.zeros((HIDP, D), np.float32)
        W2p[0:HID, :] = w2[l]
        dL.append(dict(
            q=ctx.din(f"w_{pfx}_q{l}", Wq), qr=ctx.din(f"w_{pfx}_qr{l}", perm64(Wq)),
            k=ctx.din(f"w_{pfx}_k{l}", Wk), kr=ctx.din(f"w_{pfx}_kr{l}", perm64(Wk)),
            v=ctx.din(f"w_{pfx}_v{l}", Wv),
            wo=ctx.din(f"w_{pfx}_wo{l}", np.ascontiguousarray(wo[l])),
            w1=ctx.din(f"w_{pfx}_w1{l}", W1p),
            b1=ctx.din(f"w_{pfx}_b1{l}", b1p),
            w2=ctx.din(f"w_{pfx}_w2{l}", W2p),
            b2=ctx.din(f"w_{pfx}_b2{l}", b2[l][None, :]),
            qs=ctx.din(f"w_{pfx}_qs{l}", np.tile(q_scale[l], 2)[:, None]),
            qsp=ctx.din(f"w_{pfx}_qsp{l}",
                        np.tile(np.concatenate([q_scale[l][32:], q_scale[l][:32]]), 2)[:, None]),
            ks=ctx.din(f"w_{pfx}_ks{l}", np.tile(k_scale[l], 2)[:, None]),
            ksp=ctx.din(f"w_{pfx}_ksp{l}",
                        np.tile(np.concatenate([k_scale[l][32:], k_scale[l][:32]]), 2)[:, None]),
            lg=ctx.din(f"w_{pfx}_lg{l}", ln_g[l].reshape(8, 128).T.copy()),
            lb=ctx.din(f"w_{pfx}_lb{l}", ln_b[l].reshape(8, 128).T.copy()),
        ))

    half_idx = np.arange(0, DH, 2, dtype=np.float64)
    inv_freq = 1.0 / 10000.0 ** (half_idx / DH)
    tt = np.arange(S, dtype=np.float64)
    freqs = tt[:, None] * inv_freq[None, :]
    freqs = np.concatenate([freqs, freqs], -1)
    base = (half_idx + 0.4 * DH) / (1.4 * DH)
    power = (tt - S // 2) / float(W // 2)
    xsc = base[None, :] ** power[:, None]
    xsc = np.concatenate([xsc, xsc], -1)
    cosv = np.cos(freqs)
    sinv = np.sin(freqs)
    sgn = np.where(np.arange(DH) < 32, -1.0, 1.0)[None, :]
    cq8 = np.tile((cosv * xsc * 8.0).T, (2, 1))
    sq8 = np.tile((sinv * xsc * 8.0 * sgn).T, (2, 1))
    ck = np.tile((cosv / xsc).T, (2, 1))
    sk = np.tile((sinv / xsc * sgn).T, (2, 1))
    d_cq = ctx.din(f"c_{pfx}_cq", np.ascontiguousarray(cq8, np.float32))
    d_sq = ctx.din(f"c_{pfx}_sq", np.ascontiguousarray(sq8, np.float32))
    d_ck = ctx.din(f"c_{pfx}_ck", np.ascontiguousarray(ck, np.float32))
    d_sk = ctx.din(f"c_{pfx}_sk", np.ascontiguousarray(sk, np.float32))

    sel = np.zeros((128, 4 * 8), np.float32)
    selT = np.zeros((8, 4 * 128), np.float32)
    for jo in range(4):
        for p in range(128):
            h = 2 * jo + (1 if p >= 64 else 0)
            sel[p, jo * 8 + h] = 1.0
            selT[h, jo * 128 + p] = 1.0
    maskD = np.zeros((128, 128), np.float32)
    maskU = np.zeros((128, 128), np.float32)
    for p in range(128):
        maskD[p, p + 1:] = -1e30
        maskU[p, 0:p] = -1e30
    onescolb, _ = load_row_const(ctx, pfx + "_onescol",
                                 np.ones((128, 1), np.float32), BF16)
    onesf1, _ = load_row_const(ctx, pfx + "_onesf1", np.ones((1, 1), np.float32))

    xT = ctx.sbuf("ff_xT", [128, DT * XS], BF16)
    rowb = ctx.sbuf("ff_rowb", [128, D], BF16)

    emit_ln_inplace(ctx, x_sb, None)

    for l in range(2):
        dl = dL[l]
        qs_sb, _ = load_row_const(ctx, f"{pfx}_qs{l}v", np.zeros((1, 1), np.float32)) \
            if False else (None, None)
        qs_t = ctx.sbuf(f"ff_qs{l}", [128, 4], F32)

        @block.gpsimd
        def _(g, l=l, qs_t=qs_t):
            dd = dL[l]
            for nm in ("pe", "vec", "act"):
                if ctx.cnt.get(nm):
                    ctx.wait(g, nm)
            ctx.inc(g.dma_start(out=qs_t[:, 0:1], in_=dd["qs"]), "wdma", 16)
            ctx.inc(g.dma_start(out=qs_t[:, 1:2], in_=dd["qsp"]), "wdma", 16)
            ctx.inc(g.dma_start(out=qs_t[:, 2:3], in_=dd["ks"]), "wdma", 16)
            ctx.inc(g.dma_start(out=qs_t[:, 3:4], in_=dd["ksp"]), "wdma", 16)

        atn = ExitStack()
        TWc = ctx.sbuf("ff_TWc", [128, 512], F32, atn)
        TWs = ctx.sbuf("ff_TWs", [128, 512], F32, atn)
        PRJ = ctx.sbuf("ff_PRJ", [128, 4 * 512], F32, atn)
        PRR = ctx.sbuf("ff_PRR", [128, 4 * 512], F32, atn)
        sel_sb, _ = load_row_const(ctx, f"{pfx}_sel{l}", sel, BF16, scope=atn,
                                   guards=("pe", "vec", "act"))
        selT_sb, _ = load_row_const(ctx, f"{pfx}_selT{l}", selT, F32, scope=atn)
        maskD_sb, _ = load_row_const(ctx, f"{pfx}_maskD{l}", maskD, BF16, scope=atn)
        maskU_sb, _ = load_row_const(ctx, f"{pfx}_maskU{l}", maskU, BF16, scope=atn)
        neg2_sb, _ = load_row_const(ctx, f"{pfx}_neg2{l}",
                                    np.full((2, 128), -1.0, np.float32), BF16, scope=atn)

        emit_transpose_full(ctx, x_sb, xT, PS_A[:, 0:1024], rowb)

        QW = ctx.sbuf("ff_QW", [128, 4 * W], BF16, atn)
        K2 = ctx.sbuf("ff_K2", [128, 4 * 2 * W], BF16, atn)
        V2 = ctx.sbuf("ff_V2", [128, 8 * W], BF16, atn)
        OW = ctx.sbuf("ff_OW", [128, 4 * W], BF16, atn)
        WO = ctx.sbuf("ff_WO", [128, 4 * 1024], BF16, atn)
        ET = ctx.sbuf("ff_ET", [128, W], BF16, atn)
        scrE = ctx.sbuf("ff_scrE", [128, 640], BF16, atn)
        nrm = ctx.sbuf("ff_nrm", [8, W], F32, atn)
        MH = ctx.sbuf("ff_MH", [2, 4 * 128], BF16, atn)
        negm = ctx.sbuf("ff_negm", [128, 4], F32, atn)
        sums = ctx.sbuf("ff_sums", [128, 4], F32, atn)
        mls = ctx.sbuf("ff_mls", [128, 4], F32, atn)
        mhb = ctx.sbuf("ff_mhb", [128, 4], BF16, atn)
        mhl = ctx.sbuf("ff_mhl", [128, 8], F32, atn)

        @block.gpsimd
        def _(g, l=l):
            for nm in ("pe", "vec", "act"):
                ctx.wait(g, nm)
            srcw = dL[l]["wo"].rearrange("(j p) c -> p j c", p=128)
            dstw = WO[:].rearrange("p (j c) -> p j c", c=1024)
            ctx.inc(g.dma_start(out=dstw, in_=srcw), "wod", 16)
        wo_tok = ctx.tok("wod")

        def project_win(dram, dramr, wi, dst, d_tabc, d_tabs, qs_col, qsp_col):
            s0 = W * wi

            @block.gpsimd
            def _(g):
                for nm in ("pe", "vec", "act"):
                    ctx.wait(g, nm)
                ctx.inc(g.dma_start(out=TWc[:], in_=d_tabc[:, s0:s0 + 512]), "wdma", 16)
                ctx.inc(g.dma_start(out=TWs[:], in_=d_tabs[:, s0:s0 + 512]), "wdma", 16)

            for which, dr in (("raw", dram), ("rot", dramr)):
                slot_i = pair_flip()
                slot = ctx.consts["WSLOTS"][slot_i]
                wsem, tok = wload(ctx, dr, D, 0, 512, slot, slot_i,
                                  pe_guard=ctx.tok("pe"))
                for jo in range(4):
                    @block.tensor
                    def _(t, jo=jo, slot=slot, tok=tok, wsem=wsem, s0=s0):
                        t.wait_ge(ctx.sget(wsem), tok)
                        ctx.wait(t, "act")
                        ctx.wait(t, "vec")
                        for j in range(DT):
                            off = s0 if j >= 4 else s0 + 1
                            i = t.matmul(PS_A[:, 0:512],
                                         slot[:, j * 512 + 128 * jo: j * 512 + 128 * jo + 128],
                                         xT[:, j * XS + off: j * XS + off + 512],
                                         start=(j == 0), stop=(j == DT - 1))
                        ctx.inc(i, "pe")

                    @block.scalar
                    def _(sc, jo=jo, which=which):
                        ctx.wait(sc, "pe")
                        tgt = PRJ if which == "raw" else PRR
                        i = sc.activation(tgt[:, jo * 512:(jo + 1) * 512], PS_A[:, 0:512],
                                          AF.Identity)
                        ctx.inc(i, "act")

            for jo in range(4):
                @block.scalar
                def _(sc, jo=jo):
                    ctx.wait(sc, "act")
                    ctx.wait(sc, "pe")
                    ctx.wait(sc, "vec")
                    i = sc.activation(scrE[:, 0:512], PRJ[:, jo * 512:(jo + 1) * 512],
                                      AF.Square)
                    ctx.inc(i, "act")

                @block.tensor
                def _(t, jo=jo):
                    ctx.wait(t, "act")
                    if jo == 0:
                        ctx.wait(t, "vec")
                    i = t.matmul(PS_B[0:8, 0:512], sel_sb[:, jo * 8:(jo + 1) * 8],
                                 scrE[:, 0:512], start=(jo == 0), stop=(jo == 3))
                    ctx.inc(i, "pe")

            @block.scalar
            def _(sc):
                ctx.wait(sc, "pe")
                ctx.inc(sc.activation(nrm[:], PS_B[0:8, 0:512], AF.Sqrt), "act")

            @block.vector
            def _(v):
                ctx.wait(v, "act")
                ctx.inc(v.tensor_scalar_max(nrm[:], nrm[:], 1e-12), "vec")
                ctx.wait(v, "vec")
                ctx.inc(v.reciprocal(nrm[:], nrm[:]), "vec")

            for jo in range(4):
                @block.tensor
                def _(t, jo=jo):
                    ctx.wait(t, "vec")
                    i = t.matmul(PS_B[:, 512:1024], selT_sb[0:8, jo * 128:(jo + 1) * 128],
                                 nrm[:], start=True, stop=True)
                    ctx.inc(i, "pe")

                @block.vector
                def _(v, jo=jo, qs_col=qs_col, qsp_col=qsp_col, dst=dst):
                    ctx.wait(v, "pe")
                    ctx.wait(v, "wdma")
                    dsl = slice(jo * 512, (jo + 1) * 512)
                    i = v.tensor_scalar(PRJ[:, dsl], PRJ[:, dsl], qs_col, None, ALU.mult)
                    ctx.inc(i, "vec")
                    ctx.wait(v, "vec")
                    i = v.tensor_scalar(PRR[:, dsl], PRR[:, dsl], qsp_col, None, ALU.mult)
                    ctx.inc(i, "vec")
                    ctx.wait(v, "vec")
                    i = v.tensor_mul(PRJ[:, dsl], PRJ[:, dsl], TWc[:])
                    ctx.inc(i, "vec")
                    ctx.wait(v, "vec")
                    i = v.tensor_mul(PRR[:, dsl], PRR[:, dsl], TWs[:])
                    ctx.inc(i, "vec")
                    ctx.wait(v, "vec")
                    i = v.tensor_add(PRJ[:, dsl], PRJ[:, dsl], PRR[:, dsl])
                    ctx.inc(i, "vec")
                    ctx.wait(v, "vec")
                    i = v.tensor_mul(PRJ[:, dsl], PRJ[:, dsl], PS_B[:, 512:1024])
                    ctx.inc(i, "vec")
                    ctx.wait(v, "vec")
                    i = v.tensor_copy(dst[:, jo * W:(jo + 1) * W], PRJ[:, dsl])
                    ctx.inc(i, "vec")

        def pair_flip():
            pair_flip.f = (getattr(pair_flip, "f", 0) + 1) % 2
            return pair_flip.f

        for wi in range(4):
            project_win(dl["q"], dl["qr"], wi, QW, d_cq, d_sq,
                        qs_t[:, 0:1], qs_t[:, 1:2])
            project_win(dl["k"], dl["kr"], wi, OW, d_ck, d_sk,
                        qs_t[:, 2:3], qs_t[:, 3:4])
            if ctx.consts.get("DBGQ") is not None and l == 0 and wi == 1:
                @block.sync
                def _(sync):
                    for nm in ("pe", "vec", "act"):
                        ctx.wait(sync, nm)
                    ctx.inc(sync.dma_start(out=ctx.consts["DBGQ"][:],
                                           in_=QW[:]), "sdma", 16)
                    ctx.inc(sync.dma_start(out=ctx.consts["DBGK"][:],
                                           in_=OW[:]), "sdma", 16)

                @block.vector
                def _(v):
                    ctx.wait(v, "sdma")
                    ctx.inc(v.nop(), "vec")
            for jo in range(4):
                @block.vector
                def _(v, jo=jo, wi=wi):
                    ctx.wait(v, "vec")
                    ctx.wait(v, "pe")
                    dstk = K2[:, jo * 2 * W + (wi % 2) * W: jo * 2 * W + (wi % 2) * W + W]
                    ctx.inc(v.tensor_copy(dstk, OW[:, jo * W:(jo + 1) * W]), "vec")

            slot_i = pair_flip()
            slot = ctx.consts["WSLOTS"][slot_i]
            wsem, tok = wload(ctx, dl["v"], D, 0, 512, slot, slot_i,
                              pe_guard=ctx.tok("pe"))
            for ti in range(4):
                @block.tensor
                def _(t, ti=ti, wi=wi, slot=slot, tok=tok, wsem=wsem):
                    t.wait_ge(ctx.sget(wsem), tok)
                    ctx.wait(t, "act")
                    ctx.wait(t, "vec")
                    s0 = W * wi + 128 * ti
                    for j in range(DT):
                        off = s0 if j >= 4 else s0 + 1
                        i = t.matmul(PS_A[:, 1024:1536],
                                     xT[:, j * XS + off: j * XS + off + 128],
                                     slot[:, j * 512:(j + 1) * 512],
                                     start=(j == 0), stop=(j == DT - 1))
                    ctx.inc(i, "pe")

                @block.vector
                def _(v, ti=ti, wi=wi):
                    ctx.wait(v, "pe")
                    dstv = V2[:, ((wi % 2) * 4 + ti) * W: ((wi % 2) * 4 + ti) * W + W]
                    ctx.inc(v.tensor_copy(dstv, PS_A[:, 1024:1536]), "vec")

            for h in range(HEADS):
                jo = h // 2
                ro = 64 * (h % 2)
                kjs_all = list(range(8))
                if wi == 0:
                    kjs_all = [kj for kj in kjs_all if kj >= 4]

                def kcol(kj):
                    win = 0 if kj < 4 else 1
                    slot_w = (wi - 1 + win) % 2
                    return jo * 2 * W + slot_w * W + 128 * (kj % 4)

                def vcol(kj, hloc):
                    win = 0 if kj < 4 else 1
                    slot_w = (wi - 1 + win) % 2
                    return (slot_w * 4 + (kj % 4)) * W + 64 * hloc

                for qt in range(4):
                    kjs = [kj for kj in range(qt, qt + 5) if kj in kjs_all]
                    @block.tensor
                    def _(t, qt=qt, kjs=kjs, jo=jo, ro=ro, wi=wi):
                        ctx.wait(t, "act")
                        ctx.wait(t, "vec")
                        for bi, kj in enumerate(kjs):
                            i = t.matmul(PS_A[:, 128 * bi:128 * bi + 128],
                                         QW[ro:ro + 64, jo * W + 128 * qt: jo * W + 128 * qt + 128],
                                         K2[ro:ro + 64, kcol(kj):kcol(kj) + 128],
                                         start=True, stop=True)
                        ctx.inc(i, "pe")

                    @block.vector
                    def _(v, qt=qt, kjs=kjs):
                        ctx.wait(v, "pe")
                        nb = len(kjs)
                        for bi, kj in enumerate(kjs):
                            mk = maskU_sb if kj == qt else (maskD_sb if kj == qt + 4 else None)
                            if mk is not None:
                                db = PS_A[:, 128 * bi:128 * bi + 128]
                                ctx.inc(v.tensor_add(db, db, mk[:]), "vec")
                                ctx.wait(v, "vec")
                        ctx.inc(v.reduce_max(negm[:, qt:qt + 1], PS_A[:, 0:128 * nb],
                                             axis=AX.X, negate=True), "vec")

                    @block.scalar
                    def _(sc, qt=qt, kjs=kjs):
                        ctx.wait(sc, "vec")
                        ctx.wait(sc, "act")
                        i = sc.activation(scrE[:, 0:128 * len(kjs)],
                                          PS_A[:, 0:128 * len(kjs)], AF.Exp,
                                          bias=negm[:, qt:qt + 1],
                                          accum_out=sums[:, qt:qt + 1])
                        ctx.inc(i, "act")

                @block.scalar
                def _(sc):
                    ctx.wait(sc, "act")
                    ctx.inc(sc.activation(mls[:, 0:4], sums[:, 0:4], AF.Ln), "act")

                @block.vector
                def _(v):
                    ctx.wait(v, "act")
                    ctx.inc(v.tensor_sub(mls[:, 0:4], mls[:, 0:4], negm[:, 0:4]), "vec")
                    ctx.wait(v, "vec")
                    ctx.inc(v.tensor_copy(mhb[:, 0:4], mls[:, 0:4]), "vec")
                    ctx.wait(v, "vec")
                    ctx.inc(v.tensor_copy(
                        mhl[:].rearrange("p (t two) -> p two t", two=2)[:, 0, :],
                        mhb[:, 0:4]), "vec")
                    ctx.wait(v, "vec")
                    ctx.inc(v.tensor_sub(
                        mhl[:].rearrange("p (t two) -> p two t", two=2)[:, 1, :],
                        mls[:, 0:4],
                        mhl[:].rearrange("p (t two) -> p two t", two=2)[:, 0, :]), "vec")

                for qt in range(4):
                    @block.tensor
                    def _(t, qt=qt):
                        ctx.wait(t, "vec")
                        i = t.matmul(PS_B[0:2, 0:128], mhl[:, 2 * qt:2 * qt + 2],
                                     ctx.consts["IDENTF"][:], start=True, stop=True)
                        ctx.inc(i, "pe")

                    @block.vector
                    def _(v, qt=qt):
                        ctx.wait(v, "pe")
                        ctx.inc(v.tensor_copy(MH[0:2, 128 * qt:128 * qt + 128],
                                              PS_B[0:2, 0:128]), "vec")

                if (ctx.consts.get("DBGS") is not None and l == 0 and wi == 1
                        and h == 0):
                    @block.sync
                    def _(sync):
                        for nm in ("pe", "vec", "act"):
                            ctx.wait(sync, nm)
                        ctx.inc(sync.dma_start(out=ctx.consts["DBGV"][:],
                                               in_=V2[:]), "sdma", 16)

                    @block.vector
                    def _(v):
                        ctx.wait(v, "sdma")
                        ctx.inc(v.nop(), "vec")
                if (ctx.consts.get("DBGS") is not None and l == 0 and wi == 1
                        and h == 0):
                    @block.sync
                    def _(sync):
                        for nm in ("pe", "vec", "act"):
                            ctx.wait(sync, nm)
                        ctx.inc(sync.dma_start(out=ctx.consts["DBGS"][0:128, 0:4],
                                               in_=negm[:, 0:4]), "sdma", 16)
                        ctx.inc(sync.dma_start(out=ctx.consts["DBGS"][0:128, 4:8],
                                               in_=sums[:, 0:4]), "sdma", 16)
                        ctx.inc(sync.dma_start(out=ctx.consts["DBGS"][0:128, 8:16],
                                               in_=mhl[:, 0:8]), "sdma", 16)

                    @block.vector
                    def _(v):
                        ctx.wait(v, "sdma")
                        ctx.inc(v.nop(), "vec")

                h_vec_tok = ctx.tok("vec")
                for kj in kjs_all:
                    qt0 = max(0, kj - 4)
                    qt1 = min(3, kj)
                    qn = 128 * (qt1 - qt0 + 1)

                    @block.tensor
                    def _(t, kj=kj, qt0=qt0, qn=qn, jo=jo, ro=ro, h_vec_tok=h_vec_tok):
                        t.wait_ge(ctx.sget("vec"), h_vec_tok)
                        ctx.wait(t, "act")
                        ctx.wait(t, "vec")
                        ps = PS_A[:, 0:qn]
                        t.matmul(ps, K2[ro:ro + 64, kcol(kj):kcol(kj) + 128],
                                 QW[ro:ro + 64, jo * W + 128 * qt0: jo * W + 128 * qt0 + qn],
                                 start=True, stop=False)
                        i = t.matmul(ps, neg2_sb[0:2, :],
                                     MH[0:2, 128 * qt0:128 * qt0 + qn],
                                     start=False, stop=True)
                        ctx.inc(i, "pe")

                    @block.vector
                    def _(v, kj=kj, qt0=qt0, qt1=qt1):
                        ctx.wait(v, "pe")
                        if qt0 <= kj <= qt1:
                            bi = kj - qt0
                            db = PS_A[:, 128 * bi:128 * bi + 128]
                            ctx.inc(v.tensor_add(db, db, maskD_sb[:]), "vec")
                        if qt0 <= kj - 4 <= qt1:
                            bi = kj - 4 - qt0
                            db = PS_A[:, 128 * bi:128 * bi + 128]
                            ctx.inc(v.tensor_add(db, db, maskU_sb[:]), "vec")

                    @block.scalar
                    def _(sc, kj=kj, qn=qn):
                        ctx.wait(sc, "vec")
                        ctx.wait(sc, "act")
                        i = sc.activation(ET[:, 0:qn], PS_A[:, 0:qn], AF.Exp)
                        ctx.inc(i, "act")

                    if (ctx.consts.get("DBGE") is not None and l == 0 and wi == 1
                            and h == 0 and kj == 4):
                        @block.sync
                        def _(sync):
                            for nm in ("pe", "vec", "act"):
                                ctx.wait(sync, nm)
                            ctx.inc(sync.dma_start(out=ctx.consts["DBGE"][:],
                                                   in_=ET[:, 0:512]), "sdma", 16)

                        @block.vector
                        def _(v):
                            ctx.wait(v, "sdma")
                            ctx.inc(v.nop(), "vec")

                    @block.tensor
                    def _(t, kj=kj, qt0=qt0, qt1=qt1, h=h, wi=wi):
                        ctx.wait(t, "act")
                        first_kj = kjs_all[0] if True else 0
                        for qt in range(qt0, qt1 + 1):
                            start = (kj == max(qt, first_kj))
                            stop = (kj == qt + 4)
                            i = t.matmul(PS_B[0:64, 512 * qt:512 * qt + 128],
                                         V2[:, vcol(kj, h):vcol(kj, h) + 64],
                                         ET[:, 128 * (qt - qt0):128 * (qt - qt0) + 128],
                                         start=start, stop=stop,
                                         skip_group_check=True)
                        ctx.inc(i, "pe")

                @block.scalar
                def _(sc, h=h, jo=jo, ro=ro):
                    ctx.wait(sc, "pe")
                    for qt in range(4):
                        i = sc.activation(OW[ro:ro + 64, jo * W + 128 * qt: jo * W + 128 * qt + 128],
                                          PS_B[0:64, 512 * qt:512 * qt + 128], AF.Identity)
                    ctx.inc(i, "act")

            if ctx.consts.get("DBGO") is not None and l == 0 and wi in (0, 1):
                @block.sync
                def _(sync, wi=wi):
                    for nm in ("pe", "vec", "act"):
                        ctx.wait(sync, nm)
                    ctx.inc(sync.dma_start(out=ctx.consts["DBGO"][:, 2048 * wi:2048 * wi + 2048],
                                           in_=OW[:]), "sdma", 16)

                @block.vector
                def _(v):
                    ctx.wait(v, "sdma")
                    ctx.inc(v.nop(), "vec")

            @block.tensor
            def _(t, wi=wi):
                t.wait_ge(ctx.sget("wod"), wo_tok)
            for ti in range(4):
                for dc in range(2):
                    @block.tensor
                    def _(t, ti=ti, dc=dc, wi=wi):
                        ctx.wait(t, "act")
                        ctx.wait(t, "vec")
                        for jo in range(4):
                            i = t.matmul(PS_A[:, 1536:2048],
                                         OW[:, jo * W + 128 * ti: jo * W + 128 * ti + 128],
                                         WO[:, jo * 1024 + 512 * dc: jo * 1024 + 512 * dc + 512],
                                         start=(jo == 0), stop=(jo == 3))
                        ctx.inc(i, "pe")

                    @block.vector
                    def _(v, ti=ti, dc=dc, wi=wi):
                        ctx.wait(v, "pe")
                        st_g = 4 * wi + ti
                        dstx = x_sb[:, st_g * D + 512 * dc: st_g * D + 512 * dc + 512]
                        ctx.inc(v.tensor_add(dstx, dstx, PS_A[:, 1536:2048]), "vec")

        atn.close()
        if ctx.consts.get("DBG") is not None and l == 0:
            dbg = ctx.consts["DBG"]

            @block.sync
            def _(sync):
                for nm in ("pe", "vec", "act"):
                    ctx.wait(sync, nm)
                dstd = dbg.rearrange("(t p) d -> p t d", p=128)
                srcd = x_sb[:].rearrange("p (t d) -> p t d", d=D)
                ctx.inc(sync.dma_start(out=dstd, in_=srcd), "sdma", 16)

            @block.vector
            def _(v):
                ctx.wait(v, "sdma")
                ctx.inc(v.nop(), "vec")
        emit_transpose_full(ctx, x_sb, xT, PS_A[:, 0:1024], rowb)

        mlp = ExitStack()
        mnr = ctx.sbuf("ff_mnr", [1, S], F32, mlp)
        rrow = ctx.sbuf("ff_rrow", [1, S], F32, mlp)
        scr1 = ctx.sbuf("ff_scr1", [1, 512], F32, mlp)
        rcol = ctx.sbuf("ff_rcol", [128, NT], F32, mlp)
        sqc = ctx.sbuf("ff_sqc", [128, 512], BF16, mlp)
        lg_t = ctx.sbuf("ff_lgt", [128, 8], F32, mlp)
        lb_t = ctx.sbuf("ff_lbt", [128, 8], F32, mlp)
        lg_b = ctx.sbuf("ff_lgb", [128, 8], BF16, mlp)
        lb_b = ctx.sbuf("ff_lbb", [128, 8], BF16, mlp)
        b1row = ctx.sbuf("ff_b1row", [1, 2 * HIDP], BF16, mlp)
        b2row = ctx.sbuf("ff_b2row", [1, D], BF16, mlp)
        urow = ctx.sbuf("ff_urow", [1, 1024], F32, mlp)
        bxp = ctx.sbuf("ff_bxp", [128, 512], BF16, mlp)
        hT = ctx.sbuf("ff_hT", [128, 22 * 512], BF16, mlp)
        ab_sb = ctx.sbuf("ff_ab", [128, 512], BF16, mlp)
        W2R = ctx.sbuf("ff_W2R", [128, 22 * 512], BF16, mlp)

        @block.gpsimd
        def _(g, l=l):
            for nm in ("pe", "vec", "act"):
                ctx.wait(g, nm)
            ctx.inc(g.dma_start(out=lg_t[:], in_=dL[l]["lg"]), "wdma", 16)
            ctx.inc(g.dma_start(out=lb_t[:], in_=dL[l]["lb"]), "wdma", 16)
            ctx.inc(g.dma_start(out=lg_b[:], in_=dL[l]["lg"]), "wdma", 16)
            ctx.inc(g.dma_start(out=lb_b[:], in_=dL[l]["lb"]), "wdma", 16)
            ctx.inc(g.dma_start(out=b1row[:], in_=dL[l]["b1"]), "wdma", 16)
            ctx.inc(g.dma_start(out=b2row[:], in_=dL[l]["b2"]), "wdma", 16)

        for scn in range(4):
            for j in range(DT):
                @block.scalar
                def _(sc, j=j, scn=scn):
                    ctx.wait(sc, "act")
                    ctx.wait(sc, "pe")
                    ctx.wait(sc, "vec")
                    s0 = 512 * scn
                    off = s0 if j >= 4 else s0 + 1
                    i = sc.activation(sqc[:], xT[:, j * XS + off: j * XS + off + 512],
                                      AF.Square)
                    ctx.inc(i, "act")

                @block.tensor
                def _(t, j=j, scn=scn):
                    ctx.wait(t, "act")
                    if j == 0:
                        ctx.wait(t, "vec")
                    s0 = 512 * scn
                    off = s0 if j >= 4 else s0 + 1
                    t.matmul(PS_B[0:1, 0:512], onescolb[:],
                             xT[:, j * XS + off: j * XS + off + 512],
                             start=(j == 0), stop=(j == DT - 1))
                    i = t.matmul(PS_B[0:1, 512:1024], onescolb[:], sqc[:],
                                 start=(j == 0), stop=(j == DT - 1))
                    ctx.inc(i, "pe")

            @block.vector
            def _(v, scn=scn):
                ctx.wait(v, "pe")
                s0 = 512 * scn
                ctx.inc(v.tensor_scalar_mul(mnr[0:1, s0:s0 + 512], PS_B[0:1, 0:512],
                                            1.0 / float(D)), "vec")
                ctx.wait(v, "vec")
                ctx.inc(v.tensor_scalar_mul(rrow[0:1, s0:s0 + 512], PS_B[0:1, 512:1024],
                                            1.0 / float(D)), "vec")
                ctx.wait(v, "vec")
                i = v.scalar_tensor_tensor(scr1[:], mnr[0:1, s0:s0 + 512],
                                           -1.0, mnr[0:1, s0:s0 + 512],
                                           ALU.mult, ALU.mult)
                ctx.inc(i, "vec")
                ctx.wait(v, "vec")
                i = v.tensor_add(rrow[0:1, s0:s0 + 512], rrow[0:1, s0:s0 + 512],
                                 scr1[:])
                ctx.inc(i, "vec")
                ctx.wait(v, "vec")
                ctx.inc(v.tensor_scalar_add(rrow[0:1, s0:s0 + 512],
                                            rrow[0:1, s0:s0 + 512], 1e-5), "vec")

        @block.scalar
        def _(sc):
            ctx.wait(sc, "vec")
            ctx.inc(sc.activation(rrow[:], rrow[:], AF.Sqrt), "act")

        @block.vector
        def _(v):
            ctx.wait(v, "act")
            ctx.inc(v.reciprocal(rrow[:], rrow[:]), "vec")
            ctx.wait(v, "vec")
            i = v.tensor_mul(mnr[:], mnr[:], rrow[:])
            ctx.inc(i, "vec")
            ctx.wait(v, "vec")
            ctx.inc(v.tensor_scalar_mul(mnr[:], mnr[:], -1.0), "vec")

        for t_i in range(NT):
            @block.tensor
            def _(t, t_i=t_i):
                ctx.wait(t, "vec")
                i = t.matmul(PS_B[:, 1024 + t_i:1024 + t_i + 1],
                             rrow[0:1, 128 * t_i:128 * t_i + 128], onesf1[:],
                             start=True, stop=True)
                ctx.inc(i, "pe")

        @block.vector
        def _(v):
            ctx.wait(v, "pe")
            ctx.inc(v.tensor_copy(rcol[:], PS_B[:, 1024:1024 + NT]), "vec")

        w2half_sem = "wd0"
        for dhalf in range(2):
            @block.gpsimd
            def _(g, l=l, dhalf=dhalf):
                ctx.wait(g, "pe")
                srcw = dL[l]["w2"].rearrange("(j p) c -> p j c", p=128)[:, :, 512 * dhalf:512 * dhalf + 512]
                dstw = W2R[:].rearrange("p (j c) -> p j c", c=512)
                ctx.inc(g.dma_start(out=dstw, in_=srcw), "w2dma", 16)
            w2_tok = ctx.tok("w2dma")

            for sg in range(4):
                for c in range(NC):
                    slot_i = pair_flip()
                    slot = ctx.consts["WSLOTS"][slot_i]
                    wsem, tok = wload(ctx, dL[l]["w1"], D, 512 * c, 512 * c + 512,
                                      slot, slot_i, pe_guard=ctx.tok("pe"))
                    @block.tensor
                    def _(t, slot=slot, tok=tok, wsem=wsem, c=c):
                        t.wait_ge(ctx.sget(wsem), tok)
                        ctx.wait(t, "vec")
                        ctx.wait(t, "wdma")
                        for j in range(DT):
                            t.matmul(PS_B[0:1, 0:512], lg_b[:, j:j + 1],
                                     slot[:, j * 512:(j + 1) * 512],
                                     start=(j == 0), stop=(j == DT - 1))
                        for j in range(DT):
                            i = t.matmul(PS_B[0:1, 512:1024], lb_b[:, j:j + 1],
                                         slot[:, j * 512:(j + 1) * 512],
                                         start=(j == 0), stop=(j == DT - 1))
                        ctx.inc(i, "pe")

                    @block.vector
                    def _(v, c=c):
                        ctx.wait(v, "pe")
                        ctx.inc(v.tensor_copy(urow[0:1, 0:512], PS_B[0:1, 0:512]), "vec")
                        ctx.wait(v, "vec")
                        i = v.tensor_add(urow[0:1, 512:1024], PS_B[0:1, 512:1024],
                                         b1row[0:1, 512 * c:512 * c + 512])
                        ctx.inc(i, "vec")

                    @block.tensor
                    def _(t, c=c):
                        ctx.wait(t, "vec")
                        i = t.matmul(PS_B[:, 0:512], ctx.consts["ONESF128"][0:1, :],
                                     urow[0:1, 512:1024], start=True, stop=True)
                        ctx.inc(i, "pe")

                    @block.vector
                    def _(v, c=c):
                        ctx.wait(v, "pe")
                        ctx.inc(v.tensor_copy(bxp[:, 0:512], PS_B[:, 0:512]), "vec")

                    for j in range(DT):
                        @block.vector
                        def _(v, j=j, slot=slot):
                            ctx.wait(v, "vec")
                            if j == 0:
                                ctx.wait(v, "pe")
                            sl = slot[:, j * 512:(j + 1) * 512]
                            i = v.tensor_scalar(sl, sl, lg_t[:, j:j + 1], None, ALU.mult)
                            ctx.inc(i, "vec")

                    slot_vec_tok = ctx.tok("vec")
                    for ti in range(4):
                        t_i = 4 * sg + ti
                        @block.tensor
                        def _(t, t_i=t_i, c=c, slot=slot):
                            ctx.wait(t, "vec")
                            ctx.wait(t, "act")
                            s0 = 128 * t_i
                            for half, pso in ((0, 1024), (1, 1536)):
                                for j in range(DT):
                                    off = s0 if j >= 4 else s0 + 1
                                    t.matmul(PS_A[:, pso:pso + 256],
                                             xT[:, j * XS + off: j * XS + off + 128],
                                             slot[:, j * 512 + 256 * half: j * 512 + 256 * half + 256],
                                             start=(j == 0), stop=False)
                                i = t.matmul(PS_A[:, pso:pso + 256],
                                             mnr[0:1, s0:s0 + 128],
                                             urow[0:1, 256 * half:256 * half + 256],
                                             start=False, stop=True)
                            ctx.inc(i, "pe")

                        @block.vector
                        def _(v, t_i=t_i):
                            ctx.wait(v, "pe")
                            i = v.scalar_tensor_tensor(ab_sb[:, 0:256], PS_A[:, 1024:1280],
                                                       rcol[:, t_i:t_i + 1], bxp[:, 0:256],
                                                       ALU.mult, ALU.add)
                            ctx.inc(i, "vec")
                            ctx.wait(v, "vec")
                            i = v.scalar_tensor_tensor(ab_sb[:, 256:512], PS_A[:, 1536:1792],
                                                       rcol[:, t_i:t_i + 1], bxp[:, 256:512],
                                                       ALU.mult, ALU.add)
                            ctx.inc(i, "vec")

                        @block.scalar
                        def _(sc, t_i=t_i):
                            ctx.wait(sc, "vec")
                            ctx.wait(sc, "act")
                            i = sc.activation(ab_sb[:, 256:512], ab_sb[:, 256:512], AF.Gelu)
                            ctx.inc(i, "act")

                        @block.vector
                        def _(v, t_i=t_i):
                            ctx.wait(v, "act")
                            i = v.tensor_mul(ab_sb[:, 0:256], ab_sb[:, 0:256],
                                             ab_sb[:, 256:512])
                            ctx.inc(i, "vec")

                        @block.tensor
                        def _(t, ti=ti, c=c):
                            ctx.wait(t, "vec")
                            for b2i in range(2):
                                i = t.matmul(PS_A[:, 1024 + 128 * b2i:1024 + 128 * b2i + 128],
                                             ab_sb[:, 128 * b2i:128 * b2i + 128],
                                             ctx.consts["IDENT"][:], start=True, stop=True,
                                             skip_group_check=True)
                            ctx.inc(i, "pe")

                        @block.vector
                        def _(v, ti=ti, c=c):
                            ctx.wait(v, "pe")
                            dsth = hT[:, (2 * c) * 512 + 128 * ti: (2 * c) * 512 + 128 * ti + 128]
                            ctx.inc(v.tensor_copy(dsth, PS_A[:, 1024:1152]), "vec")
                            ctx.wait(v, "vec")
                            dsth = hT[:, (2 * c + 1) * 512 + 128 * ti: (2 * c + 1) * 512 + 128 * ti + 128]
                            ctx.inc(v.tensor_copy(dsth, PS_A[:, 1152:1280]), "vec")

                for ti in range(4):
                    t_i = 4 * sg + ti
                    @block.tensor
                    def _(t, t_i=t_i, ti=ti, dhalf=dhalf, w2_tok=w2_tok):
                        t.wait_ge(ctx.sget("w2dma"), w2_tok)
                        ctx.wait(t, "vec")
                        ctx.wait(t, "act")
                        for j in range(22):
                            t.matmul(PS_A[:, 1536:2048],
                                     hT[:, j * 512 + 128 * ti: j * 512 + 128 * ti + 128],
                                     W2R[:, j * 512:(j + 1) * 512],
                                     start=(j == 0), stop=False)
                        i = t.matmul(PS_A[:, 1536:2048], ctx.consts["ONES"][0:1, 0:128],
                                     b2row[0:1, 512 * dhalf:512 * dhalf + 512],
                                     start=False, stop=True)
                        ctx.inc(i, "pe")

                    @block.vector
                    def _(v, t_i=t_i, dhalf=dhalf):
                        ctx.wait(v, "pe")
                        dstx = x_sb[:, t_i * D + 512 * dhalf: t_i * D + 512 * dhalf + 512]
                        ctx.inc(v.tensor_add(dstx, dstx, PS_A[:, 1536:2048]), "vec")
        mlp.close()
        if ctx.consts.get("DBG2") is not None and l == 0:
            dbg = ctx.consts["DBG2"]

            @block.sync
            def _(sync):
                for nm in ("pe", "vec", "act"):
                    ctx.wait(sync, nm)
                dstd = dbg.rearrange("(t p) d -> p t d", p=128)
                srcd = x_sb[:].rearrange("p (t d) -> p t d", d=D)
                ctx.inc(sync.dma_start(out=dstd, in_=srcd), "sdma", 16)

            @block.vector
            def _(v):
                ctx.wait(v, "sdma")
                ctx.inc(v.nop(), "vec")

    xin_b = ctx.sbuf("ff_xinb", [128, D], F32)
    for t_i in range(NT):
        @block.sync
        def _(sync, t_i=t_i):
            if t_i > 0:
                ctx.wait(sync, "vec")
            else:
                for nm in ("pe", "vec", "act"):
                    ctx.wait(sync, nm)
            src = x_in_dram.rearrange("(t p) d -> p t d", p=128)[:, t_i, :]
            ctx.inc(sync.dma_start(out=xin_b[:], in_=src), "xrl", 16)

        @block.vector
        def _(v, t_i=t_i):
            ctx.wait(v, "xrl")
            ctx.wait(v, "vec")
            dstx = x_sb[:, t_i * D:(t_i + 1) * D]
            ctx.inc(v.tensor_add(dstx, dstx, xin_b[:]), "vec")





def emit_l1ab(ctx, x_sb, w, PS_A, PS_B):
    block = ctx.block
    DS = 16
    C2 = 2048
    CT = C2 // 128
    XS2 = S + 3

    W_in = np.asarray(w["L1_attn_b_W_in"], np.float32)
    Wxi = np.ascontiguousarray(W_in[:, 0:C2])
    Wz = np.ascontiguousarray(W_in[:, C2:2 * C2])
    conv_w = np.asarray(w["L1_attn_b_conv_w"], np.float32)
    conv_b = np.asarray(w["L1_attn_b_conv_b"], np.float32)
    W_x = np.asarray(w["L1_attn_b_W_x"], np.float32)
    W_dt = np.asarray(w["L1_attn_b_W_dt"], np.float32)
    b_dt = np.asarray(w["L1_attn_b_b_dt"], np.float32)
    A_log = np.asarray(w["L1_attn_b_A_log"], np.float32)
    Dd = np.asarray(w["L1_attn_b_D"], np.float32)
    W_out = np.asarray(w["L1_attn_b_W_out"], np.float32)

    d_xi = ctx.din("w_l1ab_xi", Wxi)
    d_z = ctx.din("w_l1ab_z", Wz)
    d_out = ctx.din("w_l1ab_out", W_out)
    cw_t, _ = load_row_const(ctx, "l1ab_cw",
                             conv_w[:, 0, :].reshape(CT, 128, 4).transpose(1, 0, 2).reshape(128, CT * 4))
    cb_t, _ = load_row_const(ctx, "l1ab_cb", conv_b.reshape(CT, 128).T.copy())
    alog_r, _ = load_row_const(ctx, "l1ab_alog", A_log[None, :])
    d_col, _ = load_row_const(ctx, "l1ab_d", Dd.reshape(CT, 128).T.copy())
    onesrowb, _ = load_row_const(ctx, "l1ab_ones512", np.ones((1, 512), np.float32), BF16)
    onecolf, _ = load_row_const(ctx, "l1ab_onecol", np.ones((128, 1), np.float32))

    XI = ctx.sbuf("mb_XI", [128, CT * XS2], BF16)
    ln1 = ExitStack()
    lnT = ctx.sbuf("mb_lnT", [128, DT * S], BF16, ln1)

    emit_ln_transpose(ctx, x_sb, lnT, PS_A[:, 0:1024], scope=ln1,
                      lnrow_buf=XI[:, 0:D])

    @block.vector
    def _(v):
        ctx.wait(v, "vec")
        ctx.wait(v, "act")
        z0 = XI[:].rearrange("p (j s) -> p j s", s=XS2)[:, :, 0:3]
        ctx.inc(v.memset(z0, 0.0), "vec")
    gemm_T(ctx, d_xi, D, C2, lnT, S, XI, XS2, PS_A[:, 0:512], dst_off=3)
    ln1.close()

    scan_scope = ExitStack()
    wdt_r, _ = load_row_const(ctx, "l1ab_wdt", W_dt[:, 0][None, :], BF16,
                              scope=scan_scope)
    bdt_r, _ = load_row_const(ctx, "l1ab_bdt", b_dt[None, :], BF16,
                              scope=scan_scope)
    wb_sb, _ = load_row_const(ctx, "l1ab_wb",
                              W_x[:, 1:17].reshape(CT, 128, 16).transpose(1, 0, 2).reshape(128, CT * 16),
                              BF16, scope=scan_scope)
    wc_sb, _ = load_row_const(ctx, "l1ab_wc",
                              W_x[:, 17:33].reshape(CT, 128, 16).transpose(1, 0, 2).reshape(128, CT * 16),
                              BF16, scope=scan_scope)
    wd_sb, _ = load_row_const(ctx, "l1ab_wd",
                              W_x[:, 0:1].reshape(CT, 128).T.copy(), BF16,
                              scope=scan_scope)
    selst_np = np.zeros((16, 16 * 128), np.float32)
    for st in range(16):
        selst_np[st, 128 * st:128 * (st + 1)] = 1.0
    selst, _ = load_row_const(ctx, "l1ab_selst", selst_np, BF16, scope=scan_scope)
    ydl = ctx.sbuf("mb_y", [128, S], F32, scan_scope)
    dlt = ctx.sbuf("mb_dlt", [128, S], BF16, scan_scope)
    wdx = ctx.sbuf("mb_wdx", [128, S], BF16, scan_scope)
    abuf = ctx.sbuf("mb_a", [128, S], BF16, scan_scope)
    bbuf = ctx.sbuf("mb_b", [128, S], BF16, scan_scope)
    hbuf = ctx.sbuf("mb_h", [128, S], BF16, scan_scope)
    tbuf = ctx.sbuf("mb_t", [128, S], BF16, scan_scope)
    cacc = ctx.sbuf("mb_cacc", [128, 512], F32, scan_scope)
    h0 = ctx.sbuf("mb_h0", [128, 1], F32, scan_scope)
    XP0 = ctx.sbuf("mb_xp0", [1, S], BF16, scan_scope)
    XPB = ctx.sbuf("mb_xpb", [16, S], BF16, scan_scope)
    XPC = ctx.sbuf("mb_xpc", [16, S], BF16, scan_scope)
    abc = ctx.sbuf("mb_abc", [128, DS], F32, scan_scope)

    for cch in (3, 2, 1, 0):
        for j in range(CT):
            @block.vector
            def _(v, cch=cch, j=j):
                ctx.wait(v, "act")
                ctx.wait(v, "vec")
                base = j * XS2 + 512 * cch
                i = v.tensor_scalar(cacc[:], XI[:, base:base + 512],
                                    cw_t[:, 4 * j:4 * j + 1], None, ALU.mult)
                ctx.inc(i, "vec")
                for kk in (1, 2, 3):
                    ctx.wait(v, "vec")
                    i = v.scalar_tensor_tensor(cacc[:], XI[:, base + kk:base + kk + 512],
                                               cw_t[:, 4 * j + kk:4 * j + kk + 1],
                                               cacc[:], ALU.mult, ALU.add)
                    ctx.inc(i, "vec")

            @block.scalar
            def _(sc, cch=cch, j=j):
                ctx.wait(sc, "vec")
                ctx.wait(sc, "act")
                i = sc.activation(XI[:, j * XS2 + 3 + 512 * cch: j * XS2 + 3 + 512 * cch + 512],
                                  cacc[:], AF.Silu, bias=cb_t[:, j:j + 1])
                ctx.inc(i, "act")

    for scn in range(4):
        @block.tensor
        def _(t, scn=scn):
            ctx.wait(t, "act")
            ctx.wait(t, "vec")
            ctx.wait(t, "wdma")
            sl = slice(512 * scn, 512 * scn + 512)
            for j in range(CT):
                xi_sl = XI[:, j * XS2 + 3 + 512 * scn: j * XS2 + 3 + 512 * scn + 512]
                t.matmul(PS_A[0:1, 0:512], wd_sb[:, j:j + 1], xi_sl,
                         start=(j == 0), stop=(j == CT - 1))
                t.matmul(PS_A[0:16, 512:1024], wb_sb[:, 16 * j:16 * (j + 1)], xi_sl,
                         start=(j == 0), stop=(j == CT - 1))
                i = t.matmul(PS_A[0:16, 1024:1536], wc_sb[:, 16 * j:16 * (j + 1)], xi_sl,
                             start=(j == 0), stop=(j == CT - 1))
            ctx.inc(i, "pe")

        @block.scalar
        def _(sc, scn=scn):
            ctx.wait(sc, "pe")
            sl = slice(512 * scn, 512 * scn + 512)
            sc.activation(XP0[0:1, sl], PS_A[0:1, 0:512], AF.Identity)
            sc.activation(XPB[0:16, sl], PS_A[0:16, 512:1024], AF.Identity)
            i = sc.activation(XPC[0:16, sl], PS_A[0:16, 1024:1536], AF.Identity)
            ctx.inc(i, "act")

    @block.scalar
    def _(sc):
        ctx.wait(sc, "wdma")
        ctx.wait(sc, "act")
        ctx.inc(sc.activation(alog_r[:], alog_r[:], AF.Exp), "act")

    @block.tensor
    def _(t):
        ctx.wait(t, "act")
        ctx.wait(t, "vec")
        i = t.matmul(PS_B[:, 0:16], ctx.consts["ONESF128"][0:1, :], alog_r[:],
                     start=True, stop=True)
        ctx.inc(i, "pe")

    @block.vector
    def _(v):
        ctx.wait(v, "pe")
        ctx.inc(v.tensor_scalar_mul(abc[:], PS_B[:, 0:16], -1.0), "vec")

    for ct in range(CT):
        for cch in range(4):
            @block.tensor
            def _(t, ct=ct, cch=cch):
                ctx.wait(t, "act")
                ctx.wait(t, "vec")
                t.matmul(PS_A[:, 512 * cch:512 * cch + 512],
                         wdt_r[0:1, 128 * ct:128 * ct + 128],
                         XP0[0:1, 512 * cch:512 * cch + 512],
                         start=True, stop=False)
                i = t.matmul(PS_A[:, 512 * cch:512 * cch + 512],
                             bdt_r[0:1, 128 * ct:128 * ct + 128],
                             onesrowb[0:1, 0:512],
                             start=False, stop=True)
                ctx.inc(i, "pe")

        @block.scalar
        def _(sc, ct=ct):
            ctx.wait(sc, "pe")
            ctx.wait(sc, "act")
            ctx.wait(sc, "vec")
            i = sc.activation(tbuf[:], PS_A[:, 0:S], AF.Abs)
            ctx.inc(i, "act")
            ctx.wait(sc, "act")
            i = sc.activation(tbuf[:], tbuf[:], AF.Exp, scale=-1.0)
            ctx.inc(i, "act")
            ctx.wait(sc, "act")
            i = sc.activation(tbuf[:], tbuf[:], AF.Ln, bias=onecolf[:, 0:1])
            ctx.inc(i, "act")

        @block.vector
        def _(v, ct=ct):
            ctx.wait(v, "act")
            i = v.tensor_scalar_max(dlt[:], PS_A[:, 0:S], 0.0)
            ctx.inc(i, "vec")
            ctx.wait(v, "vec")
            i = v.tensor_add(dlt[:], dlt[:], tbuf[:])
            ctx.inc(i, "vec")
            ctx.wait(v, "vec")
            i = v.tensor_mul(wdx[:], dlt[:], XI[:, ct * XS2 + 3: ct * XS2 + 3 + S])
            ctx.inc(i, "vec")
            ctx.wait(v, "vec")
            ctx.inc(v.memset(ydl[:], 0.0), "vec")
            ctx.wait(v, "vec")
            ctx.inc(v.memset(h0[:], 0.0), "vec")

        for st in range(DS):
            @block.scalar
            def _(sc, ct=ct, st=st):
                ctx.wait(sc, "vec")
                ctx.wait(sc, "act")
                ctx.wait(sc, "pe")
                i = sc.activation(abuf[:], dlt[:], AF.Exp, scale=abc[:, st:st + 1])
                ctx.inc(i, "act")

            for cch in range(4):
                @block.tensor
                def _(t, ct=ct, st=st, cch=cch):
                    ctx.wait(t, "vec")
                    ctx.wait(t, "act")
                    i = t.matmul(PS_B[:, 512 * cch:512 * cch + 512],
                                 selst[0:16, 128 * st:128 * st + 128],
                                 XPB[0:16, 512 * cch:512 * cch + 512],
                                 start=True, stop=True)
                    ctx.inc(i, "pe")

            @block.vector
            def _(v, ct=ct, st=st):
                ctx.wait(v, "pe")
                ctx.wait(v, "vec")
                i = v.tensor_mul(bbuf[:], wdx[:], PS_B[:, 0:S])
                ctx.inc(i, "vec")
                ctx.wait(v, "vec")
                ctx.wait(v, "act")
                i = v.tensor_tensor_scan(hbuf[:], abuf[:], bbuf[:], h0[:],
                                         ALU.mult, ALU.add)
                ctx.inc(i, "vec")

            for cch in range(4):
                @block.tensor
                def _(t, ct=ct, st=st, cch=cch):
                    ctx.wait(t, "vec")
                    i = t.matmul(PS_B[:, 512 * cch:512 * cch + 512],
                                 selst[0:16, 128 * st:128 * st + 128],
                                 XPC[0:16, 512 * cch:512 * cch + 512],
                                 start=True, stop=True)
                    ctx.inc(i, "pe")

            @block.vector
            def _(v, ct=ct, st=st):
                ctx.wait(v, "pe")
                ctx.wait(v, "vec")
                i = v.tensor_mul(tbuf[:], hbuf[:], PS_B[:, 0:S])
                ctx.inc(i, "vec")
                ctx.wait(v, "vec")
                i = v.tensor_add(ydl[:], ydl[:], tbuf[:])
                ctx.inc(i, "vec")

        @block.vector
        def _(v, ct=ct):
            ctx.wait(v, "vec")
            i = v.scalar_tensor_tensor(ydl[:], XI[:, ct * XS2 + 3: ct * XS2 + 3 + S],
                                       d_col[:, ct:ct + 1], ydl[:], ALU.mult, ALU.add)
            ctx.inc(i, "vec")
            ctx.wait(v, "vec")
            ctx.inc(v.tensor_copy(XI[:, ct * XS2 + 3: ct * XS2 + 3 + S], ydl[:]), "vec")

    scan_scope.close()
    zph = ExitStack()
    zbuf = ctx.sbuf("mb_zbuf", [128, CT * 512], BF16, zph)
    lnT2 = lnT
    lnTz = ctx.sbuf("mb_lnTz", [128, DT * S], BF16, zph)
    emit_ln_transpose(ctx, x_sb, lnTz, PS_A[:, 0:1024], scope=zph,
                      lnrow_buf=zbuf[:, 0:D])

    for scn in range(4):
        gemm_T(ctx, d_z, D, C2, lnTz, S, zbuf, 512, PS_A[:, 0:512],
               s_lo=512 * scn, s_hi=512 * scn + 512)
        for j in range(CT):
            @block.scalar
            def _(sc, scn=scn, j=j):
                ctx.wait(sc, "act")
                ctx.wait(sc, "pe")
                sl = zbuf[:, j * 512:(j + 1) * 512]
                i = sc.activation(sl, sl, AF.Silu)
                ctx.inc(i, "act")

            @block.vector
            def _(v, scn=scn, j=j):
                ctx.wait(v, "act")
                ctx.wait(v, "vec")
                sl = zbuf[:, j * 512:(j + 1) * 512]
                i = v.tensor_mul(sl, sl, XI[:, j * XS2 + 3 + 512 * scn: j * XS2 + 3 + 512 * scn + 512])
                ctx.inc(i, "vec")

        WS = ctx.consts["WSLOTS"]
        w_cols = 256
        for li, c0 in enumerate(range(0, D, w_cols)):
            slot = WS[li % 2]
            wsem, tok = wload(ctx, d_out, C2, c0, c0 + w_cols, slot, li % 2,
                              pe_guard=ctx.tok("pe"))
            for ti in range(4):
                t_i = 4 * scn + ti

                @block.tensor
                def _(t, t_i=t_i, ti=ti, c0=c0, slot=slot, tok=tok, wsem=wsem):
                    t.wait_ge(ctx.sget(wsem), tok)
                    ctx.wait(t, "vec")
                    ctx.wait(t, "act")
                    for j in range(CT):
                        i = t.matmul(PS_A[:, 1536:1536 + w_cols],
                                     zbuf[:, j * 512 + 128 * ti: j * 512 + 128 * ti + 128],
                                     slot[:, j * w_cols:(j + 1) * w_cols],
                                     start=(j == 0), stop=(j == CT - 1))
                    ctx.inc(i, "pe")

                @block.vector
                def _(v, t_i=t_i, c0=c0):
                    ctx.wait(v, "pe")
                    dstx = x_sb[:, t_i * D + c0: t_i * D + c0 + w_cols]
                    i = v.tensor_add(dstx, dstx, PS_A[:, 1536:1536 + w_cols])
                    ctx.inc(i, "vec")
    zph.close()



def build_single_sublayer_program(name, weights):
    ctx = Ctx()
    with ctx.stack:
        x_in = ctx.nc.dram_tensor("x_in", [S, D], F32, kind="ExternalInput").ap()
        y_out = ctx.dout("y_out", [S, D], F32)
        x_sb = ctx.sbuf("x_sb", [128, NT * D], F32)
        PS_A = ctx.psum([128, 2048], F32)
        PS_B = ctx.psum([128, 2048], F32)
        wsc = 4096 if (name.endswith("ffn") or name == "l1ab") else WSLOT_COLS
        ctx.consts["WSLOT_COLS"] = wsc
        ws0 = ctx.sbuf("WSLOT0", [128, wsc], BF16)
        ws1 = ctx.sbuf("WSLOT1", [128, wsc], BF16)
        ctx.consts["WSLOTS"] = (ws0, ws1)
        import os
        if os.environ.get("KTR"):
            ctx.consts["KTR"] = os.environ["KTR"]
        if os.environ.get("KDBG"):
            ctx.consts["DBG"] = ctx.dout("dbg1", [S, D], F32)
            ctx.consts["DBG2"] = ctx.dout("dbg2", [S, D], F32)
            ctx.consts["DBGQ"] = ctx.nc.dram_tensor("dbgq", [128, 4 * 512], mybir.dt.bfloat16, kind="ExternalOutput").ap()
            ctx.consts["DBGK"] = ctx.nc.dram_tensor("dbgk", [128, 4 * 512], mybir.dt.bfloat16, kind="ExternalOutput").ap()
            ctx.consts["DBGO"] = ctx.nc.dram_tensor("dbgo", [128, 2 * 4 * 512], mybir.dt.bfloat16, kind="ExternalOutput").ap()
            ctx.consts["DBGS"] = ctx.dout("dbgs", [128, 16], F32)
            ctx.consts["DBGV"] = ctx.nc.dram_tensor("dbgv", [128, 8 * 512], mybir.dt.bfloat16, kind="ExternalOutput").ap()
            ctx.consts["DBGE"] = ctx.nc.dram_tensor("dbge", [128, 512], mybir.dt.bfloat16, kind="ExternalOutput").ap()
        with ctx.nc.Block() as block:
            ctx.block = block
            emit_load_consts(ctx, need_identf=(name not in ("l0aa", "l1ab")))
            emit_load_x(ctx, x_in, x_sb)
            if name.endswith("ffn"):
                emit_ffn(ctx, x_sb, weights, name[:2].upper().replace("L", "L") and
                         ("L0_ffn" if name == "l0ffn" else "L1_ffn"),
                         PS_A, PS_B, x_in)
            else:
                EMITTERS[name](ctx, x_sb, weights, PS_A, PS_B)
            emit_store_x(ctx, x_sb, y_out)
        ctx.nc.finalize()
    return ctx


EMITTERS = {"l0aa": emit_l0aa, "l0ab": emit_l0ab, "l1aa": emit_l1aa,
            "l1ab": emit_l1ab}


def emit_ffn_launch(ctx, x_sb, weights, PS_A, PS_B, pfx, x_in):
    emit_ffn(ctx, x_sb, weights, pfx, PS_A, PS_B, x_in)


def kernel(**inputs):
    x = np.ascontiguousarray(np.asarray(inputs["L0_attn_a_x"], np.float32))
    B = x.shape[0]
    chain = ["l0aa", "l0ffn", "l0ab", "l0ffn", "l1aa", "l1ffn", "l1ab", "l1ffn"]
    progs = {}
    for name in chain:
        if name not in progs:
            progs[name] = build_single_sublayer_program(name, inputs)
    x_cur = x
    for name in chain:
        ctx = progs[name]
        in_maps = []
        for b in range(B):
            m = dict(ctx.inputs)
            m["x_in"] = np.ascontiguousarray(x_cur[b])
            in_maps.append(m)
        res = run_bass_kernel_spmd(ctx.nc, in_maps, core_ids=list(range(B)))
        x_cur = np.stack([r["y_out"] for r in res.results])
    return x_cur


# version 06
# speedup vs baseline: 1.0036x; 1.0036x over previous
import numpy as np
from contextlib import ExitStack

import concourse.bass as bass
import concourse.mybir as mybir
from concourse.bass_utils import run_bass_kernel_spmd

F32 = mybir.dt.float32
BF16 = mybir.dt.bfloat16
AF = mybir.ActivationFunctionType
ALU = mybir.AluOpType
AX = mybir.AxisListType

S = 2048
D = 1024
NT = S // 128
DT = D // 128


class Ctx:

    def __init__(self):
        self.nc = bass.Bass("TRN2", target_bir_lowering=False)
        self.stack = ExitStack()
        self.sem = {}
        self.cnt = {}
        self.inputs = {}
        self.consts = {}
        self.block = None
        self.cur_scope = None
        self.uid = 0

    def nid(self):
        self.uid += 1
        return self.uid

    def sget(self, name):
        if name not in self.sem:
            self.sem[name] = self.stack.enter_context(self.nc.semaphore(name))
            self.cnt[name] = 0
        return self.sem[name]

    def inc(self, ins, name, k=1):
        ins.then_inc(self.sget(name), k)
        self.cnt[name] += k
        return self.cnt[name]

    def tok(self, name):
        self.sget(name)
        return self.cnt[name]

    def wait(self, eng, name, val=None):
        eng.wait_ge(self.sget(name), self.cnt[name] if val is None else val)

    def din(self, name, arr, dtype=F32):
        if name in self.inputs:
            return self._dins[name]
        t = self.nc.dram_tensor(name, list(arr.shape), dtype, kind="ExternalInput")
        self.inputs[name] = np.ascontiguousarray(arr)
        if not hasattr(self, "_dins"):
            self._dins = {}
        self._dins[name] = t.ap()
        return t.ap()

    def dout(self, name, shape, dtype=F32):
        return self.nc.dram_tensor(name, list(shape), dtype, kind="ExternalOutput").ap()

    def sbuf(self, name, shape, dtype, scope=None):
        return (scope or self.cur_scope or self.stack).enter_context(
            self.nc.sbuf_tensor(f"{name}_{self.nid()}", list(shape), dtype))

    def psum(self, shape, dtype=F32):
        return self.stack.enter_context(
            self.nc.psum_tensor(f"ps_{self.nid()}", list(shape), dtype))



def emit_load_consts(ctx, need_identf=True):
    block = ctx.block
    ident = ctx.sbuf("IDENT", [128, 128], BF16)
    ctx.consts["IDENT"] = ident
    idd = ctx.din("c_ident", np.eye(128, dtype=np.float32))
    if need_identf:
        identf = ctx.sbuf("IDENTF", [128, 128], F32)
        ctx.consts["IDENTF"] = identf
    ones = ctx.sbuf("ONES", [1, 128], BF16)
    ctx.consts["ONES"] = ones
    onesf128 = ctx.sbuf("ONESF128", [1, 128], F32)
    ctx.consts["ONESF128"] = onesf128
    oned = ctx.din("c_ones", np.ones((1, 512), np.float32))

    @block.gpsimd
    def _(g):
        ctx.inc(g.dma_start(out=ident[:], in_=idd[:]), "wdma", 16)
        if need_identf:
            ctx.inc(g.dma_start(out=identf[:], in_=idd[:]), "wdma", 16)
        ctx.inc(g.dma_start(out=ones[:], in_=oned[0:1, 0:128]), "wdma", 16)
        ctx.inc(g.dma_start(out=onesf128[:], in_=oned[0:1, 0:128]), "wdma", 16)
    ctx.consts["CONST_TOK"] = ctx.tok("wdma")


def emit_load_x(ctx, x_dram, x_sb):
    block = ctx.block

    @block.sync
    def _(sync):
        src = x_dram.rearrange("(t p) d -> p t d", p=128)
        dst = x_sb[:].rearrange("p (t d) -> p t d", d=D)
        ctx.inc(sync.dma_start(out=dst, in_=src), "xin", 16)


def emit_store_x(ctx, x_sb, y_dram):
    block = ctx.block

    @block.sync
    def _(sync):
        for name in ("pe", "vec", "act", "gp"):
            if name in ctx.cnt and ctx.cnt[name]:
                ctx.wait(sync, name)
        dst = y_dram.rearrange("(t p) d -> p t d", p=128)
        src2 = x_sb[:].rearrange("p (t d) -> p t d", d=D)
        ctx.inc(sync.dma_start(out=dst, in_=src2), "sdma", 16)
        ctx.wait(sync, "sdma")


def emit_ln_transpose(ctx, x_sb, out_T, ps_tr, gb_ap=None, out_also=None, scope=None,
                      lnrow_buf=None):
    block = ctx.block
    raw = ctx.sbuf("ln_raw", [128, 12], F32, scope)
    mv = ctx.sbuf("ln_mv", [128, 2 * NT], F32, scope)
    rstd = ctx.sbuf("ln_rstd", [128, NT], F32, scope)
    nmr = ctx.sbuf("ln_nmr", [128, NT], F32, scope)
    lnrow = lnrow_buf if lnrow_buf is not None else ctx.sbuf("ln_row", [128, D], BF16, scope)

    @block.vector
    def _(v):
        ctx.wait(v, "xin")
        if ctx.cnt.get("vec"):
            ctx.wait(v, "vec")
        for t in range(NT):
            xt = x_sb[:, t * D:(t + 1) * D]
            v.bn_stats(raw[:, 0:6], xt[:, 0:512])
            ctx.inc(v.bn_stats(raw[:, 6:12], xt[:, 512:1024]), "vec")
            ctx.wait(v, "vec")
            ctx.inc(v.bn_aggr(mv[:, 2 * t:2 * t + 2], raw[:]), "vec")
            ctx.wait(v, "vec")
        var_v = mv[:].rearrange("p (t two) -> p t two", two=2)[:, :, 1:2]
        ctx.inc(v.tensor_scalar_add(rstd[:].rearrange("p t -> p t ()"), var_v, 1e-5), "vec")

    @block.scalar
    def _(sc):
        ctx.wait(sc, "vec")
        ctx.inc(sc.activation(rstd[:], rstd[:], AF.Sqrt), "act")

    @block.vector
    def _(v):
        ctx.wait(v, "act")
        ctx.inc(v.reciprocal(rstd[:], rstd[:]), "vec")
        ctx.wait(v, "vec")
        mean_v = mv[:].rearrange("p (t two) -> p t two", two=2)[:, :, 0:1]
        i = v.tensor_tensor(nmr[:].rearrange("p t -> p t ()"), mean_v,
                            rstd[:].rearrange("p t -> p t ()"), ALU.mult)
        ctx.inc(i, "vec")
        ctx.wait(v, "vec")
        ctx.inc(v.tensor_scalar_mul(nmr[:], nmr[:], -1.0), "vec")

    stats_tok = ctx.tok("vec")
    for t in range(NT):
        @block.scalar
        def _(sc, t=t):
            if t == 0:
                sc.wait_ge(ctx.sget("vec"), stats_tok)
            else:
                ctx.wait(sc, "pe")
            xt = x_sb[:, t * D:(t + 1) * D]
            i = sc.activation(lnrow[:], xt, AF.Identity,
                              bias=nmr[:, t:t + 1], scale=rstd[:, t:t + 1])
            ctx.inc(i, "act")

        @block.tensor
        def _(tens, t=t):
            ctx.wait(tens, "act")
            if t == 0:
                ctx.wait(tens, "wdma")
            if t > 0:
                ctx.wait(tens, "vec")
            for half in range(2):
                for j4 in range(4):
                    j = half * 4 + j4
                    i = tens.matmul(ps_tr[:, 256 * half + 0:][:, 128 * j4:128 * j4 + 128]
                                    if False else
                                    ps_tr[:, 512 * half + 128 * j4: 512 * half + 128 * j4 + 128],
                                    lnrow[:, 128 * j:128 * (j + 1)],
                                    ctx.consts["IDENT"][:], start=True, stop=True)
            ctx.inc(i, "pe")

        @block.vector
        def _(v, t=t):
            ctx.wait(v, "pe")
            src = ps_tr[:].rearrange("p (j c) -> p j c", c=128)
            dst = out_T[:].rearrange("p (j s) -> p j s", s=S)[:, :, 128 * t:128 * t + 128]
            if gb_ap is None:
                i = v.tensor_copy(dst, src)
            else:
                g_t, b_t = gb_ap
                i = v.scalar_tensor_tensor(dst, src, g_t[:].rearrange("p j -> p j ()"),
                                           b_t[:].rearrange("p j -> p j ()"),
                                           ALU.mult, ALU.add)
            ctx.inc(i, "vec")
            if out_also is not None:
                ctx.wait(v, "vec")
                ctx.inc(v.tensor_copy(
                    out_also[:].rearrange("p (j s) -> p j s", s=S)[:, :, 128 * t:128 * t + 128],
                    src), "vec")


def load_weight_bf16(ctx, name, arr2d, tile_rows=128):
    R, C = arr2d.shape
    assert R % tile_rows == 0
    d = ctx.din(f"w_{name}_{ctx.nid()}", np.ascontiguousarray(arr2d, np.float32))
    sb = ctx.sbuf("ws_" + name, [tile_rows, (R // tile_rows) * C], BF16)
    block = ctx.block

    @block.gpsimd
    def _(g):
        src = d.rearrange("(j p) c -> p j c", p=tile_rows)
        dst = sb[:].rearrange("p (j c) -> p j c", j=R // tile_rows)
        ctx.inc(g.dma_start(out=dst, in_=src), "wdma", 16)
    return sb, ctx.tok("wdma")


def load_row_const(ctx, name, arr, dtype=F32, guards=(), scope=None):
    a = np.ascontiguousarray(arr, np.float32)
    if a.ndim == 1:
        a = a[None, :]
    d = ctx.din(f"r_{name}_{ctx.nid()}", a)
    sb = ctx.sbuf("rs_" + name, list(a.shape), dtype, scope)
    block = ctx.block

    @block.gpsimd
    def _(g):
        for nm in ("pe", "vec", "act"):
            if ctx.cnt.get(nm):
                ctx.wait(g, nm)
        ctx.inc(g.dma_start(out=sb[:], in_=d[:]), "wdma", 16)
    return sb, ctx.tok("wdma")




WSLOT_COLS = 2048


def wload(ctx, dram_ap, R, c0, c1, slot, slot_idx, pe_guard=None):
    import os as _os
    serial = bool(_os.environ.get("KSERIAL"))
    """DMA-cast fp32 dram weight rows[all] cols[c0:c1] into bf16 slot.
    Slot layout: row-tile j at cols [j*(c1-c0) : (j+1)*(c1-c0)].
    Each slot has its own completion semaphore ("wd0"/"wd1"); waits must use
    the slot's full cumulative count (DMA completions are unordered).
    pe_guard: wait for PE count before overwriting the slot.
    Returns (sem_name, token)."""
    block = ctx.block
    width = c1 - c0
    assert (R // 128) * width <= ctx.consts.get("WSLOT_COLS", WSLOT_COLS)
    semname = f"wd{slot_idx}"

    @block.gpsimd
    def _(g):
        if serial:
            for nm in ("pe", "vec", "act"):
                if ctx.cnt.get(nm):
                    ctx.wait(g, nm)
        elif pe_guard is not None:
            g.wait_ge(ctx.sget("pe"), pe_guard)
        src = dram_ap.rearrange("(j p) c -> p j c", p=128)[:, :, c0:c1]
        dst = slot[:].rearrange("p (j c) -> p j c", c=width)[:, 0:R // 128, :]
        ctx.inc(g.dma_start(out=dst, in_=src), semname, 16)
    return semname, ctx.tok(semname)


def gemm_T(ctx, dram_w, din, dout, actT, act_S, dstT, dst_S, ps,
           bias_tile=None, pre_wait=None, s_lo=0, s_hi=None, dst_off=0,
           act_off=0):
    block = ctx.block
    WS = ctx.consts["WSLOTS"]
    nj = din // 128
    w_cols = ctx.consts.get("WSLOT_COLS", WSLOT_COLS) // nj
    w_cols -= w_cols % 128
    s_hi = act_S if s_hi is None else s_hi
    nsc = (s_hi - s_lo) // 512
    for li, c0 in enumerate(range(0, dout, w_cols)):
        slot = WS[li % 2]
        wsem, tok = wload(ctx, dram_w, din, c0, c0 + w_cols, slot, li % 2,
                          pe_guard=ctx.tok("pe"))
        for jo_i in range(w_cols // 128):
            jo = c0 // 128 + jo_i
            for scn in range(nsc):
                s0 = s_lo + 512 * scn

                @block.tensor
                def _(t, jo_i=jo_i, jo=jo, s0=s0, slot=slot, tok=tok, wsem=wsem):
                    t.wait_ge(ctx.sget(wsem), tok)
                    if pre_wait is not None:
                        for nm, tv in pre_wait:
                            t.wait_ge(ctx.sget(nm), tv)
                    ctx.wait(t, "act")
                    ctx.wait(t, "vec")
                    ctx.wait(t, "wdma")
                    for j in range(nj):
                        i = t.matmul(ps,
                                     slot[:, j * w_cols + 128 * jo_i: j * w_cols + 128 * jo_i + 128],
                                     actT[:, j * act_S + act_off + s0: j * act_S + act_off + s0 + 512],
                                     start=(j == 0), stop=(j == nj - 1))
                    ctx.inc(i, "pe")

                @block.scalar
                def _(sc, jo=jo, s0=s0):
                    ctx.wait(sc, "pe")
                    o0 = jo * dst_S + dst_off + (s0 - s_lo)
                    if bias_tile is not None:
                        i = sc.activation(dstT[:, o0: o0 + 512],
                                          ps, AF.Identity, bias=bias_tile[:, jo:jo + 1])
                    else:
                        i = sc.activation(dstT[:, o0: o0 + 512],
                                          ps, AF.Identity)
                    ctx.inc(i, "act")


def gemm_N(ctx, dram_w, din, dout, actT, act_S, ps, evac,
           bias_row=None, pre_wait=None, evac_chan="vec"):
    block = ctx.block
    WS = ctx.consts["WSLOTS"]
    nj = din // 128
    w_cols = min(512, ctx.consts.get("WSLOT_COLS", WSLOT_COLS) // nj)
    ones_sb = ctx.consts["ONES"]
    for li, c0 in enumerate(range(0, dout, w_cols)):
        slot = WS[li % 2]
        wsem, tok = wload(ctx, dram_w, din, c0, c0 + w_cols, slot, li % 2,
                          pe_guard=ctx.tok("pe"))
        for t_i in range(NT):
            @block.tensor
            def _(t, t_i=t_i, c0=c0, slot=slot, tok=tok, wsem=wsem):
                t.wait_ge(ctx.sget(wsem), tok)
                if pre_wait is not None:
                    for nm, tv in pre_wait:
                        t.wait_ge(ctx.sget(nm), tv)
                ctx.wait(t, evac_chan)
                ctx.wait(t, "vec")
                ctx.wait(t, "wdma")
                for j in range(nj):
                    i = t.matmul(ps[:, 0:w_cols],
                                 actT[:, j * act_S + 128 * t_i: j * act_S + 128 * t_i + 128],
                                 slot[:, j * w_cols: (j + 1) * w_cols],
                                 start=(j == 0), stop=(j == nj - 1 and bias_row is None))
                if bias_row is not None:
                    i = t.matmul(ps[:, 0:w_cols], ones_sb[0:1, 0:128],
                                 bias_row[0:1, c0:c0 + w_cols],
                                 start=False, stop=True)
                ctx.inc(i, "pe")

            if evac_chan == "vec":
                @block.vector
                def _(v, t_i=t_i, c0=c0):
                    ctx.wait(v, "pe")
                    ctx.inc(evac(v, ps[:, 0:w_cols], t_i, c0, w_cols), "vec")
            else:
                @block.scalar
                def _(sc, t_i=t_i, c0=c0):
                    ctx.wait(sc, "pe")
                    ctx.inc(evac(sc, ps[:, 0:w_cols], t_i, c0, w_cols), "act")



def emit_l0aa(ctx, x_sb, w, PS_A, PS_B):
    block = ctx.block
    H, P = 16, 64

    Wq = np.transpose(w["L0_attn_a_q_w"], (2, 0, 1)).reshape(D, H * P)
    Wk = np.transpose(w["L0_attn_a_k_w"], (2, 0, 1)).reshape(D, H * P)
    Wv = np.transpose(w["L0_attn_a_v_w"], (2, 0, 1)).reshape(D, H * P)
    Wl = np.ascontiguousarray(np.asarray(w["L0_attn_a_l_w"]).T)
    bq = np.asarray(w["L0_attn_a_q_b"])[:, 0, :].reshape(H * P)
    bk = np.asarray(w["L0_attn_a_k_b"])[:, 0, :].reshape(H * P)
    bv = np.asarray(w["L0_attn_a_v_b"])[:, 0, :].reshape(H * P)
    bl = np.asarray(w["L0_attn_a_l_b"]).reshape(D)

    bq_sb, _ = load_row_const(ctx, "l0aa_bq", bq.reshape(8, 128).T)
    bk_sb, _ = load_row_const(ctx, "l0aa_bk", bk.reshape(8, 128).T)
    QT = ctx.sbuf("aa_QT", [128, DT * S], BF16)
    KT = ctx.sbuf("aa_KT", [128, DT * S], BF16)
    V = ctx.sbuf("aa_V", [128, NT * (H * P)], BF16)
    Eb = ctx.sbuf("aa_E", [128, S], BF16)
    ln_scope = ExitStack()
    lnT = ctx.sbuf("aa_lnT", [128, DT * S], BF16, ln_scope)
    bv_sb = ctx.sbuf("aa_bv", [1, D], BF16, ln_scope)
    dbv = ctx.din("r_l0aa_bv", bv[None, :].astype(np.float32))

    @block.gpsimd
    def _(g):
        for nm in ("pe", "vec", "act"):
            if ctx.cnt.get(nm):
                ctx.wait(g, nm)
        ctx.inc(g.dma_start(out=bv_sb[:], in_=dbv[:]), "wdma", 16)

    emit_ln_transpose(ctx, x_sb, lnT, PS_A[:, 0:1024], scope=ln_scope,
                      lnrow_buf=Eb[:, 0:D])

    dq = ctx.din("w_l0aa_q", Wq)
    dk = ctx.din("w_l0aa_k", Wk)
    dv = ctx.din("w_l0aa_v", Wv)
    dl = ctx.din("w_l0aa_l", Wl)

    ps_q = PS_A[:, 0:512]
    ps_v = PS_A[:, 1024:1536]

    gemm_T(ctx, dq, D, D, lnT, S, QT, S, ps_q, bias_tile=bq_sb)
    gemm_T(ctx, dk, D, D, lnT, S, KT, S, ps_q, bias_tile=bk_sb)

    def v_evac(v, psap, t_i, c0, w):
        return v.tensor_copy(V[:, t_i * 1024 + c0: t_i * 1024 + c0 + w], psap)
    gemm_N(ctx, dv, D, D, lnT, S, ps_v, v_evac, bias_row=bv_sb)

    ln_scope.close()
    bl_sb, _ = load_row_const(ctx, "l0aa_bl", bl[None, :], BF16,
                              guards=("pe", "vec", "act"))

    ps_sc = PS_A
    ps_cx = PS_B[0:64, :]
    negm = ctx.sbuf("aa_negm", [128, 1], F32)
    sums = ctx.sbuf("aa_sums", [128, 1], F32)
    rsum = ctx.sbuf("aa_rsum", [128, 1], F32)
    Vh = ctx.sbuf("aa_Vh", [128, P], BF16)

    proj_act_tok = ctx.tok("act")
    proj_vec_tok = ctx.tok("vec")

    for h in range(H):
        jt = h // 2
        ro = 64 * (h % 2)
        for kt in range(NT):
            @block.tensor
            def _(t, h=h, kt=kt, jt=jt, ro=ro):
                if h == 0 and kt == 0:
                    t.wait_ge(ctx.sget("act"), proj_act_tok)
                    t.wait_ge(ctx.sget("vec"), proj_vec_tok)
                else:
                    ctx.wait(t, "act")
                for qc in range(4):
                    i = t.matmul(ps_sc[:, 512 * qc:512 * qc + 512],
                                 KT[ro:ro + 64, jt * S + 128 * kt: jt * S + 128 * kt + 128],
                                 QT[ro:ro + 64, jt * S + 512 * qc: jt * S + 512 * qc + 512],
                                 start=True, stop=True)
                    ctx.inc(i, "pe")

            @block.vector
            def _(v, h=h, kt=kt):
                ctx.wait(v, "pe")
                i = v.reduce_max(negm[:], ps_sc[:], axis=AX.X, negate=True)
                ctx.inc(i, "vec")
                ctx.wait(v, "vec")
                ctx.inc(v.tensor_scalar_mul(negm[:], negm[:], 0.125), "vec")

            @block.scalar
            def _(sc, h=h, kt=kt):
                ctx.wait(sc, "vec")
                ctx.wait(sc, "pe")
                i = sc.activation(Eb[:], ps_sc[:], AF.Exp,
                                  bias=negm[:], scale=0.125, accum_out=sums[:])
                ctx.inc(i, "act")

            @block.vector
            def _(v, h=h, kt=kt):
                ctx.wait(v, "act")
                ctx.inc(v.reciprocal(rsum[:], sums[:]), "vec")
                ctx.wait(v, "vec")
                i = v.tensor_scalar(Vh[:], V[:, kt * 1024 + 64 * h: kt * 1024 + 64 * h + 64],
                                    rsum[:], None, ALU.mult)
                ctx.inc(i, "vec")

            @block.tensor
            def _(t, h=h, kt=kt):
                ctx.wait(t, "vec")
                for qc in range(4):
                    i = t.matmul(ps_cx[:, 512 * qc:512 * qc + 512],
                                 Vh[:], Eb[:, 512 * qc:512 * qc + 512],
                                 start=(kt == 0), stop=(kt == NT - 1))
                ctx.inc(i, "pe")

        @block.scalar
        def _(sc, h=h, jt=jt, ro=ro):
            ctx.wait(sc, "pe")
            i = sc.activation(KT[ro:ro + 64, jt * S: jt * S + S], ps_cx, AF.Identity)
            ctx.inc(i, "act")

    attn_act_tok = ctx.tok("act")
    ps_o = PS_B[:, 0:512]

    def o_evac(v, psap, t_i, c0, w):
        dst = x_sb[:, t_i * D + c0: t_i * D + c0 + w]
        return v.tensor_add(dst, dst, psap)
    gemm_N(ctx, dl, D, D, KT, S, ps_o, o_evac, bias_row=bl_sb,
           pre_wait=[("act", attn_act_tok)])





def emit_l0ab(ctx, x_sb, w, PS_A, PS_B):
    block = ctx.block
    H, DH = 16, 64

    w_q = np.asarray(w["L0_attn_b_w_q"], np.float32)
    w_kv = np.asarray(w["L0_attn_b_w_kv"], np.float32)
    wk = np.ascontiguousarray(w_kv[:, 0:D])
    wv = np.ascontiguousarray(w_kv[:, D:2 * D])
    w_o = np.asarray(w["L0_attn_b_w_out"], np.float32)
    b_o = np.asarray(w["L0_attn_b_b_out"]).reshape(D)
    scale_in = np.asarray(w["L0_attn_b_scale"]).reshape(H, 1)

    dq = ctx.din("w_l0ab_q", w_q)
    dk = ctx.din("w_l0ab_k", wk)
    dv = ctx.din("w_l0ab_v", wv)
    do = ctx.din("w_l0ab_o", w_o)

    sel = np.zeros((128, 8 * 16), np.float32)
    selT = np.zeros((16, 8 * 128), np.float32)
    for j in range(8):
        for p in range(128):
            h = 2 * j + (1 if p >= 64 else 0)
            sel[p, j * 16 + h] = 1.0
            selT[h, j * 128 + p] = 1.0
    sel_sb, _ = load_row_const(ctx, "l0ab_sel", sel, BF16)
    selT_sb, _ = load_row_const(ctx, "l0ab_selT", selT, F32)
    esc_d = ctx.din("r_l0ab_scale", scale_in)
    esc = ctx.sbuf("ab_esc", [H, 1], F32)

    @block.gpsimd
    def _(g):
        for nm in ("pe", "vec", "act"):
            if ctx.cnt.get(nm):
                ctx.wait(g, nm)
        ctx.inc(g.dma_start(out=esc[:], in_=esc_d[:]), "wdma", 16)

    @block.scalar
    def _(sc):
        ctx.wait(sc, "wdma")
        ctx.inc(sc.activation(esc[:], esc[:], AF.Exp), "act")

    QT = ctx.sbuf("ab_QT", [128, DT * S], BF16)
    KT = ctx.sbuf("ab_KT", [128, DT * S], BF16)
    V = ctx.sbuf("ab_V", [128, NT * D], BF16)
    ln_scope = ExitStack()
    lnT = ctx.sbuf("ab_lnT", [128, DT * S], BF16, ln_scope)
    emit_ln_transpose(ctx, x_sb, lnT, PS_A[:, 0:1024], scope=ln_scope,
                      lnrow_buf=QT[:, 0:D])

    gemm_T(ctx, dq, D, D, lnT, S, QT, S, PS_A[:, 0:512])
    gemm_T(ctx, dk, D, D, lnT, S, KT, S, PS_A[:, 512:1024])

    def v_evac(v, psap, t_i, c0, wd):
        return v.tensor_copy(V[:, t_i * D + c0: t_i * D + c0 + wd], psap)
    gemm_N(ctx, dv, D, D, lnT, S, PS_A[:, 1024:1536], v_evac)
    ln_scope.close()

    sqb = ctx.sbuf("ab_sq", [128, 512], BF16)
    nrm = ctx.sbuf("ab_nrm", [16, 512], F32)
    ps_ss = PS_B[0:16, 0:512]
    ps_rb = PS_B[:, 512:1024]
    for which, src in (("q", QT), ("k", KT)):
        for scn in range(4):
            for j in range(DT):
                @block.scalar
                def _(sc, j=j, scn=scn, src=src):
                    ctx.wait(sc, "act")
                    ctx.wait(sc, "vec")
                    ctx.wait(sc, "pe")
                    i = sc.activation(sqb[:], src[:, j * S + 512 * scn: j * S + 512 * scn + 512],
                                      AF.Square)
                    ctx.inc(i, "act")

                @block.tensor
                def _(t, j=j, scn=scn):
                    ctx.wait(t, "act")
                    if j == 0:
                        ctx.wait(t, "vec")
                    i = t.matmul(ps_ss, sel_sb[:, j * 16:(j + 1) * 16], sqb[:],
                                 start=(j == 0), stop=(j == DT - 1))
                    ctx.inc(i, "pe")

            @block.scalar
            def _(sc, which=which, scn=scn):
                ctx.wait(sc, "pe")
                ctx.inc(sc.activation(nrm[:], ps_ss, AF.Sqrt), "act")

            @block.vector
            def _(v, which=which, scn=scn):
                ctx.wait(v, "act")
                ctx.inc(v.tensor_scalar_max(nrm[:], nrm[:], 1e-12), "vec")
                ctx.wait(v, "vec")
                ctx.inc(v.reciprocal(nrm[:], nrm[:]), "vec")
                if which == "q":
                    ctx.wait(v, "vec")
                    ctx.inc(v.tensor_scalar(nrm[:], nrm[:], esc[:], None, ALU.mult), "vec")

            for j in range(DT):
                @block.tensor
                def _(t, j=j, scn=scn):
                    ctx.wait(t, "vec")
                    i = t.matmul(ps_rb, selT_sb[0:16, j * 128:(j + 1) * 128], nrm[:],
                                 start=True, stop=True)
                    ctx.inc(i, "pe")

                @block.vector
                def _(v, j=j, scn=scn, src=src):
                    ctx.wait(v, "pe")
                    sl = src[:, j * S + 512 * scn: j * S + 512 * scn + 512]
                    ctx.inc(v.tensor_mul(sl, sl, ps_rb), "vec")

    QA = ctx.sbuf("ab_QA", [66, 2 * S], BF16)
    KA = ctx.sbuf("ab_KA", [66, 2 * S], BF16)
    ET = ctx.sbuf("ab_ET", [128, 2 * S], BF16)
    scr = ctx.sbuf("ab_scr", [128, S], BF16)
    negm = ctx.sbuf("ab_negm", [128, NT], F32)
    sums = ctx.sbuf("ab_sums", [128, NT], F32)
    mls = ctx.sbuf("ab_mls", [128, NT], F32)
    mhb = ctx.sbuf("ab_mhb", [128, NT], BF16)
    mhl = ctx.sbuf("ab_mhl", [128, 2 * NT], F32)
    norm_vec_tok = ctx.tok("vec")

    @block.vector
    def _(v):
        ctx.wait(v, "vec")
        ctx.inc(v.memset(KA[64:66, :], -1.0), "vec")

    for pair in range(8):
        for hh in range(2):
            h = 2 * pair + hh
            jt = h // 2
            ro = 64 * (h % 2)
            for qt in range(NT):
                @block.tensor
                def _(t, qt=qt, jt=jt, ro=ro, pair=pair, hh=hh):
                    if pair == 0 and hh == 0 and qt == 0:
                        t.wait_ge(ctx.sget("vec"), norm_vec_tok)
                    ctx.wait(t, "act")
                    for kc in range(4):
                        i = t.matmul(PS_A[:, 512 * kc:512 * kc + 512],
                                     QT[ro:ro + 64, jt * S + 128 * qt: jt * S + 128 * qt + 128],
                                     KT[ro:ro + 64, jt * S + 512 * kc: jt * S + 512 * kc + 512],
                                     start=True, stop=True)
                    ctx.inc(i, "pe")

                @block.vector
                def _(v, qt=qt):
                    ctx.wait(v, "pe")
                    ctx.inc(v.reduce_max(negm[:, qt:qt + 1], PS_A[:], axis=AX.X,
                                         negate=True), "vec")

                @block.scalar
                def _(sc, qt=qt):
                    ctx.wait(sc, "vec")
                    ctx.wait(sc, "act")
                    i = sc.activation(scr[:], PS_A[:], AF.Exp,
                                      bias=negm[:, qt:qt + 1],
                                      accum_out=sums[:, qt:qt + 1])
                    ctx.inc(i, "act")

            @block.scalar
            def _(sc):
                ctx.wait(sc, "act")
                ctx.inc(sc.activation(mls[:], sums[:], AF.Ln), "act")

            @block.vector
            def _(v):
                ctx.wait(v, "act")
                ctx.inc(v.tensor_sub(mls[:], mls[:], negm[:]), "vec")
                ctx.wait(v, "vec")
                ctx.inc(v.tensor_copy(mhb[:], mls[:]), "vec")
                ctx.wait(v, "vec")
                ctx.inc(v.tensor_copy(mhl[:].rearrange("p (t two) -> p two t", two=2)[:, 0, :],
                                      mhb[:]), "vec")
                ctx.wait(v, "vec")
                ctx.inc(v.tensor_sub(mhl[:].rearrange("p (t two) -> p two t", two=2)[:, 1, :],
                                     mls[:],
                                     mhl[:].rearrange("p (t two) -> p two t", two=2)[:, 0, :]),
                        "vec")

            @block.vector
            def _(v, hh=hh, jt=jt, ro=ro):
                ctx.wait(v, "vec")
                ctx.inc(v.tensor_copy(QA[0:64, hh * S:(hh + 1) * S],
                                      QT[ro:ro + 64, jt * S: jt * S + S]), "vec")
                ctx.wait(v, "vec")
                ctx.inc(v.tensor_copy(KA[0:64, hh * S:(hh + 1) * S],
                                      KT[ro:ro + 64, jt * S: jt * S + S]), "vec")

            for qt in range(NT):
                @block.tensor
                def _(t, qt=qt):
                    ctx.wait(t, "vec")
                    i = t.matmul(PS_B[0:2, 0:128], mhl[:, 2 * qt:2 * qt + 2],
                                 ctx.consts["IDENTF"][:], start=True, stop=True)
                    ctx.inc(i, "pe")

                @block.vector
                def _(v, qt=qt, hh=hh):
                    ctx.wait(v, "pe")
                    dst = QA[64:66, hh * S + 128 * qt: hh * S + 128 * qt + 128]
                    ctx.inc(v.tensor_copy(dst, PS_B[0:2, 0:128]), "vec")

        pair_vec_tok = ctx.tok("vec")
        for kt in range(NT):
            for hh in range(2):
                h = 2 * pair + hh

                @block.tensor
                def _(t, kt=kt, hh=hh, pair_vec_tok=pair_vec_tok):
                    t.wait_ge(ctx.sget("vec"), pair_vec_tok)
                    ctx.wait(t, "act")
                    for qc in range(4):
                        i = t.matmul(PS_A[:, 512 * qc:512 * qc + 512],
                                     KA[0:66, hh * S + 128 * kt: hh * S + 128 * kt + 128],
                                     QA[0:66, hh * S + 512 * qc: hh * S + 512 * qc + 512],
                                     start=True, stop=True)
                    ctx.inc(i, "pe")

                @block.scalar
                def _(sc, kt=kt, hh=hh):
                    ctx.wait(sc, "pe")
                    ctx.wait(sc, "act")
                    i = sc.activation(ET[:, hh * S:(hh + 1) * S], PS_A[:], AF.Exp)
                    ctx.inc(i, "act")

                @block.tensor
                def _(t, kt=kt, hh=hh, h=h):
                    ctx.wait(t, "act")
                    for qc in range(4):
                        i = t.matmul(PS_B[64 * hh:64 * hh + 64, 512 * qc:512 * qc + 512],
                                     V[:, kt * D + 64 * h: kt * D + 64 * h + 64],
                                     ET[:, hh * S + 512 * qc: hh * S + 512 * qc + 512],
                                     start=(kt == 0), stop=(kt == NT - 1),
                                     tile_position=(0, 64 * hh),
                                     skip_group_check=True)
                    ctx.inc(i, "pe")

        @block.scalar
        def _(sc, pair=pair):
            ctx.wait(sc, "pe")
            i = sc.activation(QT[:, pair * S:(pair + 1) * S], PS_B[:, 0:S], AF.Identity)
            ctx.inc(i, "act")

    bo_sb, _ = load_row_const(ctx, "l0ab_bo", b_o[None, :], BF16,
                              guards=("pe", "vec", "act"))
    att_act_tok = ctx.tok("act")

    def o_evac(v, psap, t_i, c0, wd):
        dst = x_sb[:, t_i * D + c0: t_i * D + c0 + wd]
        return v.tensor_add(dst, dst, psap)
    gemm_N(ctx, do, D, D, QT, S, PS_A[:, 0:512], o_evac, bias_row=bo_sb,
           pre_wait=[("act", att_act_tok)])





def emit_l1aa(ctx, x_sb, w, PS_A, PS_B):
    block = ctx.block
    H, DH, RD = 16, 64, 32
    EPS = 1e-6
    SCL = float(1.0 / np.sqrt(np.float32(96.0)))

    W_kv = np.asarray(w["L1_attn_a_W_kv"], np.float32)
    g_kv = np.asarray(w["L1_attn_a_g_kv"], np.float32)
    W_kdec = np.asarray(w["L1_attn_a_W_kdec"], np.float32)
    W_vdec = np.asarray(w["L1_attn_a_W_vdec"], np.float32)
    W_q = np.asarray(w["L1_attn_a_W_q"], np.float32)
    W_qdec = np.asarray(w["L1_attn_a_W_qdec"], np.float32)
    W_krope = np.asarray(w["L1_attn_a_W_krope"], np.float32)
    W_qrope = np.asarray(w["L1_attn_a_W_qrope"], np.float32)
    W_o = np.asarray(w["L1_attn_a_W_o"], np.float32)

    def perm_rope(W):
        Wr = np.zeros_like(W)
        for h in range(H):
            blk = W[:, 32 * h:32 * h + 32]
            Wr[:, 32 * h:32 * h + 16] = blk[:, 16:32]
            Wr[:, 32 * h + 16:32 * h + 32] = blk[:, 0:16]
        return np.ascontiguousarray(Wr)

    d_kv = ctx.din("w_l1aa_kv", W_kv)
    d_kdec = ctx.din("w_l1aa_kdec", W_kdec)
    d_vdec = ctx.din("w_l1aa_vdec", W_vdec)
    d_q = ctx.din("w_l1aa_q", W_q)
    d_qdec = ctx.din("w_l1aa_qdec", W_qdec)
    d_krp = ctx.din("w_l1aa_krp", W_krope)
    d_krr = ctx.din("w_l1aa_krr", perm_rope(W_krope))
    d_qrp = ctx.din("w_l1aa_qrp", W_qrope)
    d_qrr = ctx.din("w_l1aa_qrr", perm_rope(W_qrope))
    d_o = ctx.din("w_l1aa_o", W_o)

    inv_freq = 1.0 / 10000.0 ** (np.arange(0, RD, 2, dtype=np.float64) / RD)
    tpos = np.arange(S, dtype=np.float64)
    fr = np.outer(tpos, inv_freq)
    emb = np.concatenate([fr, fr], -1)
    cos32 = np.cos(emb).T
    sin32 = np.sin(emb).T
    cosT = np.zeros((128, S), np.float32)
    sinT = np.zeros((128, S), np.float32)
    for r in range(128):
        d = r % 32
        cosT[r] = cos32[d]
        sinT[r] = sin32[d] * (-1.0 if d < 16 else 1.0)
    cos_sb, _ = load_row_const(ctx, "l1aa_cos", cosT, BF16)
    sin_sb, _ = load_row_const(ctx, "l1aa_sin", sinT, BF16)
    gk_tile, _ = load_row_const(ctx, "l1aa_gkv", g_kv.reshape(2, 128).T)
    onescolb, _ = load_row_const(ctx, "l1aa_onescol", np.ones((128, 1), np.float32), BF16)
    maskTs = []
    for r in range(4):
        m = np.zeros((128, 512), np.float32)
        for qb in range(4):
            if qb < r:
                m[:, qb * 128:(qb + 1) * 128] = -1e30
            elif qb == r:
                for p in range(128):
                    m[p, qb * 128: qb * 128 + p] = -1e30
        maskTs.append(m)
    maskT_sb, _ = load_row_const(ctx, "l1aa_maskT", np.concatenate(maskTs, 1), BF16)
    maskD0 = np.zeros((128, 128), np.float32)
    for p in range(128):
        maskD0[p, p + 1:] = -1e30
    maskD_sb, _ = load_row_const(ctx, "l1aa_maskD", maskD0, BF16)
    eps_sb, _ = load_row_const(ctx, "l1aa_eps", np.array([[EPS]], np.float32))
    onesrowf, _ = load_row_const(ctx, "l1aa_onesrowf", np.ones((1, 128), np.float32), F32)

    QC = ctx.sbuf("ma_QC", [128, 4 * S], BF16)
    KVT = ctx.sbuf("ma_KVT", [128, 2 * S], BF16)
    AOT = ctx.sbuf("ma_AOT", [128, DT * S], BF16)
    KRF = ctx.sbuf("ma_KRF", [128, 4 * S], BF16)
    scr512 = ctx.sbuf("ma_scr", [128, 512], BF16)
    ph1 = ExitStack()
    lnT = ctx.sbuf("ma_lnT", [128, DT * S], BF16, ph1)
    KRR = AOT[:, 4 * S:8 * S]
    rms = ctx.sbuf("ma_rms", [1, S], F32, ph1)
    import os as _os9
    if _os9.environ.get("KLNZ"):
        @block.vector
        def _(v):
            ctx.wait(v, "xin")
            ctx.inc(v.memset(lnT[:], 0.0), "vec")
            ctx.wait(v, "vec")
            ctx.inc(v.memset(QC[:], 0.0), "vec")
            ctx.wait(v, "vec")
            ctx.inc(v.memset(KVT[:], 0.0), "vec")
    else:
        emit_ln_transpose(ctx, x_sb, lnT, PS_A[:, 0:1024], lnrow_buf=AOT[:, 0:D],
                          scope=ph1)

    _sk2 = set((_os9.environ.get("KSKIP2") or "").split(","))
    if "kv" not in _sk2:
        gemm_T(ctx, d_kv, D, 256, lnT, S, KVT, S, PS_A[:, 0:512])
    if "q" not in _sk2:
        gemm_T(ctx, d_q, D, 512, lnT, S, QC, S, PS_A[:, 512:1024])
    if "krg" not in _sk2:
        gemm_T(ctx, d_krp, D, 512, lnT, S, KRF, S, PS_A[:, 0:512])
        gemm_T(ctx, d_krr, D, 512, lnT, S, KRR, S, PS_A[:, 512:1024])
    for j in (() if "krft" in _sk2 else range(4)):
        for scn in range(4):
            @block.vector
            def _(v, j=j, scn=scn):
                ctx.wait(v, "act")
                ctx.wait(v, "vec")
                sl = slice(j * S + 512 * scn, j * S + 512 * scn + 512)
                ssl = slice(512 * scn, 512 * scn + 512)
                i = v.tensor_mul(scr512[:], KRF[:, sl], cos_sb[:, ssl])
                ctx.inc(i, "vec")
                ctx.wait(v, "vec")
                i = v.tensor_mul(KRR[:, sl], KRR[:, sl], sin_sb[:, ssl])
                ctx.inc(i, "vec")
                ctx.wait(v, "vec")
                i = v.tensor_add(KRF[:, sl], scr512[:], KRR[:, sl])
                ctx.inc(i, "vec")

    if "rms" in _sk2:
        _rms_skipped = True
    for scn in (() if "rms" in _sk2 else range(4)):
        for j in range(2):
            @block.scalar
            def _(sc, j=j, scn=scn):
                ctx.wait(sc, "act")
                ctx.wait(sc, "vec")
                ctx.wait(sc, "pe")
                i = sc.activation(scr512[:], KVT[:, j * S + 512 * scn: j * S + 512 * scn + 512],
                                  AF.Square)
                ctx.inc(i, "act")

            @block.tensor
            def _(t, j=j, scn=scn):
                ctx.wait(t, "act")
                i = t.matmul(PS_B[0:1, 0:512], onescolb[:], scr512[:],
                             start=(j == 0), stop=(j == 1))
                ctx.inc(i, "pe")

        @block.scalar
        def _(sc, scn=scn):
            ctx.wait(sc, "pe")
            i = sc.activation(rms[0:1, 512 * scn:512 * scn + 512], PS_B[0:1, 0:512],
                              AF.Sqrt, scale=1.0 / 256.0, bias=eps_sb[0:1, 0:1])
            ctx.inc(i, "act")

        @block.vector
        def _(v, scn=scn):
            ctx.wait(v, "act")
            sl = rms[0:1, 512 * scn:512 * scn + 512]
            ctx.inc(v.reciprocal(sl, sl), "vec")

    sq_act_tok = ctx.tok("act")
    for j in (() if "rms" in _sk2 else range(2)):
        @block.vector
        def _(v, j=j):
            v.wait_ge(ctx.sget("act"), sq_act_tok)
            ctx.wait(v, "vec")
            i = v.tensor_scalar(KVT[:, j * S:(j + 1) * S], KVT[:, j * S:(j + 1) * S],
                                gk_tile[:, j:j + 1], None, ALU.mult)
            ctx.inc(i, "vec")
    for j in range(2):
        for scn in range(4):
            @block.tensor
            def _(t, j=j, scn=scn):
                ctx.wait(t, "vec")
                i = t.matmul(PS_B[:, 512:1024], onesrowf[0:1, :],
                             rms[0:1, 512 * scn:512 * scn + 512],
                             start=True, stop=True)
                ctx.inc(i, "pe")

            @block.vector
            def _(v, j=j, scn=scn):
                ctx.wait(v, "pe")
                sl = KVT[:, j * S + 512 * scn: j * S + 512 * scn + 512]
                ctx.inc(v.tensor_mul(sl, sl, PS_B[:, 512:1024]), "vec")

    if ctx.consts.get("KTR") == "A":
        ph1.close()
        return
    ph1.close()
    QCp = ctx.sbuf("ma_QCp", [128, S], BF16)
    KCp = ctx.sbuf("ma_KCp", [128, S], BF16)
    QRp = ctx.sbuf("ma_QRp", [64, S], BF16)
    QRr = ctx.sbuf("ma_QRr", [64, S], BF16)
    KRp = ctx.sbuf("ma_KRp", [64, S], BF16)
    Vp = ctx.sbuf("ma_Vp", [128, S], BF16)
    KRX = ctx.sbuf("ma_KRX", [128, 2 * S], BF16)
    QRX = ctx.sbuf("ma_QRX", [128, 2 * S], BF16)
    ET = ctx.sbuf("ma_ET", [128, 2 * S], BF16)
    scrE = ET[:, 0:S]
    negm = ctx.sbuf("ma_negm", [128, NT], F32)
    sums = ctx.sbuf("ma_sums", [128, NT], F32)
    mls = ctx.sbuf("ma_mls", [128, NT], F32)
    mhb = ctx.sbuf("ma_mhb", [128, NT], BF16)
    mhl = ctx.sbuf("ma_mhl", [128, 2 * NT], F32)

    @block.vector
    def _(v):
        ctx.wait(v, "vec")
        ctx.inc(v.memset(KRX[32:64, :], 0.0), "vec")
        ctx.wait(v, "vec")
        ctx.inc(v.memset(KRX[32:34, :], -1.0), "vec")
        ctx.wait(v, "vec")
        ctx.inc(v.memset(QRX[32:64, :], 0.0), "vec")

    def pair_gemm(dram, din, actT, c0, dst, dst_c0, ncols, ps):
        nj = din // 128
        slot_i = pair_gemm.flip = (getattr(pair_gemm, "flip", 0) + 1) % 2
        slot = ctx.consts["WSLOTS"][slot_i]
        wide = 2 * ncols
        cb = (c0 // wide) * wide
        off = c0 - cb
        wsem, tok = wload(ctx, dram, din, cb, cb + wide, slot, slot_i,
                          pe_guard=ctx.tok("pe"))
        for scn in range(4):
            @block.tensor
            def _(t, scn=scn, slot=slot, tok=tok, wsem=wsem, nj=nj, ncols=ncols,
                  off=off, wide=wide):
                t.wait_ge(ctx.sget(wsem), tok)
                ctx.wait(t, "act")
                ctx.wait(t, "vec")
                for j in range(nj):
                    i = t.matmul(ps[0:ncols, 0:512],
                                 slot[:, j * wide + off: j * wide + off + ncols],
                                 actT[:, j * S + 512 * scn: j * S + 512 * scn + 512],
                                 start=(j == 0), stop=(j == nj - 1))
                ctx.inc(i, "pe")

            @block.scalar
            def _(sc, scn=scn, ncols=ncols):
                ctx.wait(sc, "pe")
                i = sc.activation(dst[dst_c0:dst_c0 + ncols, 512 * scn:512 * scn + 512],
                                  ps[0:ncols, 0:512], AF.Identity)
                ctx.inc(i, "act")

    def pair_gemm_n(dram, din, actT, c0, ps):
        nj = din // 128
        slot_i = pair_gemm.flip = (getattr(pair_gemm, "flip", 0) + 1) % 2
        slot = ctx.consts["WSLOTS"][slot_i]
        cb = (c0 // 256) * 256
        off = c0 - cb
        wsem, tok = wload(ctx, dram, din, cb, cb + 256, slot, slot_i,
                          pe_guard=ctx.tok("pe"))
        for t_i in range(NT):
            @block.tensor
            def _(t, t_i=t_i, slot=slot, tok=tok, wsem=wsem, nj=nj, off=off):
                t.wait_ge(ctx.sget(wsem), tok)
                ctx.wait(t, "act")
                ctx.wait(t, "vec")
                for j in range(nj):
                    i = t.matmul(ps[:, 0:128],
                                 actT[:, j * S + 128 * t_i: j * S + 128 * t_i + 128],
                                 slot[:, j * 256 + off: j * 256 + off + 128],
                                 start=(j == 0), stop=(j == nj - 1))
                ctx.inc(i, "pe")

            @block.vector
            def _(v, t_i=t_i):
                ctx.wait(v, "pe")
                ctx.inc(v.tensor_copy(Vp[:, t_i * 128: t_i * 128 + 128], ps[:, 0:128]),
                        "vec")

    def rope_apply(dst, rot, n_rows):
        for scn in range(4):
            @block.vector
            def _(v, scn=scn):
                ctx.wait(v, "act")
                ctx.wait(v, "vec")
                sl = slice(512 * scn, 512 * scn + 512)
                i = v.tensor_mul(scr512[0:n_rows, 0:512], dst[:, sl], cos_sb[0:n_rows, sl])
                ctx.inc(i, "vec")
                ctx.wait(v, "vec")
                i = v.tensor_mul(rot[:, sl], rot[:, sl], sin_sb[0:n_rows, sl])
                ctx.inc(i, "vec")
                ctx.wait(v, "vec")
                i = v.tensor_add(dst[:, sl], scr512[0:n_rows, 0:512], rot[:, sl])
                ctx.inc(i, "vec")

    import os as _os
    _skips = set((_os.environ.get("KSKIP") or "").split(","))
    _np_ = int(_os.environ.get("KNPAIR") or 8)
    for pair in range(_np_):
        pb = PS_A[:, 0:512] if pair % 2 == 0 else PS_A[:, 512:1024]
        if "qk" not in _skips:
            pair_gemm(d_qdec, 512, QC, 128 * pair, QCp, 0, 128, pb)
            if "kd" not in _skips:
                pair_gemm(d_kdec, 256, KVT, 128 * pair, KCp, 0, 128, pb)
        if "qr" not in _skips:
            pair_gemm(d_qrp, 512, QC, 64 * pair, QRp, 0, 64, PS_A[:, 512:1024])
            pair_gemm(d_qrr, 512, QC, 64 * pair, QRr, 0, 64, PS_A[:, 512:1024])
            if "rope" not in _skips:
                rope_apply(QRp, QRr, 64)
        if "v" not in _skips:
            pair_gemm_n(d_vdec, 256, KVT, 128 * pair, PS_A[:, 1024:1536])

        @block.vector
        def _(v, pair=pair):
            import os as _os
            if _os.environ.get("KNOKRCOPY"):
                return
            ctx.wait(v, "vec")
            ctx.wait(v, "pe")
            jsrc = pair // 2
            r0 = 64 * (pair % 2)
            i = v.tensor_copy(KRp[:, :], KRF[r0:r0 + 64, jsrc * S:(jsrc + 1) * S])
            ctx.inc(i, "vec")
            ctx.wait(v, "act")
            for hh in range(2):
                ctx.wait(v, "vec")
                i = v.tensor_copy(KRX[0:32, hh * S:(hh + 1) * S],
                                  KRp[32 * hh:32 * hh + 32, :])
                ctx.inc(i, "vec")
                ctx.wait(v, "vec")
                i = v.tensor_copy(QRX[0:32, hh * S:(hh + 1) * S],
                                  QRp[32 * hh:32 * hh + 32, :])
                ctx.inc(i, "vec")
                ctx.wait(v, "vec")
                i = v.tensor_copy(KRX[64:128, hh * S:(hh + 1) * S],
                                  KCp[64 * hh:64 * hh + 64, :])
                ctx.inc(i, "vec")
                ctx.wait(v, "vec")
                i = v.tensor_copy(QRX[64:128, hh * S:(hh + 1) * S],
                                  QCp[64 * hh:64 * hh + 64, :])
                ctx.inc(i, "vec")

        if _os.environ.get("KSTOP") == "proj":
            continue
        for hh in range(2):
            ro = 64 * hh
            rr = 32 * hh
            for qt in range(NT):
                import os as _os2
                nkc = qt // 4 + 1
                if _os2.environ.get("KFULLK"):
                    nkc = 4
                @block.tensor
                def _(t, qt=qt, ro=ro, rr=rr, nkc=nkc):
                    import os as _os
                    ctx.wait(t, "act")
                    ctx.wait(t, "vec")
                    for kc in range(nkc):
                        klen = min(512, 128 * (qt + 1) - 512 * kc)
                        if _os.environ.get("KFULLK"):
                            klen = 512
                        ps = PS_A[:, 512 * kc:512 * kc + klen]
                        if _os.environ.get("KNOROPE"):
                            i = t.matmul(ps, QCp[ro:ro + 64, 128 * qt:128 * qt + 128],
                                         KCp[ro:ro + 64, 512 * kc:512 * kc + klen],
                                         start=True, stop=True)
                        else:
                            t.matmul(ps, QCp[ro:ro + 64, 128 * qt:128 * qt + 128],
                                     KCp[ro:ro + 64, 512 * kc:512 * kc + klen],
                                     start=True, stop=False)
                            i = t.matmul(ps, QRp[rr:rr + 32, 128 * qt:128 * qt + 128],
                                         KRp[rr:rr + 32, 512 * kc:512 * kc + klen],
                                         start=False, stop=True)
                    ctx.inc(i, "pe")

                @block.vector
                def _(v, qt=qt):
                    import os as _os
                    ctx.wait(v, "pe")
                    if not _os.environ.get("KNOMASK"):
                        db = PS_A[:, 128 * qt:128 * qt + 128]
                        ctx.inc(v.tensor_add(db, db, maskD_sb[:]), "vec")
                        ctx.wait(v, "vec")
                    if _os.environ.get("KNORED"):
                        ctx.inc(v.nop(), "vec")
                    else:
                        ctx.inc(v.reduce_max(negm[:, qt:qt + 1], PS_A[:, 0:128 * (qt + 1)],
                                             axis=AX.X, negate=True), "vec")
                        ctx.wait(v, "vec")
                        ctx.inc(v.tensor_scalar_mul(negm[:, qt:qt + 1], negm[:, qt:qt + 1],
                                                    SCL), "vec")

                @block.scalar
                def _(sc, qt=qt):
                    import os as _os
                    ctx.wait(sc, "vec")
                    ctx.wait(sc, "act")
                    if _os.environ.get("KNOEXP"):
                        fd = 2048 if _os.environ.get("KFULLK") else 128 * (qt + 1)
                        i = sc.activation(scrE[:, 0:fd], PS_A[:, 0:fd], AF.Identity)
                    else:
                        i = sc.activation(scrE[:, 0:128 * (qt + 1)], PS_A[:, 0:128 * (qt + 1)],
                                          AF.Exp, bias=negm[:, qt:qt + 1], scale=SCL,
                                          accum_out=sums[:, qt:qt + 1])
                    ctx.inc(i, "act")


            @block.scalar
            def _(sc):
                ctx.wait(sc, "act")
                ctx.inc(sc.activation(mls[:], sums[:], AF.Ln), "act")

            @block.vector
            def _(v):
                ctx.wait(v, "act")
                ctx.inc(v.tensor_sub(mls[:], mls[:], negm[:]), "vec")
                ctx.wait(v, "vec")
                ctx.inc(v.tensor_scalar_mul(mls[:], mls[:], 1.0 / SCL), "vec")
                ctx.wait(v, "vec")
                ctx.inc(v.tensor_copy(mhb[:], mls[:]), "vec")
                ctx.wait(v, "vec")
                ctx.inc(v.tensor_copy(mhl[:].rearrange("p (t two) -> p two t", two=2)[:, 0, :],
                                      mhb[:]), "vec")
                ctx.wait(v, "vec")
                ctx.inc(v.tensor_sub(mhl[:].rearrange("p (t two) -> p two t", two=2)[:, 1, :],
                                     mls[:],
                                     mhl[:].rearrange("p (t two) -> p two t", two=2)[:, 0, :]),
                        "vec")

            for qt in range(NT):
                @block.tensor
                def _(t, qt=qt):
                    ctx.wait(t, "vec")
                    i = t.matmul(PS_B[0:2, 0:128], mhl[:, 2 * qt:2 * qt + 2],
                                 ctx.consts["IDENTF"][:], start=True, stop=True)
                    ctx.inc(i, "pe")

                @block.vector
                def _(v, qt=qt, hh=hh):
                    ctx.wait(v, "pe")
                    dst = QRX[32:34, hh * S + 128 * qt: hh * S + 128 * qt + 128]
                    ctx.inc(v.tensor_copy(dst, PS_B[0:2, 0:128]), "vec")

        if _os.environ.get("KSTOP") == "pass1":
            continue
        _p2s = set((_os.environ.get("KP2SKIP") or "").split(","))
        pair_vec_tok = ctx.tok("vec")
        for kt in range(NT):
            qc0 = kt // 4
            for hh in range(2):
                ro = 64 * hh
                rr = 32 * hh

                @block.tensor
                def _(t, kt=kt, hh=hh, ro=ro, rr=rr, qc0=qc0, pair_vec_tok=pair_vec_tok):
                    t.wait_ge(ctx.sget("vec"), pair_vec_tok)
                    ctx.wait(t, "act")
                    ctx.wait(t, "vec")
                    for qc in range(qc0, 4):
                        ps = PS_A[:, 512 * qc:512 * qc + 512]
                        i = t.matmul(ps,
                                     KRX[0:128, hh * S + 128 * kt: hh * S + 128 * kt + 128],
                                     QRX[0:128, hh * S + 512 * qc: hh * S + 512 * qc + 512],
                                     start=True, stop=True)
                    ctx.inc(i, "pe")

                if "mask" not in _p2s:
                    @block.vector
                    def _(v, kt=kt, qc0=qc0):
                        ctx.wait(v, "pe")
                        ps = PS_A[:, 512 * qc0:512 * qc0 + 512]
                        i = v.tensor_add(ps, ps, maskT_sb[:, 512 * (kt % 4):512 * (kt % 4) + 512])
                        ctx.inc(i, "vec")

                if "exp" not in _p2s:
                    @block.scalar
                    def _(sc, kt=kt, hh=hh, qc0=qc0):
                        ctx.wait(sc, "vec")
                        ctx.wait(sc, "act")
                        i = sc.activation(ET[:, hh * S + 512 * qc0: (hh + 1) * S],
                                          PS_A[:, 512 * qc0: 2048], AF.Exp, scale=SCL)
                        ctx.inc(i, "act")

                if "av" not in _p2s and not (hh == 1 and "avh1" in _p2s):
                    @block.tensor
                    def _(t, kt=kt, hh=hh, qc0=qc0):
                        ctx.wait(t, "act")
                        for qc in range(qc0, 4):
                            i = t.matmul(PS_B[64 * hh:64 * hh + 64, 512 * qc:512 * qc + 512],
                                         Vp[:, kt * 128 + 64 * hh: kt * 128 + 64 * hh + 64],
                                         ET[:, hh * S + 512 * qc: hh * S + 512 * qc + 512],
                                         start=(kt == 0), stop=(kt == NT - 1),
                                         tile_position=(0, 64 * hh),
                                         skip_group_check=True)
                        ctx.inc(i, "pe")

        if _os.environ.get("KSTOP") == "pass2":
            continue
        @block.scalar
        def _(sc, pair=pair):
            ctx.wait(sc, "pe")
            i = sc.activation(AOT[:, pair * S:(pair + 1) * S], PS_B[:, 0:S], AF.Identity)
            ctx.inc(i, "act")

    if _os.environ.get("KNOOUT"):
        return
    att_act_tok = ctx.tok("act")

    def o_evac(v, psap, t_i, c0, wd):
        dst = x_sb[:, t_i * D + c0: t_i * D + c0 + wd]
        return v.tensor_add(dst, dst, psap)
    gemm_N(ctx, d_o, D, D, AOT, S, PS_A[:, 0:512], o_evac,
           pre_wait=[("act", att_act_tok)])





XS = S + 1


def emit_transpose_full(ctx, x_sb, out_T, ps_tr, rowbuf):
    block = ctx.block

    @block.vector
    def _(v):
        ctx.wait(v, "vec")
        ctx.wait(v, "pe")
        z = out_T[:].rearrange("p (j s) -> p j s", s=XS)[:, :, 0:1]
        ctx.inc(v.memset(z, 0.0), "vec")
    for t in range(NT):
        @block.scalar
        def _(sc, t=t):
            ctx.wait(sc, "pe")
            ctx.wait(sc, "vec")
            ctx.wait(sc, "act")
            i = sc.activation(rowbuf[:], x_sb[:, t * D:(t + 1) * D], AF.Identity)
            ctx.inc(i, "act")

        @block.tensor
        def _(tens, t=t):
            ctx.wait(tens, "act")
            ctx.wait(tens, "vec")
            if t == 0:
                ctx.wait(tens, "wdma")
            for half in range(2):
                for j4 in range(4):
                    j = half * 4 + j4
                    i = tens.matmul(ps_tr[:, 512 * half + 128 * j4: 512 * half + 128 * j4 + 128],
                                    rowbuf[:, 128 * j:128 * (j + 1)],
                                    ctx.consts["IDENT"][:], start=True, stop=True)
            ctx.inc(i, "pe")

        @block.vector
        def _(v, t=t):
            ctx.wait(v, "pe")
            srcv = ps_tr[:].rearrange("p (j c) -> p j c", c=128)
            dst = out_T[:].rearrange("p (j s) -> p j s", s=XS)[:, :, 1 + 128 * t:1 + 128 * t + 128]
            ctx.inc(v.tensor_copy(dst, srcv), "vec")


def emit_ln_inplace(ctx, x_sb, PS_unused):
    block = ctx.block
    raw = ctx.sbuf("lni_raw", [128, 12], F32)
    mv = ctx.sbuf("lni_mv", [128, 2 * NT], F32)
    rstd = ctx.sbuf("lni_rstd", [128, NT], F32)
    nmr = ctx.sbuf("lni_nmr", [128, NT], F32)

    @block.vector
    def _(v):
        ctx.wait(v, "xin")
        ctx.wait(v, "vec")
        for t in range(NT):
            xt = x_sb[:, t * D:(t + 1) * D]
            v.bn_stats(raw[:, 0:6], xt[:, 0:512])
            ctx.inc(v.bn_stats(raw[:, 6:12], xt[:, 512:1024]), "vec")
            ctx.wait(v, "vec")
            ctx.inc(v.bn_aggr(mv[:, 2 * t:2 * t + 2], raw[:]), "vec")
            ctx.wait(v, "vec")
        var_v = mv[:].rearrange("p (t two) -> p t two", two=2)[:, :, 1:2]
        ctx.inc(v.tensor_scalar_add(rstd[:].rearrange("p t -> p t ()"), var_v, 1e-5), "vec")

    @block.scalar
    def _(sc):
        ctx.wait(sc, "vec")
        ctx.inc(sc.activation(rstd[:], rstd[:], AF.Sqrt), "act")

    @block.vector
    def _(v):
        ctx.wait(v, "act")
        ctx.inc(v.reciprocal(rstd[:], rstd[:]), "vec")
        ctx.wait(v, "vec")
        mean_v = mv[:].rearrange("p (t two) -> p t two", two=2)[:, :, 0:1]
        i = v.tensor_tensor(nmr[:].rearrange("p t -> p t ()"), mean_v,
                            rstd[:].rearrange("p t -> p t ()"), ALU.mult)
        ctx.inc(i, "vec")
        ctx.wait(v, "vec")
        ctx.inc(v.tensor_scalar_mul(nmr[:], nmr[:], -1.0), "vec")

    for t in range(NT):
        @block.scalar
        def _(sc, t=t):
            ctx.wait(sc, "vec")
            xt = x_sb[:, t * D:(t + 1) * D]
            i = sc.activation(xt, xt, AF.Identity,
                              bias=nmr[:, t:t + 1], scale=rstd[:, t:t + 1])
            ctx.inc(i, "act")


def emit_ffn(ctx, x_sb, w, pfx, PS_A, PS_B, x_in_dram):
    block = ctx.block
    HEADS, DH, W = 8, 64, 512
    HID = 2730
    HIDP = 2816
    NC = 11

    wqkv = np.asarray(w[pfx + "_wqkv"], np.float32)
    q_scale = np.asarray(w[pfx + "_q_scale"], np.float32)
    k_scale = np.asarray(w[pfx + "_k_scale"], np.float32)
    wo = np.asarray(w[pfx + "_wo"], np.float32)
    ln_g = np.asarray(w[pfx + "_ln_g"], np.float32)
    ln_b = np.asarray(w[pfx + "_ln_b"], np.float32)
    w1 = np.asarray(w[pfx + "_w1"], np.float32)
    b1 = np.asarray(w[pfx + "_b1"], np.float32)
    w2 = np.asarray(w[pfx + "_w2"], np.float32)
    b2 = np.asarray(w[pfx + "_b2"], np.float32)

    def perm64(Wm):
        Wr = np.zeros_like(Wm)
        nh = Wm.shape[1] // 64
        for h in range(nh):
            blk = Wm[:, 64 * h:64 * h + 64]
            Wr[:, 64 * h:64 * h + 32] = blk[:, 32:64]
            Wr[:, 64 * h + 32:64 * h + 64] = blk[:, 0:32]
        return np.ascontiguousarray(Wr)

    dL = []
    for l in range(2):
        Wq = np.ascontiguousarray(wqkv[l][:, 0:512])
        Wk = np.ascontiguousarray(wqkv[l][:, 512:1024])
        Wv = np.ascontiguousarray(wqkv[l][:, 1024:1536])
        W1p = np.zeros((D, 2 * HIDP), np.float32)
        b1p = np.zeros((1, 2 * HIDP), np.float32)
        a_part = np.zeros((D, HIDP), np.float32)
        g_part = np.zeros((D, HIDP), np.float32)
        a_part[:, 0:HID] = w1[l][:, 0:HID]
        g_part[:, 0:HID] = w1[l][:, HID:2 * HID]
        ab = np.zeros((1, HIDP), np.float32)
        gb = np.zeros((1, HIDP), np.float32)
        ab[0, 0:HID] = b1[l][0:HID]
        gb[0, 0:HID] = b1[l][HID:2 * HID]
        for c in range(NC):
            W1p[:, 512 * c:512 * c + 256] = a_part[:, 256 * c:256 * c + 256]
            W1p[:, 512 * c + 256:512 * c + 512] = g_part[:, 256 * c:256 * c + 256]
            b1p[0, 512 * c:512 * c + 256] = ab[0, 256 * c:256 * c + 256]
            b1p[0, 512 * c + 256:512 * c + 512] = gb[0, 256 * c:256 * c + 256]
        W2p = np.zeros((HIDP, D), np.float32)
        W2p[0:HID, :] = w2[l]
        dL.append(dict(
            q=ctx.din(f"w_{pfx}_q{l}", Wq), qr=ctx.din(f"w_{pfx}_qr{l}", perm64(Wq)),
            k=ctx.din(f"w_{pfx}_k{l}", Wk), kr=ctx.din(f"w_{pfx}_kr{l}", perm64(Wk)),
            v=ctx.din(f"w_{pfx}_v{l}", Wv),
            wo=ctx.din(f"w_{pfx}_wo{l}", np.ascontiguousarray(wo[l])),
            w1=ctx.din(f"w_{pfx}_w1{l}", W1p),
            b1=ctx.din(f"w_{pfx}_b1{l}", b1p),
            w2=ctx.din(f"w_{pfx}_w2{l}", W2p),
            b2=ctx.din(f"w_{pfx}_b2{l}", b2[l][None, :]),
            qs=ctx.din(f"w_{pfx}_qs{l}", np.tile(q_scale[l], 2)[:, None]),
            qsp=ctx.din(f"w_{pfx}_qsp{l}",
                        np.tile(np.concatenate([q_scale[l][32:], q_scale[l][:32]]), 2)[:, None]),
            ks=ctx.din(f"w_{pfx}_ks{l}", np.tile(k_scale[l], 2)[:, None]),
            ksp=ctx.din(f"w_{pfx}_ksp{l}",
                        np.tile(np.concatenate([k_scale[l][32:], k_scale[l][:32]]), 2)[:, None]),
            lg=ctx.din(f"w_{pfx}_lg{l}", ln_g[l].reshape(8, 128).T.copy()),
            lb=ctx.din(f"w_{pfx}_lb{l}", ln_b[l].reshape(8, 128).T.copy()),
        ))

    half_idx = np.arange(0, DH, 2, dtype=np.float64)
    inv_freq = 1.0 / 10000.0 ** (half_idx / DH)
    tt = np.arange(S, dtype=np.float64)
    freqs = tt[:, None] * inv_freq[None, :]
    freqs = np.concatenate([freqs, freqs], -1)
    base = (half_idx + 0.4 * DH) / (1.4 * DH)
    power = (tt - S // 2) / float(W // 2)
    xsc = base[None, :] ** power[:, None]
    xsc = np.concatenate([xsc, xsc], -1)
    cosv = np.cos(freqs)
    sinv = np.sin(freqs)
    sgn = np.where(np.arange(DH) < 32, -1.0, 1.0)[None, :]
    cq8 = np.tile((cosv * xsc * 8.0).T, (2, 1))
    sq8 = np.tile((sinv * xsc * 8.0 * sgn).T, (2, 1))
    ck = np.tile((cosv / xsc).T, (2, 1))
    sk = np.tile((sinv / xsc * sgn).T, (2, 1))
    d_cq = ctx.din(f"c_{pfx}_cq", np.ascontiguousarray(cq8, np.float32))
    d_sq = ctx.din(f"c_{pfx}_sq", np.ascontiguousarray(sq8, np.float32))
    d_ck = ctx.din(f"c_{pfx}_ck", np.ascontiguousarray(ck, np.float32))
    d_sk = ctx.din(f"c_{pfx}_sk", np.ascontiguousarray(sk, np.float32))

    sel = np.zeros((128, 4 * 8), np.float32)
    selT = np.zeros((8, 4 * 128), np.float32)
    for jo in range(4):
        for p in range(128):
            h = 2 * jo + (1 if p >= 64 else 0)
            sel[p, jo * 8 + h] = 1.0
            selT[h, jo * 128 + p] = 1.0
    maskD = np.zeros((128, 128), np.float32)
    maskU = np.zeros((128, 128), np.float32)
    for p in range(128):
        maskD[p, p + 1:] = -1e30
        maskU[p, 0:p] = -1e30
    onescolb, _ = load_row_const(ctx, pfx + "_onescol",
                                 np.ones((128, 1), np.float32), BF16)
    onesf1, _ = load_row_const(ctx, pfx + "_onesf1", np.ones((1, 1), np.float32))

    xT = ctx.sbuf("ff_xT", [128, DT * XS], BF16)
    rowb = ctx.sbuf("ff_rowb", [128, D], BF16)

    emit_ln_inplace(ctx, x_sb, None)

    for l in range(2):
        dl = dL[l]
        qs_sb, _ = load_row_const(ctx, f"{pfx}_qs{l}v", np.zeros((1, 1), np.float32)) \
            if False else (None, None)
        qs_t = ctx.sbuf(f"ff_qs{l}", [128, 4], F32)

        @block.gpsimd
        def _(g, l=l, qs_t=qs_t):
            dd = dL[l]
            for nm in ("pe", "vec", "act"):
                if ctx.cnt.get(nm):
                    ctx.wait(g, nm)
            ctx.inc(g.dma_start(out=qs_t[:, 0:1], in_=dd["qs"]), "wdma", 16)
            ctx.inc(g.dma_start(out=qs_t[:, 1:2], in_=dd["qsp"]), "wdma", 16)
            ctx.inc(g.dma_start(out=qs_t[:, 2:3], in_=dd["ks"]), "wdma", 16)
            ctx.inc(g.dma_start(out=qs_t[:, 3:4], in_=dd["ksp"]), "wdma", 16)

        atn = ExitStack()
        TWc = ctx.sbuf("ff_TWc", [128, 512], F32, atn)
        TWs = ctx.sbuf("ff_TWs", [128, 512], F32, atn)
        PRJ = ctx.sbuf("ff_PRJ", [128, 4 * 512], F32, atn)
        PRR = ctx.sbuf("ff_PRR", [128, 4 * 512], F32, atn)
        sel_sb, _ = load_row_const(ctx, f"{pfx}_sel{l}", sel, BF16, scope=atn,
                                   guards=("pe", "vec", "act"))
        selT_sb, _ = load_row_const(ctx, f"{pfx}_selT{l}", selT, F32, scope=atn)
        maskD_sb, _ = load_row_const(ctx, f"{pfx}_maskD{l}", maskD, BF16, scope=atn)
        maskU_sb, _ = load_row_const(ctx, f"{pfx}_maskU{l}", maskU, BF16, scope=atn)
        neg2_sb, _ = load_row_const(ctx, f"{pfx}_neg2{l}",
                                    np.full((2, 128), -1.0, np.float32), BF16, scope=atn)

        emit_transpose_full(ctx, x_sb, xT, PS_A[:, 0:1024], rowb)

        QW = ctx.sbuf("ff_QW", [128, 4 * W], BF16, atn)
        K2 = ctx.sbuf("ff_K2", [128, 4 * 2 * W], BF16, atn)
        V2 = ctx.sbuf("ff_V2", [128, 8 * W], BF16, atn)
        OW = ctx.sbuf("ff_OW", [128, 4 * W], BF16, atn)
        WO = ctx.sbuf("ff_WO", [128, 4 * 1024], BF16, atn)
        ET = ctx.sbuf("ff_ET", [128, W], BF16, atn)
        scrE = ctx.sbuf("ff_scrE", [128, 640], BF16, atn)
        nrm = ctx.sbuf("ff_nrm", [8, W], F32, atn)
        MH = ctx.sbuf("ff_MH", [2, 4 * 128], BF16, atn)
        negm = ctx.sbuf("ff_negm", [128, 4], F32, atn)
        sums = ctx.sbuf("ff_sums", [128, 4], F32, atn)
        mls = ctx.sbuf("ff_mls", [128, 4], F32, atn)
        mhb = ctx.sbuf("ff_mhb", [128, 4], BF16, atn)
        mhl = ctx.sbuf("ff_mhl", [128, 8], F32, atn)

        @block.gpsimd
        def _(g, l=l):
            for nm in ("pe", "vec", "act"):
                ctx.wait(g, nm)
            srcw = dL[l]["wo"].rearrange("(j p) c -> p j c", p=128)
            dstw = WO[:].rearrange("p (j c) -> p j c", c=1024)
            ctx.inc(g.dma_start(out=dstw, in_=srcw), "wod", 16)
        wo_tok = ctx.tok("wod")

        def project_win(dram, dramr, wi, dst, d_tabc, d_tabs, qs_col, qsp_col):
            s0 = W * wi

            @block.gpsimd
            def _(g):
                for nm in ("pe", "vec", "act"):
                    ctx.wait(g, nm)
                ctx.inc(g.dma_start(out=TWc[:], in_=d_tabc[:, s0:s0 + 512]), "wdma", 16)
                ctx.inc(g.dma_start(out=TWs[:], in_=d_tabs[:, s0:s0 + 512]), "wdma", 16)

            for which, dr in (("raw", dram), ("rot", dramr)):
                slot_i = pair_flip()
                slot = ctx.consts["WSLOTS"][slot_i]
                wsem, tok = wload(ctx, dr, D, 0, 512, slot, slot_i,
                                  pe_guard=ctx.tok("pe"))
                for jo in range(4):
                    @block.tensor
                    def _(t, jo=jo, slot=slot, tok=tok, wsem=wsem, s0=s0):
                        t.wait_ge(ctx.sget(wsem), tok)
                        ctx.wait(t, "act")
                        ctx.wait(t, "vec")
                        for j in range(DT):
                            off = s0 if j >= 4 else s0 + 1
                            i = t.matmul(PS_A[:, 0:512],
                                         slot[:, j * 512 + 128 * jo: j * 512 + 128 * jo + 128],
                                         xT[:, j * XS + off: j * XS + off + 512],
                                         start=(j == 0), stop=(j == DT - 1))
                        ctx.inc(i, "pe")

                    @block.scalar
                    def _(sc, jo=jo, which=which):
                        ctx.wait(sc, "pe")
                        tgt = PRJ if which == "raw" else PRR
                        i = sc.activation(tgt[:, jo * 512:(jo + 1) * 512], PS_A[:, 0:512],
                                          AF.Identity)
                        ctx.inc(i, "act")

            for jo in range(4):
                @block.scalar
                def _(sc, jo=jo):
                    ctx.wait(sc, "act")
                    ctx.wait(sc, "pe")
                    ctx.wait(sc, "vec")
                    i = sc.activation(scrE[:, 0:512], PRJ[:, jo * 512:(jo + 1) * 512],
                                      AF.Square)
                    ctx.inc(i, "act")

                @block.tensor
                def _(t, jo=jo):
                    ctx.wait(t, "act")
                    if jo == 0:
                        ctx.wait(t, "vec")
                    i = t.matmul(PS_B[0:8, 0:512], sel_sb[:, jo * 8:(jo + 1) * 8],
                                 scrE[:, 0:512], start=(jo == 0), stop=(jo == 3))
                    ctx.inc(i, "pe")

            @block.scalar
            def _(sc):
                ctx.wait(sc, "pe")
                ctx.inc(sc.activation(nrm[:], PS_B[0:8, 0:512], AF.Sqrt), "act")

            @block.vector
            def _(v):
                ctx.wait(v, "act")
                ctx.inc(v.tensor_scalar_max(nrm[:], nrm[:], 1e-12), "vec")
                ctx.wait(v, "vec")
                ctx.inc(v.reciprocal(nrm[:], nrm[:]), "vec")

            for jo in range(4):
                @block.tensor
                def _(t, jo=jo):
                    ctx.wait(t, "vec")
                    i = t.matmul(PS_B[:, 512:1024], selT_sb[0:8, jo * 128:(jo + 1) * 128],
                                 nrm[:], start=True, stop=True)
                    ctx.inc(i, "pe")

                @block.vector
                def _(v, jo=jo, qs_col=qs_col, qsp_col=qsp_col, dst=dst):
                    ctx.wait(v, "pe")
                    ctx.wait(v, "wdma")
                    dsl = slice(jo * 512, (jo + 1) * 512)
                    i = v.tensor_scalar(PRJ[:, dsl], PRJ[:, dsl], qs_col, None, ALU.mult)
                    ctx.inc(i, "vec")
                    ctx.wait(v, "vec")
                    i = v.tensor_scalar(PRR[:, dsl], PRR[:, dsl], qsp_col, None, ALU.mult)
                    ctx.inc(i, "vec")
                    ctx.wait(v, "vec")
                    i = v.tensor_mul(PRJ[:, dsl], PRJ[:, dsl], TWc[:])
                    ctx.inc(i, "vec")
                    ctx.wait(v, "vec")
                    i = v.tensor_mul(PRR[:, dsl], PRR[:, dsl], TWs[:])
                    ctx.inc(i, "vec")
                    ctx.wait(v, "vec")
                    i = v.tensor_add(PRJ[:, dsl], PRJ[:, dsl], PRR[:, dsl])
                    ctx.inc(i, "vec")
                    ctx.wait(v, "vec")
                    i = v.tensor_mul(PRJ[:, dsl], PRJ[:, dsl], PS_B[:, 512:1024])
                    ctx.inc(i, "vec")
                    ctx.wait(v, "vec")
                    i = v.tensor_copy(dst[:, jo * W:(jo + 1) * W], PRJ[:, dsl])
                    ctx.inc(i, "vec")

        def pair_flip():
            pair_flip.f = (getattr(pair_flip, "f", 0) + 1) % 2
            return pair_flip.f

        for wi in range(4):
            project_win(dl["q"], dl["qr"], wi, QW, d_cq, d_sq,
                        qs_t[:, 0:1], qs_t[:, 1:2])
            project_win(dl["k"], dl["kr"], wi, OW, d_ck, d_sk,
                        qs_t[:, 2:3], qs_t[:, 3:4])
            if ctx.consts.get("DBGQ") is not None and l == 0 and wi == 1:
                @block.sync
                def _(sync):
                    for nm in ("pe", "vec", "act"):
                        ctx.wait(sync, nm)
                    ctx.inc(sync.dma_start(out=ctx.consts["DBGQ"][:],
                                           in_=QW[:]), "sdma", 16)
                    ctx.inc(sync.dma_start(out=ctx.consts["DBGK"][:],
                                           in_=OW[:]), "sdma", 16)

                @block.vector
                def _(v):
                    ctx.wait(v, "sdma")
                    ctx.inc(v.nop(), "vec")
            for jo in range(4):
                @block.vector
                def _(v, jo=jo, wi=wi):
                    ctx.wait(v, "vec")
                    ctx.wait(v, "pe")
                    dstk = K2[:, jo * 2 * W + (wi % 2) * W: jo * 2 * W + (wi % 2) * W + W]
                    ctx.inc(v.tensor_copy(dstk, OW[:, jo * W:(jo + 1) * W]), "vec")

            slot_i = pair_flip()
            slot = ctx.consts["WSLOTS"][slot_i]
            wsem, tok = wload(ctx, dl["v"], D, 0, 512, slot, slot_i,
                              pe_guard=ctx.tok("pe"))
            for ti in range(4):
                @block.tensor
                def _(t, ti=ti, wi=wi, slot=slot, tok=tok, wsem=wsem):
                    t.wait_ge(ctx.sget(wsem), tok)
                    ctx.wait(t, "act")
                    ctx.wait(t, "vec")
                    s0 = W * wi + 128 * ti
                    for j in range(DT):
                        off = s0 if j >= 4 else s0 + 1
                        i = t.matmul(PS_A[:, 1024:1536],
                                     xT[:, j * XS + off: j * XS + off + 128],
                                     slot[:, j * 512:(j + 1) * 512],
                                     start=(j == 0), stop=(j == DT - 1))
                    ctx.inc(i, "pe")

                @block.vector
                def _(v, ti=ti, wi=wi):
                    ctx.wait(v, "pe")
                    dstv = V2[:, ((wi % 2) * 4 + ti) * W: ((wi % 2) * 4 + ti) * W + W]
                    ctx.inc(v.tensor_copy(dstv, PS_A[:, 1024:1536]), "vec")

            for h in range(HEADS):
                jo = h // 2
                ro = 64 * (h % 2)
                kjs_all = list(range(8))
                if wi == 0:
                    kjs_all = [kj for kj in kjs_all if kj >= 4]

                def kcol(kj):
                    win = 0 if kj < 4 else 1
                    slot_w = (wi - 1 + win) % 2
                    return jo * 2 * W + slot_w * W + 128 * (kj % 4)

                def vcol(kj, hloc):
                    win = 0 if kj < 4 else 1
                    slot_w = (wi - 1 + win) % 2
                    return (slot_w * 4 + (kj % 4)) * W + 64 * hloc

                for qt in range(4):
                    kjs = [kj for kj in range(qt, qt + 5) if kj in kjs_all]
                    @block.tensor
                    def _(t, qt=qt, kjs=kjs, jo=jo, ro=ro, wi=wi):
                        ctx.wait(t, "act")
                        ctx.wait(t, "vec")
                        for bi, kj in enumerate(kjs):
                            i = t.matmul(PS_A[:, 128 * bi:128 * bi + 128],
                                         QW[ro:ro + 64, jo * W + 128 * qt: jo * W + 128 * qt + 128],
                                         K2[ro:ro + 64, kcol(kj):kcol(kj) + 128],
                                         start=True, stop=True)
                        ctx.inc(i, "pe")

                    @block.vector
                    def _(v, qt=qt, kjs=kjs):
                        ctx.wait(v, "pe")
                        nb = len(kjs)
                        for bi, kj in enumerate(kjs):
                            mk = maskU_sb if kj == qt else (maskD_sb if kj == qt + 4 else None)
                            if mk is not None:
                                db = PS_A[:, 128 * bi:128 * bi + 128]
                                ctx.inc(v.tensor_add(db, db, mk[:]), "vec")
                                ctx.wait(v, "vec")
                        ctx.inc(v.reduce_max(negm[:, qt:qt + 1], PS_A[:, 0:128 * nb],
                                             axis=AX.X, negate=True), "vec")

                    @block.scalar
                    def _(sc, qt=qt, kjs=kjs):
                        ctx.wait(sc, "vec")
                        ctx.wait(sc, "act")
                        i = sc.activation(scrE[:, 0:128 * len(kjs)],
                                          PS_A[:, 0:128 * len(kjs)], AF.Exp,
                                          bias=negm[:, qt:qt + 1],
                                          accum_out=sums[:, qt:qt + 1])
                        ctx.inc(i, "act")

                @block.scalar
                def _(sc):
                    ctx.wait(sc, "act")
                    ctx.inc(sc.activation(mls[:, 0:4], sums[:, 0:4], AF.Ln), "act")

                @block.vector
                def _(v):
                    ctx.wait(v, "act")
                    ctx.inc(v.tensor_sub(mls[:, 0:4], mls[:, 0:4], negm[:, 0:4]), "vec")
                    ctx.wait(v, "vec")
                    ctx.inc(v.tensor_copy(mhb[:, 0:4], mls[:, 0:4]), "vec")
                    ctx.wait(v, "vec")
                    ctx.inc(v.tensor_copy(
                        mhl[:].rearrange("p (t two) -> p two t", two=2)[:, 0, :],
                        mhb[:, 0:4]), "vec")
                    ctx.wait(v, "vec")
                    ctx.inc(v.tensor_sub(
                        mhl[:].rearrange("p (t two) -> p two t", two=2)[:, 1, :],
                        mls[:, 0:4],
                        mhl[:].rearrange("p (t two) -> p two t", two=2)[:, 0, :]), "vec")

                for qt in range(4):
                    @block.tensor
                    def _(t, qt=qt):
                        ctx.wait(t, "vec")
                        i = t.matmul(PS_B[0:2, 0:128], mhl[:, 2 * qt:2 * qt + 2],
                                     ctx.consts["IDENTF"][:], start=True, stop=True)
                        ctx.inc(i, "pe")

                    @block.vector
                    def _(v, qt=qt):
                        ctx.wait(v, "pe")
                        ctx.inc(v.tensor_copy(MH[0:2, 128 * qt:128 * qt + 128],
                                              PS_B[0:2, 0:128]), "vec")

                if (ctx.consts.get("DBGS") is not None and l == 0 and wi == 1
                        and h == 0):
                    @block.sync
                    def _(sync):
                        for nm in ("pe", "vec", "act"):
                            ctx.wait(sync, nm)
                        ctx.inc(sync.dma_start(out=ctx.consts["DBGV"][:],
                                               in_=V2[:]), "sdma", 16)

                    @block.vector
                    def _(v):
                        ctx.wait(v, "sdma")
                        ctx.inc(v.nop(), "vec")
                if (ctx.consts.get("DBGS") is not None and l == 0 and wi == 1
                        and h == 0):
                    @block.sync
                    def _(sync):
                        for nm in ("pe", "vec", "act"):
                            ctx.wait(sync, nm)
                        ctx.inc(sync.dma_start(out=ctx.consts["DBGS"][0:128, 0:4],
                                               in_=negm[:, 0:4]), "sdma", 16)
                        ctx.inc(sync.dma_start(out=ctx.consts["DBGS"][0:128, 4:8],
                                               in_=sums[:, 0:4]), "sdma", 16)
                        ctx.inc(sync.dma_start(out=ctx.consts["DBGS"][0:128, 8:16],
                                               in_=mhl[:, 0:8]), "sdma", 16)

                    @block.vector
                    def _(v):
                        ctx.wait(v, "sdma")
                        ctx.inc(v.nop(), "vec")

                h_vec_tok = ctx.tok("vec")
                for kj in kjs_all:
                    qt0 = max(0, kj - 4)
                    qt1 = min(3, kj)
                    qn = 128 * (qt1 - qt0 + 1)

                    @block.tensor
                    def _(t, kj=kj, qt0=qt0, qn=qn, jo=jo, ro=ro, h_vec_tok=h_vec_tok):
                        t.wait_ge(ctx.sget("vec"), h_vec_tok)
                        ctx.wait(t, "act")
                        ctx.wait(t, "vec")
                        ps = PS_A[:, 0:qn]
                        t.matmul(ps, K2[ro:ro + 64, kcol(kj):kcol(kj) + 128],
                                 QW[ro:ro + 64, jo * W + 128 * qt0: jo * W + 128 * qt0 + qn],
                                 start=True, stop=False)
                        i = t.matmul(ps, neg2_sb[0:2, :],
                                     MH[0:2, 128 * qt0:128 * qt0 + qn],
                                     start=False, stop=True)
                        ctx.inc(i, "pe")

                    @block.vector
                    def _(v, kj=kj, qt0=qt0, qt1=qt1):
                        ctx.wait(v, "pe")
                        if qt0 <= kj <= qt1:
                            bi = kj - qt0
                            db = PS_A[:, 128 * bi:128 * bi + 128]
                            ctx.inc(v.tensor_add(db, db, maskD_sb[:]), "vec")
                        if qt0 <= kj - 4 <= qt1:
                            bi = kj - 4 - qt0
                            db = PS_A[:, 128 * bi:128 * bi + 128]
                            ctx.inc(v.tensor_add(db, db, maskU_sb[:]), "vec")

                    @block.scalar
                    def _(sc, kj=kj, qn=qn):
                        ctx.wait(sc, "vec")
                        ctx.wait(sc, "act")
                        i = sc.activation(ET[:, 0:qn], PS_A[:, 0:qn], AF.Exp)
                        ctx.inc(i, "act")

                    if (ctx.consts.get("DBGE") is not None and l == 0 and wi == 1
                            and h == 0 and kj == 4):
                        @block.sync
                        def _(sync):
                            for nm in ("pe", "vec", "act"):
                                ctx.wait(sync, nm)
                            ctx.inc(sync.dma_start(out=ctx.consts["DBGE"][:],
                                                   in_=ET[:, 0:512]), "sdma", 16)

                        @block.vector
                        def _(v):
                            ctx.wait(v, "sdma")
                            ctx.inc(v.nop(), "vec")

                    @block.tensor
                    def _(t, kj=kj, qt0=qt0, qt1=qt1, h=h, wi=wi):
                        ctx.wait(t, "act")
                        first_kj = kjs_all[0] if True else 0
                        for qt in range(qt0, qt1 + 1):
                            start = (kj == max(qt, first_kj))
                            stop = (kj == qt + 4)
                            i = t.matmul(PS_B[0:64, 512 * qt:512 * qt + 128],
                                         V2[:, vcol(kj, h):vcol(kj, h) + 64],
                                         ET[:, 128 * (qt - qt0):128 * (qt - qt0) + 128],
                                         start=start, stop=stop,
                                         skip_group_check=True)
                        ctx.inc(i, "pe")

                @block.scalar
                def _(sc, h=h, jo=jo, ro=ro):
                    ctx.wait(sc, "pe")
                    for qt in range(4):
                        i = sc.activation(OW[ro:ro + 64, jo * W + 128 * qt: jo * W + 128 * qt + 128],
                                          PS_B[0:64, 512 * qt:512 * qt + 128], AF.Identity)
                    ctx.inc(i, "act")

            if ctx.consts.get("DBGO") is not None and l == 0 and wi in (0, 1):
                @block.sync
                def _(sync, wi=wi):
                    for nm in ("pe", "vec", "act"):
                        ctx.wait(sync, nm)
                    ctx.inc(sync.dma_start(out=ctx.consts["DBGO"][:, 2048 * wi:2048 * wi + 2048],
                                           in_=OW[:]), "sdma", 16)

                @block.vector
                def _(v):
                    ctx.wait(v, "sdma")
                    ctx.inc(v.nop(), "vec")

            @block.tensor
            def _(t, wi=wi):
                t.wait_ge(ctx.sget("wod"), wo_tok)
            for ti in range(4):
                for dc in range(2):
                    @block.tensor
                    def _(t, ti=ti, dc=dc, wi=wi):
                        ctx.wait(t, "act")
                        ctx.wait(t, "vec")
                        for jo in range(4):
                            i = t.matmul(PS_A[:, 1536:2048],
                                         OW[:, jo * W + 128 * ti: jo * W + 128 * ti + 128],
                                         WO[:, jo * 1024 + 512 * dc: jo * 1024 + 512 * dc + 512],
                                         start=(jo == 0), stop=(jo == 3))
                        ctx.inc(i, "pe")

                    @block.vector
                    def _(v, ti=ti, dc=dc, wi=wi):
                        ctx.wait(v, "pe")
                        st_g = 4 * wi + ti
                        dstx = x_sb[:, st_g * D + 512 * dc: st_g * D + 512 * dc + 512]
                        ctx.inc(v.tensor_add(dstx, dstx, PS_A[:, 1536:2048]), "vec")

        atn.close()
        if ctx.consts.get("DBG") is not None and l == 0:
            dbg = ctx.consts["DBG"]

            @block.sync
            def _(sync):
                for nm in ("pe", "vec", "act"):
                    ctx.wait(sync, nm)
                dstd = dbg.rearrange("(t p) d -> p t d", p=128)
                srcd = x_sb[:].rearrange("p (t d) -> p t d", d=D)
                ctx.inc(sync.dma_start(out=dstd, in_=srcd), "sdma", 16)

            @block.vector
            def _(v):
                ctx.wait(v, "sdma")
                ctx.inc(v.nop(), "vec")
        emit_transpose_full(ctx, x_sb, xT, PS_A[:, 0:1024], rowb)

        mlp = ExitStack()
        mnr = ctx.sbuf("ff_mnr", [1, S], F32, mlp)
        rrow = ctx.sbuf("ff_rrow", [1, S], F32, mlp)
        scr1 = ctx.sbuf("ff_scr1", [1, 512], F32, mlp)
        rcol = ctx.sbuf("ff_rcol", [128, NT], F32, mlp)
        sqc = ctx.sbuf("ff_sqc", [128, 512], BF16, mlp)
        lg_t = ctx.sbuf("ff_lgt", [128, 8], F32, mlp)
        lb_t = ctx.sbuf("ff_lbt", [128, 8], F32, mlp)
        lg_b = ctx.sbuf("ff_lgb", [128, 8], BF16, mlp)
        lb_b = ctx.sbuf("ff_lbb", [128, 8], BF16, mlp)
        b1row = ctx.sbuf("ff_b1row", [1, 2 * HIDP], BF16, mlp)
        b2row = ctx.sbuf("ff_b2row", [1, D], BF16, mlp)
        urow = ctx.sbuf("ff_urow", [1, 1024], F32, mlp)
        bxp = ctx.sbuf("ff_bxp", [128, 512], BF16, mlp)
        hT = ctx.sbuf("ff_hT", [128, 22 * 512], BF16, mlp)
        ab_sb = ctx.sbuf("ff_ab", [128, 512], BF16, mlp)
        W2R = ctx.sbuf("ff_W2R", [128, 22 * 512], BF16, mlp)

        @block.gpsimd
        def _(g, l=l):
            for nm in ("pe", "vec", "act"):
                ctx.wait(g, nm)
            ctx.inc(g.dma_start(out=lg_t[:], in_=dL[l]["lg"]), "wdma", 16)
            ctx.inc(g.dma_start(out=lb_t[:], in_=dL[l]["lb"]), "wdma", 16)
            ctx.inc(g.dma_start(out=lg_b[:], in_=dL[l]["lg"]), "wdma", 16)
            ctx.inc(g.dma_start(out=lb_b[:], in_=dL[l]["lb"]), "wdma", 16)
            ctx.inc(g.dma_start(out=b1row[:], in_=dL[l]["b1"]), "wdma", 16)
            ctx.inc(g.dma_start(out=b2row[:], in_=dL[l]["b2"]), "wdma", 16)

        for scn in range(4):
            for j in range(DT):
                @block.scalar
                def _(sc, j=j, scn=scn):
                    ctx.wait(sc, "act")
                    ctx.wait(sc, "pe")
                    ctx.wait(sc, "vec")
                    s0 = 512 * scn
                    off = s0 if j >= 4 else s0 + 1
                    i = sc.activation(sqc[:], xT[:, j * XS + off: j * XS + off + 512],
                                      AF.Square)
                    ctx.inc(i, "act")

                @block.tensor
                def _(t, j=j, scn=scn):
                    ctx.wait(t, "act")
                    if j == 0:
                        ctx.wait(t, "vec")
                    s0 = 512 * scn
                    off = s0 if j >= 4 else s0 + 1
                    t.matmul(PS_B[0:1, 0:512], onescolb[:],
                             xT[:, j * XS + off: j * XS + off + 512],
                             start=(j == 0), stop=(j == DT - 1))
                    i = t.matmul(PS_B[0:1, 512:1024], onescolb[:], sqc[:],
                                 start=(j == 0), stop=(j == DT - 1))
                    ctx.inc(i, "pe")

            @block.vector
            def _(v, scn=scn):
                ctx.wait(v, "pe")
                s0 = 512 * scn
                ctx.inc(v.tensor_scalar_mul(mnr[0:1, s0:s0 + 512], PS_B[0:1, 0:512],
                                            1.0 / float(D)), "vec")
                ctx.wait(v, "vec")
                ctx.inc(v.tensor_scalar_mul(rrow[0:1, s0:s0 + 512], PS_B[0:1, 512:1024],
                                            1.0 / float(D)), "vec")
                ctx.wait(v, "vec")
                i = v.scalar_tensor_tensor(scr1[:], mnr[0:1, s0:s0 + 512],
                                           -1.0, mnr[0:1, s0:s0 + 512],
                                           ALU.mult, ALU.mult)
                ctx.inc(i, "vec")
                ctx.wait(v, "vec")
                i = v.tensor_add(rrow[0:1, s0:s0 + 512], rrow[0:1, s0:s0 + 512],
                                 scr1[:])
                ctx.inc(i, "vec")
                ctx.wait(v, "vec")
                ctx.inc(v.tensor_scalar_add(rrow[0:1, s0:s0 + 512],
                                            rrow[0:1, s0:s0 + 512], 1e-5), "vec")

        @block.scalar
        def _(sc):
            ctx.wait(sc, "vec")
            ctx.inc(sc.activation(rrow[:], rrow[:], AF.Sqrt), "act")

        @block.vector
        def _(v):
            ctx.wait(v, "act")
            ctx.inc(v.reciprocal(rrow[:], rrow[:]), "vec")
            ctx.wait(v, "vec")
            i = v.tensor_mul(mnr[:], mnr[:], rrow[:])
            ctx.inc(i, "vec")
            ctx.wait(v, "vec")
            ctx.inc(v.tensor_scalar_mul(mnr[:], mnr[:], -1.0), "vec")

        for t_i in range(NT):
            @block.tensor
            def _(t, t_i=t_i):
                ctx.wait(t, "vec")
                i = t.matmul(PS_B[:, 1024 + t_i:1024 + t_i + 1],
                             rrow[0:1, 128 * t_i:128 * t_i + 128], onesf1[:],
                             start=True, stop=True)
                ctx.inc(i, "pe")

        @block.vector
        def _(v):
            ctx.wait(v, "pe")
            ctx.inc(v.tensor_copy(rcol[:], PS_B[:, 1024:1024 + NT]), "vec")

        w2half_sem = "wd0"
        for dhalf in range(2):
            @block.gpsimd
            def _(g, l=l, dhalf=dhalf):
                ctx.wait(g, "pe")
                srcw = dL[l]["w2"].rearrange("(j p) c -> p j c", p=128)[:, :, 512 * dhalf:512 * dhalf + 512]
                dstw = W2R[:].rearrange("p (j c) -> p j c", c=512)
                ctx.inc(g.dma_start(out=dstw, in_=srcw), "w2dma", 16)
            w2_tok = ctx.tok("w2dma")

            for sg in range(4):
                for c in range(NC):
                    slot_i = pair_flip()
                    slot = ctx.consts["WSLOTS"][slot_i]
                    wsem, tok = wload(ctx, dL[l]["w1"], D, 512 * c, 512 * c + 512,
                                      slot, slot_i, pe_guard=ctx.tok("pe"))
                    @block.tensor
                    def _(t, slot=slot, tok=tok, wsem=wsem, c=c):
                        t.wait_ge(ctx.sget(wsem), tok)
                        ctx.wait(t, "vec")
                        ctx.wait(t, "wdma")
                        for j in range(DT):
                            t.matmul(PS_B[0:1, 0:512], lg_b[:, j:j + 1],
                                     slot[:, j * 512:(j + 1) * 512],
                                     start=(j == 0), stop=(j == DT - 1))
                        for j in range(DT):
                            i = t.matmul(PS_B[0:1, 512:1024], lb_b[:, j:j + 1],
                                         slot[:, j * 512:(j + 1) * 512],
                                         start=(j == 0), stop=(j == DT - 1))
                        ctx.inc(i, "pe")

                    @block.vector
                    def _(v, c=c):
                        ctx.wait(v, "pe")
                        ctx.inc(v.tensor_copy(urow[0:1, 0:512], PS_B[0:1, 0:512]), "vec")
                        ctx.wait(v, "vec")
                        i = v.tensor_add(urow[0:1, 512:1024], PS_B[0:1, 512:1024],
                                         b1row[0:1, 512 * c:512 * c + 512])
                        ctx.inc(i, "vec")

                    @block.tensor
                    def _(t, c=c):
                        ctx.wait(t, "vec")
                        i = t.matmul(PS_B[:, 0:512], ctx.consts["ONESF128"][0:1, :],
                                     urow[0:1, 512:1024], start=True, stop=True)
                        ctx.inc(i, "pe")

                    @block.vector
                    def _(v, c=c):
                        ctx.wait(v, "pe")
                        ctx.inc(v.tensor_copy(bxp[:, 0:512], PS_B[:, 0:512]), "vec")

                    for j in range(DT):
                        @block.vector
                        def _(v, j=j, slot=slot):
                            ctx.wait(v, "vec")
                            if j == 0:
                                ctx.wait(v, "pe")
                            sl = slot[:, j * 512:(j + 1) * 512]
                            i = v.tensor_scalar(sl, sl, lg_t[:, j:j + 1], None, ALU.mult)
                            ctx.inc(i, "vec")

                    slot_vec_tok = ctx.tok("vec")
                    for ti in range(4):
                        t_i = 4 * sg + ti
                        @block.tensor
                        def _(t, t_i=t_i, c=c, slot=slot):
                            ctx.wait(t, "vec")
                            ctx.wait(t, "act")
                            s0 = 128 * t_i
                            for half, pso in ((0, 1024), (1, 1536)):
                                for j in range(DT):
                                    off = s0 if j >= 4 else s0 + 1
                                    t.matmul(PS_A[:, pso:pso + 256],
                                             xT[:, j * XS + off: j * XS + off + 128],
                                             slot[:, j * 512 + 256 * half: j * 512 + 256 * half + 256],
                                             start=(j == 0), stop=False)
                                i = t.matmul(PS_A[:, pso:pso + 256],
                                             mnr[0:1, s0:s0 + 128],
                                             urow[0:1, 256 * half:256 * half + 256],
                                             start=False, stop=True)
                            ctx.inc(i, "pe")

                        @block.vector
                        def _(v, t_i=t_i):
                            ctx.wait(v, "pe")
                            i = v.scalar_tensor_tensor(ab_sb[:, 0:256], PS_A[:, 1024:1280],
                                                       rcol[:, t_i:t_i + 1], bxp[:, 0:256],
                                                       ALU.mult, ALU.add)
                            ctx.inc(i, "vec")
                            ctx.wait(v, "vec")
                            i = v.scalar_tensor_tensor(ab_sb[:, 256:512], PS_A[:, 1536:1792],
                                                       rcol[:, t_i:t_i + 1], bxp[:, 256:512],
                                                       ALU.mult, ALU.add)
                            ctx.inc(i, "vec")

                        @block.scalar
                        def _(sc, t_i=t_i):
                            ctx.wait(sc, "vec")
                            ctx.wait(sc, "act")
                            i = sc.activation(ab_sb[:, 256:512], ab_sb[:, 256:512], AF.Gelu)
                            ctx.inc(i, "act")

                        @block.vector
                        def _(v, t_i=t_i):
                            ctx.wait(v, "act")
                            i = v.tensor_mul(ab_sb[:, 0:256], ab_sb[:, 0:256],
                                             ab_sb[:, 256:512])
                            ctx.inc(i, "vec")

                        @block.tensor
                        def _(t, ti=ti, c=c):
                            ctx.wait(t, "vec")
                            for b2i in range(2):
                                i = t.matmul(PS_A[:, 1024 + 128 * b2i:1024 + 128 * b2i + 128],
                                             ab_sb[:, 128 * b2i:128 * b2i + 128],
                                             ctx.consts["IDENT"][:], start=True, stop=True,
                                             skip_group_check=True)
                            ctx.inc(i, "pe")

                        @block.vector
                        def _(v, ti=ti, c=c):
                            ctx.wait(v, "pe")
                            dsth = hT[:, (2 * c) * 512 + 128 * ti: (2 * c) * 512 + 128 * ti + 128]
                            ctx.inc(v.tensor_copy(dsth, PS_A[:, 1024:1152]), "vec")
                            ctx.wait(v, "vec")
                            dsth = hT[:, (2 * c + 1) * 512 + 128 * ti: (2 * c + 1) * 512 + 128 * ti + 128]
                            ctx.inc(v.tensor_copy(dsth, PS_A[:, 1152:1280]), "vec")

                for ti in range(4):
                    t_i = 4 * sg + ti
                    @block.tensor
                    def _(t, t_i=t_i, ti=ti, dhalf=dhalf, w2_tok=w2_tok):
                        t.wait_ge(ctx.sget("w2dma"), w2_tok)
                        ctx.wait(t, "vec")
                        ctx.wait(t, "act")
                        for j in range(22):
                            t.matmul(PS_A[:, 1536:2048],
                                     hT[:, j * 512 + 128 * ti: j * 512 + 128 * ti + 128],
                                     W2R[:, j * 512:(j + 1) * 512],
                                     start=(j == 0), stop=False)
                        i = t.matmul(PS_A[:, 1536:2048], ctx.consts["ONES"][0:1, 0:128],
                                     b2row[0:1, 512 * dhalf:512 * dhalf + 512],
                                     start=False, stop=True)
                        ctx.inc(i, "pe")

                    @block.vector
                    def _(v, t_i=t_i, dhalf=dhalf):
                        ctx.wait(v, "pe")
                        dstx = x_sb[:, t_i * D + 512 * dhalf: t_i * D + 512 * dhalf + 512]
                        ctx.inc(v.tensor_add(dstx, dstx, PS_A[:, 1536:2048]), "vec")
        mlp.close()
        if ctx.consts.get("DBG2") is not None and l == 0:
            dbg = ctx.consts["DBG2"]

            @block.sync
            def _(sync):
                for nm in ("pe", "vec", "act"):
                    ctx.wait(sync, nm)
                dstd = dbg.rearrange("(t p) d -> p t d", p=128)
                srcd = x_sb[:].rearrange("p (t d) -> p t d", d=D)
                ctx.inc(sync.dma_start(out=dstd, in_=srcd), "sdma", 16)

            @block.vector
            def _(v):
                ctx.wait(v, "sdma")
                ctx.inc(v.nop(), "vec")

    xin_b = ctx.sbuf("ff_xinb", [128, D], F32)
    for t_i in range(NT):
        @block.sync
        def _(sync, t_i=t_i):
            if t_i > 0:
                ctx.wait(sync, "vec")
            else:
                for nm in ("pe", "vec", "act"):
                    ctx.wait(sync, nm)
            if "xsp" in ctx.cnt and ctx.cnt["xsp"]:
                ctx.wait(sync, "xsp")
            src = x_in_dram.rearrange("(t p) d -> p t d", p=128)[:, t_i, :]
            ctx.inc(sync.dma_start(out=xin_b[:], in_=src), "xrl", 16)

        @block.vector
        def _(v, t_i=t_i):
            ctx.wait(v, "xrl")
            ctx.wait(v, "vec")
            dstx = x_sb[:, t_i * D:(t_i + 1) * D]
            ctx.inc(v.tensor_add(dstx, dstx, xin_b[:]), "vec")





def emit_l1ab(ctx, x_sb, w, PS_A, PS_B):
    block = ctx.block
    DS = 16
    C2 = 2048
    CT = C2 // 128
    XS2 = S + 3

    W_in = np.asarray(w["L1_attn_b_W_in"], np.float32)
    Wxi = np.ascontiguousarray(W_in[:, 0:C2])
    Wz = np.ascontiguousarray(W_in[:, C2:2 * C2])
    conv_w = np.asarray(w["L1_attn_b_conv_w"], np.float32)
    conv_b = np.asarray(w["L1_attn_b_conv_b"], np.float32)
    W_x = np.asarray(w["L1_attn_b_W_x"], np.float32)
    W_dt = np.asarray(w["L1_attn_b_W_dt"], np.float32)
    b_dt = np.asarray(w["L1_attn_b_b_dt"], np.float32)
    A_log = np.asarray(w["L1_attn_b_A_log"], np.float32)
    Dd = np.asarray(w["L1_attn_b_D"], np.float32)
    W_out = np.asarray(w["L1_attn_b_W_out"], np.float32)

    d_xi = ctx.din("w_l1ab_xi", Wxi)
    d_z = ctx.din("w_l1ab_z", Wz)
    d_out = ctx.din("w_l1ab_out", W_out)
    cw_t, _ = load_row_const(ctx, "l1ab_cw",
                             conv_w[:, 0, :].reshape(CT, 128, 4).transpose(1, 0, 2).reshape(128, CT * 4))
    cb_t, _ = load_row_const(ctx, "l1ab_cb", conv_b.reshape(CT, 128).T.copy())
    alog_r, _ = load_row_const(ctx, "l1ab_alog", A_log[None, :])
    d_col, _ = load_row_const(ctx, "l1ab_d", Dd.reshape(CT, 128).T.copy())
    onesrowb, _ = load_row_const(ctx, "l1ab_ones512", np.ones((1, 512), np.float32), BF16)
    onecolf, _ = load_row_const(ctx, "l1ab_onecol", np.ones((128, 1), np.float32))

    XI = ctx.sbuf("mb_XI", [128, CT * XS2], BF16)
    ln1 = ExitStack()
    lnT = ctx.sbuf("mb_lnT", [128, DT * S], BF16, ln1)

    emit_ln_transpose(ctx, x_sb, lnT, PS_A[:, 0:1024], scope=ln1,
                      lnrow_buf=XI[:, 0:D])

    @block.vector
    def _(v):
        ctx.wait(v, "vec")
        ctx.wait(v, "act")
        z0 = XI[:].rearrange("p (j s) -> p j s", s=XS2)[:, :, 0:3]
        ctx.inc(v.memset(z0, 0.0), "vec")
    gemm_T(ctx, d_xi, D, C2, lnT, S, XI, XS2, PS_A[:, 0:512], dst_off=3)
    ln1.close()

    scan_scope = ExitStack()
    wdt_r, _ = load_row_const(ctx, "l1ab_wdt", W_dt[:, 0][None, :], BF16,
                              scope=scan_scope)
    bdt_r, _ = load_row_const(ctx, "l1ab_bdt", b_dt[None, :], BF16,
                              scope=scan_scope)
    wb_sb, _ = load_row_const(ctx, "l1ab_wb",
                              W_x[:, 1:17].reshape(CT, 128, 16).transpose(1, 0, 2).reshape(128, CT * 16),
                              BF16, scope=scan_scope)
    wc_sb, _ = load_row_const(ctx, "l1ab_wc",
                              W_x[:, 17:33].reshape(CT, 128, 16).transpose(1, 0, 2).reshape(128, CT * 16),
                              BF16, scope=scan_scope)
    wd_sb, _ = load_row_const(ctx, "l1ab_wd",
                              W_x[:, 0:1].reshape(CT, 128).T.copy(), BF16,
                              scope=scan_scope)
    selst_np = np.zeros((16, 16 * 128), np.float32)
    for st in range(16):
        selst_np[st, 128 * st:128 * (st + 1)] = 1.0
    selst, _ = load_row_const(ctx, "l1ab_selst", selst_np, BF16, scope=scan_scope)
    ydl = ctx.sbuf("mb_y", [128, S], F32, scan_scope)
    dlt = ctx.sbuf("mb_dlt", [128, S], BF16, scan_scope)
    wdx = ctx.sbuf("mb_wdx", [128, S], BF16, scan_scope)
    abuf = ctx.sbuf("mb_a", [128, S], BF16, scan_scope)
    bbuf = ctx.sbuf("mb_b", [128, S], BF16, scan_scope)
    hbuf = ctx.sbuf("mb_h", [128, S], BF16, scan_scope)
    tbuf = ctx.sbuf("mb_t", [128, S], BF16, scan_scope)
    cacc = ctx.sbuf("mb_cacc", [128, 512], F32, scan_scope)
    h0 = ctx.sbuf("mb_h0", [128, 1], F32, scan_scope)
    XP0 = ctx.sbuf("mb_xp0", [1, S], BF16, scan_scope)
    XPB = ctx.sbuf("mb_xpb", [16, S], BF16, scan_scope)
    XPC = ctx.sbuf("mb_xpc", [16, S], BF16, scan_scope)
    abc = ctx.sbuf("mb_abc", [128, DS], F32, scan_scope)

    for cch in (3, 2, 1, 0):
        for j in range(CT):
            @block.vector
            def _(v, cch=cch, j=j):
                ctx.wait(v, "act")
                ctx.wait(v, "vec")
                base = j * XS2 + 512 * cch
                i = v.tensor_scalar(cacc[:], XI[:, base:base + 512],
                                    cw_t[:, 4 * j:4 * j + 1], None, ALU.mult)
                ctx.inc(i, "vec")
                for kk in (1, 2, 3):
                    ctx.wait(v, "vec")
                    i = v.scalar_tensor_tensor(cacc[:], XI[:, base + kk:base + kk + 512],
                                               cw_t[:, 4 * j + kk:4 * j + kk + 1],
                                               cacc[:], ALU.mult, ALU.add)
                    ctx.inc(i, "vec")

            @block.scalar
            def _(sc, cch=cch, j=j):
                ctx.wait(sc, "vec")
                ctx.wait(sc, "act")
                i = sc.activation(XI[:, j * XS2 + 3 + 512 * cch: j * XS2 + 3 + 512 * cch + 512],
                                  cacc[:], AF.Silu, bias=cb_t[:, j:j + 1])
                ctx.inc(i, "act")

    for scn in range(4):
        @block.tensor
        def _(t, scn=scn):
            ctx.wait(t, "act")
            ctx.wait(t, "vec")
            ctx.wait(t, "wdma")
            sl = slice(512 * scn, 512 * scn + 512)
            for j in range(CT):
                xi_sl = XI[:, j * XS2 + 3 + 512 * scn: j * XS2 + 3 + 512 * scn + 512]
                t.matmul(PS_A[0:1, 0:512], wd_sb[:, j:j + 1], xi_sl,
                         start=(j == 0), stop=(j == CT - 1))
                t.matmul(PS_A[0:16, 512:1024], wb_sb[:, 16 * j:16 * (j + 1)], xi_sl,
                         start=(j == 0), stop=(j == CT - 1))
                i = t.matmul(PS_A[0:16, 1024:1536], wc_sb[:, 16 * j:16 * (j + 1)], xi_sl,
                             start=(j == 0), stop=(j == CT - 1))
            ctx.inc(i, "pe")

        @block.scalar
        def _(sc, scn=scn):
            ctx.wait(sc, "pe")
            sl = slice(512 * scn, 512 * scn + 512)
            sc.activation(XP0[0:1, sl], PS_A[0:1, 0:512], AF.Identity)
            sc.activation(XPB[0:16, sl], PS_A[0:16, 512:1024], AF.Identity)
            i = sc.activation(XPC[0:16, sl], PS_A[0:16, 1024:1536], AF.Identity)
            ctx.inc(i, "act")

    @block.scalar
    def _(sc):
        ctx.wait(sc, "wdma")
        ctx.wait(sc, "act")
        ctx.inc(sc.activation(alog_r[:], alog_r[:], AF.Exp), "act")

    @block.tensor
    def _(t):
        ctx.wait(t, "act")
        ctx.wait(t, "vec")
        i = t.matmul(PS_B[:, 0:16], ctx.consts["ONESF128"][0:1, :], alog_r[:],
                     start=True, stop=True)
        ctx.inc(i, "pe")

    @block.vector
    def _(v):
        ctx.wait(v, "pe")
        ctx.inc(v.tensor_scalar_mul(abc[:], PS_B[:, 0:16], -1.0), "vec")

    for ct in range(CT):
        for cch in range(4):
            @block.tensor
            def _(t, ct=ct, cch=cch):
                ctx.wait(t, "act")
                ctx.wait(t, "vec")
                t.matmul(PS_A[:, 512 * cch:512 * cch + 512],
                         wdt_r[0:1, 128 * ct:128 * ct + 128],
                         XP0[0:1, 512 * cch:512 * cch + 512],
                         start=True, stop=False)
                i = t.matmul(PS_A[:, 512 * cch:512 * cch + 512],
                             bdt_r[0:1, 128 * ct:128 * ct + 128],
                             onesrowb[0:1, 0:512],
                             start=False, stop=True)
                ctx.inc(i, "pe")

        @block.scalar
        def _(sc, ct=ct):
            ctx.wait(sc, "pe")
            ctx.wait(sc, "act")
            ctx.wait(sc, "vec")
            i = sc.activation(tbuf[:], PS_A[:, 0:S], AF.Abs)
            ctx.inc(i, "act")
            ctx.wait(sc, "act")
            i = sc.activation(tbuf[:], tbuf[:], AF.Exp, scale=-1.0)
            ctx.inc(i, "act")
            ctx.wait(sc, "act")
            i = sc.activation(tbuf[:], tbuf[:], AF.Ln, bias=onecolf[:, 0:1])
            ctx.inc(i, "act")

        @block.vector
        def _(v, ct=ct):
            ctx.wait(v, "act")
            i = v.tensor_scalar_max(dlt[:], PS_A[:, 0:S], 0.0)
            ctx.inc(i, "vec")
            ctx.wait(v, "vec")
            i = v.tensor_add(dlt[:], dlt[:], tbuf[:])
            ctx.inc(i, "vec")
            ctx.wait(v, "vec")
            i = v.tensor_mul(wdx[:], dlt[:], XI[:, ct * XS2 + 3: ct * XS2 + 3 + S])
            ctx.inc(i, "vec")
            ctx.wait(v, "vec")
            ctx.inc(v.memset(ydl[:], 0.0), "vec")
            ctx.wait(v, "vec")
            ctx.inc(v.memset(h0[:], 0.0), "vec")

        for st in range(DS):
            @block.scalar
            def _(sc, ct=ct, st=st):
                ctx.wait(sc, "vec")
                ctx.wait(sc, "act")
                ctx.wait(sc, "pe")
                i = sc.activation(abuf[:], dlt[:], AF.Exp, scale=abc[:, st:st + 1])
                ctx.inc(i, "act")

            for cch in range(4):
                @block.tensor
                def _(t, ct=ct, st=st, cch=cch):
                    ctx.wait(t, "vec")
                    ctx.wait(t, "act")
                    i = t.matmul(PS_B[:, 512 * cch:512 * cch + 512],
                                 selst[0:16, 128 * st:128 * st + 128],
                                 XPB[0:16, 512 * cch:512 * cch + 512],
                                 start=True, stop=True)
                    ctx.inc(i, "pe")

            @block.vector
            def _(v, ct=ct, st=st):
                ctx.wait(v, "pe")
                ctx.wait(v, "vec")
                i = v.tensor_mul(bbuf[:], wdx[:], PS_B[:, 0:S])
                ctx.inc(i, "vec")
                ctx.wait(v, "vec")
                ctx.wait(v, "act")
                i = v.tensor_tensor_scan(hbuf[:], abuf[:], bbuf[:], h0[:],
                                         ALU.mult, ALU.add)
                ctx.inc(i, "vec")

            for cch in range(4):
                @block.tensor
                def _(t, ct=ct, st=st, cch=cch):
                    ctx.wait(t, "vec")
                    i = t.matmul(PS_B[:, 512 * cch:512 * cch + 512],
                                 selst[0:16, 128 * st:128 * st + 128],
                                 XPC[0:16, 512 * cch:512 * cch + 512],
                                 start=True, stop=True)
                    ctx.inc(i, "pe")

            @block.vector
            def _(v, ct=ct, st=st):
                ctx.wait(v, "pe")
                ctx.wait(v, "vec")
                i = v.tensor_mul(tbuf[:], hbuf[:], PS_B[:, 0:S])
                ctx.inc(i, "vec")
                ctx.wait(v, "vec")
                i = v.tensor_add(ydl[:], ydl[:], tbuf[:])
                ctx.inc(i, "vec")

        @block.vector
        def _(v, ct=ct):
            ctx.wait(v, "vec")
            i = v.scalar_tensor_tensor(ydl[:], XI[:, ct * XS2 + 3: ct * XS2 + 3 + S],
                                       d_col[:, ct:ct + 1], ydl[:], ALU.mult, ALU.add)
            ctx.inc(i, "vec")
            ctx.wait(v, "vec")
            ctx.inc(v.tensor_copy(XI[:, ct * XS2 + 3: ct * XS2 + 3 + S], ydl[:]), "vec")

    scan_scope.close()
    zph = ExitStack()
    zbuf = ctx.sbuf("mb_zbuf", [128, CT * 512], BF16, zph)
    lnT2 = lnT
    lnTz = ctx.sbuf("mb_lnTz", [128, DT * S], BF16, zph)
    emit_ln_transpose(ctx, x_sb, lnTz, PS_A[:, 0:1024], scope=zph,
                      lnrow_buf=zbuf[:, 0:D])

    for scn in range(4):
        gemm_T(ctx, d_z, D, C2, lnTz, S, zbuf, 512, PS_A[:, 0:512],
               s_lo=512 * scn, s_hi=512 * scn + 512)
        for j in range(CT):
            @block.scalar
            def _(sc, scn=scn, j=j):
                ctx.wait(sc, "act")
                ctx.wait(sc, "pe")
                sl = zbuf[:, j * 512:(j + 1) * 512]
                i = sc.activation(sl, sl, AF.Silu)
                ctx.inc(i, "act")

            @block.vector
            def _(v, scn=scn, j=j):
                ctx.wait(v, "act")
                ctx.wait(v, "vec")
                sl = zbuf[:, j * 512:(j + 1) * 512]
                i = v.tensor_mul(sl, sl, XI[:, j * XS2 + 3 + 512 * scn: j * XS2 + 3 + 512 * scn + 512])
                ctx.inc(i, "vec")

        WS = ctx.consts["WSLOTS"]
        w_cols = 256
        for li, c0 in enumerate(range(0, D, w_cols)):
            slot = WS[li % 2]
            wsem, tok = wload(ctx, d_out, C2, c0, c0 + w_cols, slot, li % 2,
                              pe_guard=ctx.tok("pe"))
            for ti in range(4):
                t_i = 4 * scn + ti

                @block.tensor
                def _(t, t_i=t_i, ti=ti, c0=c0, slot=slot, tok=tok, wsem=wsem):
                    t.wait_ge(ctx.sget(wsem), tok)
                    ctx.wait(t, "vec")
                    ctx.wait(t, "act")
                    for j in range(CT):
                        i = t.matmul(PS_A[:, 1536:1536 + w_cols],
                                     zbuf[:, j * 512 + 128 * ti: j * 512 + 128 * ti + 128],
                                     slot[:, j * w_cols:(j + 1) * w_cols],
                                     start=(j == 0), stop=(j == CT - 1))
                    ctx.inc(i, "pe")

                @block.vector
                def _(v, t_i=t_i, c0=c0):
                    ctx.wait(v, "pe")
                    dstx = x_sb[:, t_i * D + c0: t_i * D + c0 + w_cols]
                    i = v.tensor_add(dstx, dstx, PS_A[:, 1536:1536 + w_cols])
                    ctx.inc(i, "vec")
    zph.close()



def build_single_sublayer_program(name, weights):
    ctx = Ctx()
    with ctx.stack:
        x_in = ctx.nc.dram_tensor("x_in", [S, D], F32, kind="ExternalInput").ap()
        y_out = ctx.dout("y_out", [S, D], F32)
        x_sb = ctx.sbuf("x_sb", [128, NT * D], F32)
        PS_A = ctx.psum([128, 2048], F32)
        PS_B = ctx.psum([128, 2048], F32)
        wsc = 4096 if (name.endswith("ffn") or name == "l1ab") else WSLOT_COLS
        ctx.consts["WSLOT_COLS"] = wsc
        ws0 = ctx.sbuf("WSLOT0", [128, wsc], BF16)
        ws1 = ctx.sbuf("WSLOT1", [128, wsc], BF16)
        ctx.consts["WSLOTS"] = (ws0, ws1)
        import os
        if os.environ.get("KTR"):
            ctx.consts["KTR"] = os.environ["KTR"]
        if os.environ.get("KDBG"):
            ctx.consts["DBG"] = ctx.dout("dbg1", [S, D], F32)
            ctx.consts["DBG2"] = ctx.dout("dbg2", [S, D], F32)
            ctx.consts["DBGQ"] = ctx.nc.dram_tensor("dbgq", [128, 4 * 512], mybir.dt.bfloat16, kind="ExternalOutput").ap()
            ctx.consts["DBGK"] = ctx.nc.dram_tensor("dbgk", [128, 4 * 512], mybir.dt.bfloat16, kind="ExternalOutput").ap()
            ctx.consts["DBGO"] = ctx.nc.dram_tensor("dbgo", [128, 2 * 4 * 512], mybir.dt.bfloat16, kind="ExternalOutput").ap()
            ctx.consts["DBGS"] = ctx.dout("dbgs", [128, 16], F32)
            ctx.consts["DBGV"] = ctx.nc.dram_tensor("dbgv", [128, 8 * 512], mybir.dt.bfloat16, kind="ExternalOutput").ap()
            ctx.consts["DBGE"] = ctx.nc.dram_tensor("dbge", [128, 512], mybir.dt.bfloat16, kind="ExternalOutput").ap()
        with ctx.nc.Block() as block:
            ctx.block = block
            emit_load_consts(ctx, need_identf=(name not in ("l0aa", "l1ab")))
            emit_load_x(ctx, x_in, x_sb)
            if name.endswith("ffn"):
                emit_ffn(ctx, x_sb, weights, name[:2].upper().replace("L", "L") and
                         ("L0_ffn" if name == "l0ffn" else "L1_ffn"),
                         PS_A, PS_B, x_in)
            else:
                EMITTERS[name](ctx, x_sb, weights, PS_A, PS_B)
            emit_store_x(ctx, x_sb, y_out)
        ctx.nc.finalize()
    return ctx


EMITTERS = {"l0aa": emit_l0aa, "l0ab": emit_l0ab, "l1aa": emit_l1aa,
            "l1ab": emit_l1ab}


def emit_ffn_launch(ctx, x_sb, weights, PS_A, PS_B, pfx, x_in):
    emit_ffn(ctx, x_sb, weights, pfx, PS_A, PS_B, x_in)


def emit_barrier(ctx):
    block = ctx.block

    @block.tensor
    def _(t):
        ctx.wait(t, "vec")
        ctx.wait(t, "act")
        ctx.inc(t.nop(), "pe")

    @block.vector
    def _(v):
        ctx.wait(v, "pe")
        ctx.wait(v, "act")
        ctx.inc(v.nop(), "vec")

    @block.scalar
    def _(sc):
        ctx.wait(sc, "pe")
        ctx.wait(sc, "vec")
        ctx.inc(sc.nop(), "act")


def build_group_program(names, weights):
    ctx = Ctx()
    with ctx.stack:
        x_in = ctx.nc.dram_tensor("x_in", [S, D], F32, kind="ExternalInput").ap()
        y_out = ctx.dout("y_out", [S, D], F32)
        x_sb = ctx.sbuf("x_sb", [128, NT * D], F32)
        PS_A = ctx.psum([128, 2048], F32)
        PS_B = ctx.psum([128, 2048], F32)
        with ctx.nc.Block() as block:
            ctx.block = block
            emit_load_consts(ctx, need_identf=False)
            emit_load_x(ctx, x_in, x_sb)
            for i, name in enumerate(names):
                if i > 0:
                    emit_barrier(ctx)
                scope = ExitStack()
                ctx.cur_scope = scope
                wsc = 4096 if (name.endswith("ffn") or name == "l1ab") else 2048
                ctx.consts["WSLOT_COLS"] = wsc
                ws0 = ctx.sbuf("WSLOT0", [128, wsc], BF16, scope)
                ws1 = ctx.sbuf("WSLOT1", [128, wsc], BF16, scope)
                ctx.consts["WSLOTS"] = (ws0, ws1)
                if name not in ("l0aa", "l1ab"):
                    identf = ctx.sbuf("IDENTF", [128, 128], F32, scope)
                    ctx.consts["IDENTF"] = identf
                    idd = ctx.din("c_ident", np.eye(128, dtype=np.float32))

                    @block.gpsimd
                    def _(g):
                        for nm in ("pe", "vec", "act"):
                            if ctx.cnt.get(nm):
                                ctx.wait(g, nm)
                        ctx.inc(g.dma_start(out=identf[:], in_=idd[:]), "wdma", 16)

                    @block.tensor
                    def _(t):
                        ctx.wait(t, "wdma")
                        ctx.inc(t.nop(), "pe")
                if name.endswith("ffn"):
                    pfx = "L0_ffn" if name == "l0ffn" else "L1_ffn"
                    xsp = ctx.nc.dram_tensor(f"xsp_{i}", [S, D], F32,
                                             kind="Internal").ap()

                    @block.sync
                    def _(sync):
                        for nm in ("pe", "vec", "act"):
                            if ctx.cnt.get(nm):
                                ctx.wait(sync, nm)
                        dst = xsp.rearrange("(t p) d -> p t d", p=128)
                        srcs = x_sb[:].rearrange("p (t d) -> p t d", d=D)
                        ctx.inc(sync.dma_start(out=dst, in_=srcs), "xsp", 16)

                    @block.vector
                    def _(v):
                        ctx.wait(v, "xsp")
                        ctx.inc(v.nop(), "vec")

                    @block.scalar
                    def _(sc):
                        ctx.wait(sc, "xsp")
                        ctx.inc(sc.nop(), "act")
                    emit_ffn(ctx, x_sb, weights, pfx, PS_A, PS_B, xsp)
                else:
                    EMITTERS[name](ctx, x_sb, weights, PS_A, PS_B)
                ctx.cur_scope = None
                scope.close()
            emit_store_x(ctx, x_sb, y_out)
        ctx.nc.finalize()
    return ctx


CHAIN = ["l0aa", "l0ffn", "l0ab", "l0ffn", "l1aa", "l1ffn", "l1ab", "l1ffn"]


def kernel(**inputs):
    import os
    x = np.ascontiguousarray(np.asarray(inputs["L0_attn_a_x"], np.float32))
    B = x.shape[0]
    ng = int(os.environ.get("KGROUPS") or 1)
    if ng >= len(CHAIN):
        groups = [[n] for n in CHAIN]
    else:
        per = (len(CHAIN) + ng - 1) // ng
        groups = [CHAIN[i:i + per] for i in range(0, len(CHAIN), per)]
    x_cur = x
    progs = {}
    for g in groups:
        key = ",".join(g)
        if key not in progs:
            if len(g) == 1:
                progs[key] = build_single_sublayer_program(g[0], inputs)
            else:
                progs[key] = build_group_program(g, inputs)
    for g in groups:
        ctx = progs[",".join(g)]
        in_maps = []
        for b in range(B):
            m = dict(ctx.inputs)
            m["x_in"] = np.ascontiguousarray(x_cur[b])
            in_maps.append(m)
        res = run_bass_kernel_spmd(ctx.nc, in_maps, core_ids=list(range(B)))
        x_cur = np.stack([r["y_out"] for r in res.results])
    return x_cur


# version 07
# speedup vs baseline: 1.0299x; 1.0263x over previous
import numpy as np
from contextlib import ExitStack

import concourse.bass as bass
import concourse.mybir as mybir
from concourse.bass_utils import run_bass_kernel_spmd

F32 = mybir.dt.float32
BF16 = mybir.dt.bfloat16
AF = mybir.ActivationFunctionType
ALU = mybir.AluOpType
AX = mybir.AxisListType

S = 2048
D = 1024
NT = S // 128
DT = D // 128


class Ctx:

    def __init__(self):
        self.nc = bass.Bass("TRN2", target_bir_lowering=False)
        self.stack = ExitStack()
        self.sem = {}
        self.cnt = {}
        self.inputs = {}
        self.consts = {}
        self.block = None
        self.cur_scope = None
        self.uid = 0

    def nid(self):
        self.uid += 1
        return self.uid

    def sget(self, name):
        if name not in self.sem:
            self.sem[name] = self.stack.enter_context(self.nc.semaphore(name))
            self.cnt[name] = 0
        return self.sem[name]

    def inc(self, ins, name, k=1):
        ins.then_inc(self.sget(name), k)
        self.cnt[name] += k
        return self.cnt[name]

    def tok(self, name):
        self.sget(name)
        return self.cnt[name]

    def wait(self, eng, name, val=None):
        eng.wait_ge(self.sget(name), self.cnt[name] if val is None else val)

    def din(self, name, arr, dtype=F32):
        if name in self.inputs:
            return self._dins[name]
        t = self.nc.dram_tensor(name, list(arr.shape), dtype, kind="ExternalInput")
        self.inputs[name] = np.ascontiguousarray(arr)
        if not hasattr(self, "_dins"):
            self._dins = {}
        self._dins[name] = t.ap()
        return t.ap()

    def dout(self, name, shape, dtype=F32):
        return self.nc.dram_tensor(name, list(shape), dtype, kind="ExternalOutput").ap()

    def sbuf(self, name, shape, dtype, scope=None):
        return (scope or self.cur_scope or self.stack).enter_context(
            self.nc.sbuf_tensor(f"{name}_{self.nid()}", list(shape), dtype))

    def psum(self, shape, dtype=F32):
        return self.stack.enter_context(
            self.nc.psum_tensor(f"ps_{self.nid()}", list(shape), dtype))



def emit_load_consts(ctx, need_identf=True):
    block = ctx.block
    ident = ctx.sbuf("IDENT", [128, 128], BF16)
    ctx.consts["IDENT"] = ident
    idd = ctx.din("c_ident", np.eye(128, dtype=np.float32))
    if need_identf:
        identf = ctx.sbuf("IDENTF", [128, 128], F32)
        ctx.consts["IDENTF"] = identf
    ones = ctx.sbuf("ONES", [1, 128], BF16)
    ctx.consts["ONES"] = ones
    onesf128 = ctx.sbuf("ONESF128", [1, 128], F32)
    ctx.consts["ONESF128"] = onesf128
    oned = ctx.din("c_ones", np.ones((1, 512), np.float32))

    @block.gpsimd
    def _(g):
        ctx.inc(g.dma_start(out=ident[:], in_=idd[:]), "wdma", 16)
        if need_identf:
            ctx.inc(g.dma_start(out=identf[:], in_=idd[:]), "wdma", 16)
        ctx.inc(g.dma_start(out=ones[:], in_=oned[0:1, 0:128]), "wdma", 16)
        ctx.inc(g.dma_start(out=onesf128[:], in_=oned[0:1, 0:128]), "wdma", 16)
    ctx.consts["CONST_TOK"] = ctx.tok("wdma")


def emit_load_x(ctx, x_dram, x_sb):
    block = ctx.block

    @block.sync
    def _(sync):
        src = x_dram.rearrange("(t p) d -> p t d", p=128)
        dst = x_sb[:].rearrange("p (t d) -> p t d", d=D)
        ctx.inc(sync.dma_start(out=dst, in_=src), "xin", 16)


def emit_store_x(ctx, x_sb, y_dram):
    block = ctx.block

    @block.sync
    def _(sync):
        for name in ("pe", "vec", "act", "gp"):
            if name in ctx.cnt and ctx.cnt[name]:
                ctx.wait(sync, name)
        dst = y_dram.rearrange("(t p) d -> p t d", p=128)
        src2 = x_sb[:].rearrange("p (t d) -> p t d", d=D)
        ctx.inc(sync.dma_start(out=dst, in_=src2), "sdma", 16)
        ctx.wait(sync, "sdma")


def emit_ln_transpose(ctx, x_sb, out_T, ps_tr, gb_ap=None, out_also=None, scope=None,
                      lnrow_buf=None):
    block = ctx.block
    raw = ctx.sbuf("ln_raw", [128, 12], F32, scope)
    mv = ctx.sbuf("ln_mv", [128, 2 * NT], F32, scope)
    rstd = ctx.sbuf("ln_rstd", [128, NT], F32, scope)
    nmr = ctx.sbuf("ln_nmr", [128, NT], F32, scope)
    lnrow = lnrow_buf if lnrow_buf is not None else ctx.sbuf("ln_row", [128, D], BF16, scope)

    @block.vector
    def _(v):
        ctx.wait(v, "xin")
        if ctx.cnt.get("vec"):
            ctx.wait(v, "vec")
        for t in range(NT):
            xt = x_sb[:, t * D:(t + 1) * D]
            v.bn_stats(raw[:, 0:6], xt[:, 0:512])
            ctx.inc(v.bn_stats(raw[:, 6:12], xt[:, 512:1024]), "vec")
            ctx.wait(v, "vec")
            ctx.inc(v.bn_aggr(mv[:, 2 * t:2 * t + 2], raw[:]), "vec")
            ctx.wait(v, "vec")
        var_v = mv[:].rearrange("p (t two) -> p t two", two=2)[:, :, 1:2]
        ctx.inc(v.tensor_scalar_add(rstd[:].rearrange("p t -> p t ()"), var_v, 1e-5), "vec")

    @block.scalar
    def _(sc):
        ctx.wait(sc, "vec")
        ctx.inc(sc.activation(rstd[:], rstd[:], AF.Sqrt), "act")

    @block.vector
    def _(v):
        ctx.wait(v, "act")
        ctx.inc(v.reciprocal(rstd[:], rstd[:]), "vec")
        ctx.wait(v, "vec")
        mean_v = mv[:].rearrange("p (t two) -> p t two", two=2)[:, :, 0:1]
        i = v.tensor_tensor(nmr[:].rearrange("p t -> p t ()"), mean_v,
                            rstd[:].rearrange("p t -> p t ()"), ALU.mult)
        ctx.inc(i, "vec")
        ctx.wait(v, "vec")
        ctx.inc(v.tensor_scalar_mul(nmr[:], nmr[:], -1.0), "vec")

    stats_tok = ctx.tok("vec")
    for t in range(NT):
        @block.scalar
        def _(sc, t=t):
            if t == 0:
                sc.wait_ge(ctx.sget("vec"), stats_tok)
            else:
                ctx.wait(sc, "pe")
            xt = x_sb[:, t * D:(t + 1) * D]
            i = sc.activation(lnrow[:], xt, AF.Identity,
                              bias=nmr[:, t:t + 1], scale=rstd[:, t:t + 1])
            ctx.inc(i, "act")

        @block.tensor
        def _(tens, t=t):
            ctx.wait(tens, "act")
            if t == 0:
                ctx.wait(tens, "wdma")
            if t > 0:
                ctx.wait(tens, "vec")
            for half in range(2):
                for j4 in range(4):
                    j = half * 4 + j4
                    i = tens.matmul(ps_tr[:, 256 * half + 0:][:, 128 * j4:128 * j4 + 128]
                                    if False else
                                    ps_tr[:, 512 * half + 128 * j4: 512 * half + 128 * j4 + 128],
                                    lnrow[:, 128 * j:128 * (j + 1)],
                                    ctx.consts["IDENT"][:], start=True, stop=True)
            ctx.inc(i, "pe")

        @block.vector
        def _(v, t=t):
            ctx.wait(v, "pe")
            src = ps_tr[:].rearrange("p (j c) -> p j c", c=128)
            dst = out_T[:].rearrange("p (j s) -> p j s", s=S)[:, :, 128 * t:128 * t + 128]
            if gb_ap is None:
                i = v.tensor_copy(dst, src)
            else:
                g_t, b_t = gb_ap
                i = v.scalar_tensor_tensor(dst, src, g_t[:].rearrange("p j -> p j ()"),
                                           b_t[:].rearrange("p j -> p j ()"),
                                           ALU.mult, ALU.add)
            ctx.inc(i, "vec")
            if out_also is not None:
                ctx.wait(v, "vec")
                ctx.inc(v.tensor_copy(
                    out_also[:].rearrange("p (j s) -> p j s", s=S)[:, :, 128 * t:128 * t + 128],
                    src), "vec")


def load_weight_bf16(ctx, name, arr2d, tile_rows=128):
    R, C = arr2d.shape
    assert R % tile_rows == 0
    d = ctx.din(f"w_{name}_{ctx.nid()}", np.ascontiguousarray(arr2d, np.float32))
    sb = ctx.sbuf("ws_" + name, [tile_rows, (R // tile_rows) * C], BF16)
    block = ctx.block

    @block.gpsimd
    def _(g):
        src = d.rearrange("(j p) c -> p j c", p=tile_rows)
        dst = sb[:].rearrange("p (j c) -> p j c", j=R // tile_rows)
        ctx.inc(g.dma_start(out=dst, in_=src), "wdma", 16)
    return sb, ctx.tok("wdma")


def load_row_const(ctx, name, arr, dtype=F32, guards=(), scope=None):
    a = np.ascontiguousarray(arr, np.float32)
    if a.ndim == 1:
        a = a[None, :]
    d = ctx.din(f"r_{name}_{ctx.nid()}", a)
    sb = ctx.sbuf("rs_" + name, list(a.shape), dtype, scope)
    block = ctx.block

    @block.gpsimd
    def _(g):
        for nm in ("pe", "vec", "act"):
            if ctx.cnt.get(nm):
                ctx.wait(g, nm)
        ctx.inc(g.dma_start(out=sb[:], in_=d[:]), "wdma", 16)
    return sb, ctx.tok("wdma")




WSLOT_COLS = 2048


def wload(ctx, dram_ap, R, c0, c1, slot, slot_idx, pe_guard=None):
    import os as _os
    serial = bool(_os.environ.get("KSERIAL"))
    """DMA-cast fp32 dram weight rows[all] cols[c0:c1] into bf16 slot.
    Slot layout: row-tile j at cols [j*(c1-c0) : (j+1)*(c1-c0)].
    Each slot has its own completion semaphore ("wd0"/"wd1"); waits must use
    the slot's full cumulative count (DMA completions are unordered).
    pe_guard: wait for PE count before overwriting the slot.
    Returns (sem_name, token)."""
    block = ctx.block
    width = c1 - c0
    assert (R // 128) * width <= ctx.consts.get("WSLOT_COLS", WSLOT_COLS)
    semname = f"wd{slot_idx}"

    @block.gpsimd
    def _(g):
        if serial:
            for nm in ("pe", "vec", "act"):
                if ctx.cnt.get(nm):
                    ctx.wait(g, nm)
        elif pe_guard is not None:
            g.wait_ge(ctx.sget("pe"), pe_guard)
        src = dram_ap.rearrange("(j p) c -> p j c", p=128)[:, :, c0:c1]
        dst = slot[:].rearrange("p (j c) -> p j c", c=width)[:, 0:R // 128, :]
        ctx.inc(g.dma_start(out=dst, in_=src), semname, 16)
    return semname, ctx.tok(semname)


def gemm_T(ctx, dram_w, din, dout, actT, act_S, dstT, dst_S, ps,
           bias_tile=None, pre_wait=None, s_lo=0, s_hi=None, dst_off=0,
           act_off=0):
    block = ctx.block
    WS = ctx.consts["WSLOTS"]
    nj = din // 128
    w_cols = ctx.consts.get("WSLOT_COLS", WSLOT_COLS) // nj
    w_cols -= w_cols % 128
    s_hi = act_S if s_hi is None else s_hi
    nsc = (s_hi - s_lo) // 512
    for li, c0 in enumerate(range(0, dout, w_cols)):
        slot = WS[li % 2]
        wsem, tok = wload(ctx, dram_w, din, c0, c0 + w_cols, slot, li % 2,
                          pe_guard=ctx.tok("pe"))
        for jo_i in range(w_cols // 128):
            jo = c0 // 128 + jo_i
            for scn in range(nsc):
                s0 = s_lo + 512 * scn

                @block.tensor
                def _(t, jo_i=jo_i, jo=jo, s0=s0, slot=slot, tok=tok, wsem=wsem):
                    t.wait_ge(ctx.sget(wsem), tok)
                    if pre_wait is not None:
                        for nm, tv in pre_wait:
                            t.wait_ge(ctx.sget(nm), tv)
                    ctx.wait(t, "act")
                    ctx.wait(t, "vec")
                    ctx.wait(t, "wdma")
                    for j in range(nj):
                        i = t.matmul(ps,
                                     slot[:, j * w_cols + 128 * jo_i: j * w_cols + 128 * jo_i + 128],
                                     actT[:, j * act_S + act_off + s0: j * act_S + act_off + s0 + 512],
                                     start=(j == 0), stop=(j == nj - 1))
                    ctx.inc(i, "pe")

                @block.scalar
                def _(sc, jo=jo, s0=s0):
                    ctx.wait(sc, "pe")
                    o0 = jo * dst_S + dst_off + (s0 - s_lo)
                    if bias_tile is not None:
                        i = sc.activation(dstT[:, o0: o0 + 512],
                                          ps, AF.Identity, bias=bias_tile[:, jo:jo + 1])
                    else:
                        i = sc.activation(dstT[:, o0: o0 + 512],
                                          ps, AF.Identity)
                    ctx.inc(i, "act")


def gemm_N(ctx, dram_w, din, dout, actT, act_S, ps, evac,
           bias_row=None, pre_wait=None, evac_chan="vec"):
    block = ctx.block
    WS = ctx.consts["WSLOTS"]
    nj = din // 128
    w_cols = min(512, ctx.consts.get("WSLOT_COLS", WSLOT_COLS) // nj)
    ones_sb = ctx.consts["ONES"]
    for li, c0 in enumerate(range(0, dout, w_cols)):
        slot = WS[li % 2]
        wsem, tok = wload(ctx, dram_w, din, c0, c0 + w_cols, slot, li % 2,
                          pe_guard=ctx.tok("pe"))
        for t_i in range(NT):
            @block.tensor
            def _(t, t_i=t_i, c0=c0, slot=slot, tok=tok, wsem=wsem):
                t.wait_ge(ctx.sget(wsem), tok)
                if pre_wait is not None:
                    for nm, tv in pre_wait:
                        t.wait_ge(ctx.sget(nm), tv)
                ctx.wait(t, evac_chan)
                ctx.wait(t, "vec")
                ctx.wait(t, "wdma")
                for j in range(nj):
                    i = t.matmul(ps[:, 0:w_cols],
                                 actT[:, j * act_S + 128 * t_i: j * act_S + 128 * t_i + 128],
                                 slot[:, j * w_cols: (j + 1) * w_cols],
                                 start=(j == 0), stop=(j == nj - 1 and bias_row is None))
                if bias_row is not None:
                    i = t.matmul(ps[:, 0:w_cols], ones_sb[0:1, 0:128],
                                 bias_row[0:1, c0:c0 + w_cols],
                                 start=False, stop=True)
                ctx.inc(i, "pe")

            if evac_chan == "vec":
                @block.vector
                def _(v, t_i=t_i, c0=c0):
                    ctx.wait(v, "pe")
                    ctx.inc(evac(v, ps[:, 0:w_cols], t_i, c0, w_cols), "vec")
            else:
                @block.scalar
                def _(sc, t_i=t_i, c0=c0):
                    ctx.wait(sc, "pe")
                    ctx.inc(evac(sc, ps[:, 0:w_cols], t_i, c0, w_cols), "act")



def emit_l0aa(ctx, x_sb, w, PS_A, PS_B):
    block = ctx.block
    H, P = 16, 64

    Wq = np.transpose(w["L0_attn_a_q_w"], (2, 0, 1)).reshape(D, H * P)
    Wk = np.transpose(w["L0_attn_a_k_w"], (2, 0, 1)).reshape(D, H * P)
    Wv = np.transpose(w["L0_attn_a_v_w"], (2, 0, 1)).reshape(D, H * P)
    Wl = np.ascontiguousarray(np.asarray(w["L0_attn_a_l_w"]).T)
    bq = np.asarray(w["L0_attn_a_q_b"])[:, 0, :].reshape(H * P)
    bk = np.asarray(w["L0_attn_a_k_b"])[:, 0, :].reshape(H * P)
    bv = np.asarray(w["L0_attn_a_v_b"])[:, 0, :].reshape(H * P)
    bl = np.asarray(w["L0_attn_a_l_b"]).reshape(D)

    bq_sb, _ = load_row_const(ctx, "l0aa_bq", bq.reshape(8, 128).T)
    bk_sb, _ = load_row_const(ctx, "l0aa_bk", bk.reshape(8, 128).T)
    QT = ctx.sbuf("aa_QT", [128, DT * S], BF16)
    KT = ctx.sbuf("aa_KT", [128, DT * S], BF16)
    V = ctx.sbuf("aa_V", [128, NT * (H * P)], BF16)
    Eb = ctx.sbuf("aa_E", [128, S], BF16)
    ln_scope = ExitStack()
    lnT = ctx.sbuf("aa_lnT", [128, DT * S], BF16, ln_scope)
    bv_sb = ctx.sbuf("aa_bv", [1, D], BF16, ln_scope)
    dbv = ctx.din("r_l0aa_bv", bv[None, :].astype(np.float32))

    @block.gpsimd
    def _(g):
        for nm in ("pe", "vec", "act"):
            if ctx.cnt.get(nm):
                ctx.wait(g, nm)
        ctx.inc(g.dma_start(out=bv_sb[:], in_=dbv[:]), "wdma", 16)

    emit_ln_transpose(ctx, x_sb, lnT, PS_A[:, 0:1024], scope=ln_scope,
                      lnrow_buf=Eb[:, 0:D])

    dq = ctx.din("w_l0aa_q", Wq)
    dk = ctx.din("w_l0aa_k", Wk)
    dv = ctx.din("w_l0aa_v", Wv)
    dl = ctx.din("w_l0aa_l", Wl)

    ps_q = PS_A[:, 0:512]
    ps_v = PS_A[:, 1024:1536]

    gemm_T(ctx, dq, D, D, lnT, S, QT, S, ps_q, bias_tile=bq_sb)
    gemm_T(ctx, dk, D, D, lnT, S, KT, S, ps_q, bias_tile=bk_sb)

    def v_evac(v, psap, t_i, c0, w):
        return v.tensor_copy(V[:, t_i * 1024 + c0: t_i * 1024 + c0 + w], psap)
    gemm_N(ctx, dv, D, D, lnT, S, ps_v, v_evac, bias_row=bv_sb)

    ln_scope.close()
    bl_sb, _ = load_row_const(ctx, "l0aa_bl", bl[None, :], BF16,
                              guards=("pe", "vec", "act"))

    ps_sc = PS_A
    ps_cx = PS_B[0:64, :]
    negm = ctx.sbuf("aa_negm", [128, 1], F32)
    sums = ctx.sbuf("aa_sums", [128, 1], F32)
    rsum = ctx.sbuf("aa_rsum", [128, 1], F32)
    Vh = ctx.sbuf("aa_Vh", [128, P], BF16)

    proj_act_tok = ctx.tok("act")
    proj_vec_tok = ctx.tok("vec")

    for h in range(H):
        jt = h // 2
        ro = 64 * (h % 2)
        for kt in range(NT):
            @block.tensor
            def _(t, h=h, kt=kt, jt=jt, ro=ro):
                if h == 0 and kt == 0:
                    t.wait_ge(ctx.sget("act"), proj_act_tok)
                    t.wait_ge(ctx.sget("vec"), proj_vec_tok)
                else:
                    ctx.wait(t, "act")
                for qc in range(4):
                    i = t.matmul(ps_sc[:, 512 * qc:512 * qc + 512],
                                 KT[ro:ro + 64, jt * S + 128 * kt: jt * S + 128 * kt + 128],
                                 QT[ro:ro + 64, jt * S + 512 * qc: jt * S + 512 * qc + 512],
                                 start=True, stop=True)
                    ctx.inc(i, "pe")

            @block.vector
            def _(v, h=h, kt=kt):
                ctx.wait(v, "pe")
                i = v.reduce_max(negm[:], ps_sc[:], axis=AX.X, negate=True)
                ctx.inc(i, "vec")
                ctx.wait(v, "vec")
                ctx.inc(v.tensor_scalar_mul(negm[:], negm[:], 0.125), "vec")

            @block.scalar
            def _(sc, h=h, kt=kt):
                ctx.wait(sc, "vec")
                ctx.wait(sc, "pe")
                i = sc.activation(Eb[:], ps_sc[:], AF.Exp,
                                  bias=negm[:], scale=0.125, accum_out=sums[:])
                ctx.inc(i, "act")

            @block.vector
            def _(v, h=h, kt=kt):
                ctx.wait(v, "act")
                ctx.inc(v.reciprocal(rsum[:], sums[:]), "vec")
                ctx.wait(v, "vec")
                i = v.tensor_scalar(Vh[:], V[:, kt * 1024 + 64 * h: kt * 1024 + 64 * h + 64],
                                    rsum[:], None, ALU.mult)
                ctx.inc(i, "vec")

            @block.tensor
            def _(t, h=h, kt=kt):
                ctx.wait(t, "vec")
                for qc in range(4):
                    i = t.matmul(ps_cx[:, 512 * qc:512 * qc + 512],
                                 Vh[:], Eb[:, 512 * qc:512 * qc + 512],
                                 start=(kt == 0), stop=(kt == NT - 1))
                ctx.inc(i, "pe")

        @block.scalar
        def _(sc, h=h, jt=jt, ro=ro):
            ctx.wait(sc, "pe")
            i = sc.activation(KT[ro:ro + 64, jt * S: jt * S + S], ps_cx, AF.Identity)
            ctx.inc(i, "act")

    attn_act_tok = ctx.tok("act")
    ps_o = PS_B[:, 0:512]

    def o_evac(v, psap, t_i, c0, w):
        dst = x_sb[:, t_i * D + c0: t_i * D + c0 + w]
        return v.tensor_add(dst, dst, psap)
    gemm_N(ctx, dl, D, D, KT, S, ps_o, o_evac, bias_row=bl_sb,
           pre_wait=[("act", attn_act_tok)])





def emit_l0ab(ctx, x_sb, w, PS_A, PS_B):
    block = ctx.block
    H, DH = 16, 64

    w_q = np.asarray(w["L0_attn_b_w_q"], np.float32)
    w_kv = np.asarray(w["L0_attn_b_w_kv"], np.float32)
    wk = np.ascontiguousarray(w_kv[:, 0:D])
    wv = np.ascontiguousarray(w_kv[:, D:2 * D])
    w_o = np.asarray(w["L0_attn_b_w_out"], np.float32)
    b_o = np.asarray(w["L0_attn_b_b_out"]).reshape(D)
    scale_in = np.asarray(w["L0_attn_b_scale"]).reshape(H, 1)

    dq = ctx.din("w_l0ab_q", w_q)
    dk = ctx.din("w_l0ab_k", wk)
    dv = ctx.din("w_l0ab_v", wv)
    do = ctx.din("w_l0ab_o", w_o)

    sel = np.zeros((128, 8 * 16), np.float32)
    selT = np.zeros((16, 8 * 128), np.float32)
    for j in range(8):
        for p in range(128):
            h = 2 * j + (1 if p >= 64 else 0)
            sel[p, j * 16 + h] = 1.0
            selT[h, j * 128 + p] = 1.0
    sel_sb, _ = load_row_const(ctx, "l0ab_sel", sel, BF16)
    selT_sb, _ = load_row_const(ctx, "l0ab_selT", selT, F32)
    esc_d = ctx.din("r_l0ab_scale", scale_in)
    esc = ctx.sbuf("ab_esc", [H, 1], F32)

    @block.gpsimd
    def _(g):
        for nm in ("pe", "vec", "act"):
            if ctx.cnt.get(nm):
                ctx.wait(g, nm)
        ctx.inc(g.dma_start(out=esc[:], in_=esc_d[:]), "wdma", 16)

    @block.scalar
    def _(sc):
        ctx.wait(sc, "wdma")
        ctx.inc(sc.activation(esc[:], esc[:], AF.Exp), "act")

    QT = ctx.sbuf("ab_QT", [128, DT * S], BF16)
    KT = ctx.sbuf("ab_KT", [128, DT * S], BF16)
    V = ctx.sbuf("ab_V", [128, NT * D], BF16)
    ln_scope = ExitStack()
    lnT = ctx.sbuf("ab_lnT", [128, DT * S], BF16, ln_scope)
    emit_ln_transpose(ctx, x_sb, lnT, PS_A[:, 0:1024], scope=ln_scope,
                      lnrow_buf=QT[:, 0:D])

    gemm_T(ctx, dq, D, D, lnT, S, QT, S, PS_A[:, 0:512])
    gemm_T(ctx, dk, D, D, lnT, S, KT, S, PS_A[:, 512:1024])

    def v_evac(v, psap, t_i, c0, wd):
        return v.tensor_copy(V[:, t_i * D + c0: t_i * D + c0 + wd], psap)
    gemm_N(ctx, dv, D, D, lnT, S, PS_A[:, 1024:1536], v_evac)
    ln_scope.close()

    sqb = ctx.sbuf("ab_sq", [128, 512], BF16)
    nrm = ctx.sbuf("ab_nrm", [16, 512], F32)
    ps_ss = PS_B[0:16, 0:512]
    ps_rb = PS_B[:, 512:1024]
    for which, src in (("q", QT), ("k", KT)):
        for scn in range(4):
            for j in range(DT):
                @block.scalar
                def _(sc, j=j, scn=scn, src=src):
                    ctx.wait(sc, "act")
                    ctx.wait(sc, "vec")
                    ctx.wait(sc, "pe")
                    i = sc.activation(sqb[:], src[:, j * S + 512 * scn: j * S + 512 * scn + 512],
                                      AF.Square)
                    ctx.inc(i, "act")

                @block.tensor
                def _(t, j=j, scn=scn):
                    ctx.wait(t, "act")
                    if j == 0:
                        ctx.wait(t, "vec")
                    i = t.matmul(ps_ss, sel_sb[:, j * 16:(j + 1) * 16], sqb[:],
                                 start=(j == 0), stop=(j == DT - 1))
                    ctx.inc(i, "pe")

            @block.scalar
            def _(sc, which=which, scn=scn):
                ctx.wait(sc, "pe")
                ctx.inc(sc.activation(nrm[:], ps_ss, AF.Sqrt), "act")

            @block.vector
            def _(v, which=which, scn=scn):
                ctx.wait(v, "act")
                ctx.inc(v.tensor_scalar_max(nrm[:], nrm[:], 1e-12), "vec")
                ctx.wait(v, "vec")
                ctx.inc(v.reciprocal(nrm[:], nrm[:]), "vec")
                if which == "q":
                    ctx.wait(v, "vec")
                    ctx.inc(v.tensor_scalar(nrm[:], nrm[:], esc[:], None, ALU.mult), "vec")

            for j in range(DT):
                @block.tensor
                def _(t, j=j, scn=scn):
                    ctx.wait(t, "vec")
                    i = t.matmul(ps_rb, selT_sb[0:16, j * 128:(j + 1) * 128], nrm[:],
                                 start=True, stop=True)
                    ctx.inc(i, "pe")

                @block.vector
                def _(v, j=j, scn=scn, src=src):
                    ctx.wait(v, "pe")
                    sl = src[:, j * S + 512 * scn: j * S + 512 * scn + 512]
                    ctx.inc(v.tensor_mul(sl, sl, ps_rb), "vec")

    QA = ctx.sbuf("ab_QA", [66, 2 * S], BF16)
    KA = ctx.sbuf("ab_KA", [66, 2 * S], BF16)
    ET = ctx.sbuf("ab_ET", [128, 2 * S], BF16)
    scr = ctx.sbuf("ab_scr", [128, S], BF16)
    negm = ctx.sbuf("ab_negm", [128, NT], F32)
    sums = ctx.sbuf("ab_sums", [128, NT], F32)
    mls = ctx.sbuf("ab_mls", [128, NT], F32)
    mhb = ctx.sbuf("ab_mhb", [128, NT], BF16)
    mhl = ctx.sbuf("ab_mhl", [128, 2 * NT], F32)
    norm_vec_tok = ctx.tok("vec")

    @block.vector
    def _(v):
        ctx.wait(v, "vec")
        ctx.inc(v.memset(KA[64:66, :], -1.0), "vec")

    for pair in range(8):
        for hh in range(2):
            h = 2 * pair + hh
            jt = h // 2
            ro = 64 * (h % 2)
            for qt in range(NT):
                @block.tensor
                def _(t, qt=qt, jt=jt, ro=ro, pair=pair, hh=hh):
                    if pair == 0 and hh == 0 and qt == 0:
                        t.wait_ge(ctx.sget("vec"), norm_vec_tok)
                    ctx.wait(t, "act")
                    for kc in range(4):
                        i = t.matmul(PS_A[:, 512 * kc:512 * kc + 512],
                                     QT[ro:ro + 64, jt * S + 128 * qt: jt * S + 128 * qt + 128],
                                     KT[ro:ro + 64, jt * S + 512 * kc: jt * S + 512 * kc + 512],
                                     start=True, stop=True)
                    ctx.inc(i, "pe")

                @block.vector
                def _(v, qt=qt):
                    ctx.wait(v, "pe")
                    ctx.inc(v.reduce_max(negm[:, qt:qt + 1], PS_A[:], axis=AX.X,
                                         negate=True), "vec")

                @block.scalar
                def _(sc, qt=qt):
                    ctx.wait(sc, "vec")
                    ctx.wait(sc, "act")
                    i = sc.activation(scr[:], PS_A[:], AF.Exp,
                                      bias=negm[:, qt:qt + 1],
                                      accum_out=sums[:, qt:qt + 1])
                    ctx.inc(i, "act")

            @block.scalar
            def _(sc):
                ctx.wait(sc, "act")
                ctx.inc(sc.activation(mls[:], sums[:], AF.Ln), "act")

            @block.vector
            def _(v):
                ctx.wait(v, "act")
                ctx.inc(v.tensor_sub(mls[:], mls[:], negm[:]), "vec")
                ctx.wait(v, "vec")
                ctx.inc(v.tensor_copy(mhb[:], mls[:]), "vec")
                ctx.wait(v, "vec")
                ctx.inc(v.tensor_copy(mhl[:].rearrange("p (t two) -> p two t", two=2)[:, 0, :],
                                      mhb[:]), "vec")
                ctx.wait(v, "vec")
                ctx.inc(v.tensor_sub(mhl[:].rearrange("p (t two) -> p two t", two=2)[:, 1, :],
                                     mls[:],
                                     mhl[:].rearrange("p (t two) -> p two t", two=2)[:, 0, :]),
                        "vec")

            @block.vector
            def _(v, hh=hh, jt=jt, ro=ro):
                ctx.wait(v, "vec")
                ctx.inc(v.tensor_copy(QA[0:64, hh * S:(hh + 1) * S],
                                      QT[ro:ro + 64, jt * S: jt * S + S]), "vec")
                ctx.wait(v, "vec")
                ctx.inc(v.tensor_copy(KA[0:64, hh * S:(hh + 1) * S],
                                      KT[ro:ro + 64, jt * S: jt * S + S]), "vec")

            for qt in range(NT):
                @block.tensor
                def _(t, qt=qt):
                    ctx.wait(t, "vec")
                    i = t.matmul(PS_B[0:2, 0:128], mhl[:, 2 * qt:2 * qt + 2],
                                 ctx.consts["IDENTF"][:], start=True, stop=True)
                    ctx.inc(i, "pe")

                @block.vector
                def _(v, qt=qt, hh=hh):
                    ctx.wait(v, "pe")
                    dst = QA[64:66, hh * S + 128 * qt: hh * S + 128 * qt + 128]
                    ctx.inc(v.tensor_copy(dst, PS_B[0:2, 0:128]), "vec")

        pair_vec_tok = ctx.tok("vec")
        for kt in range(NT):
            for hh in range(2):
                h = 2 * pair + hh

                @block.tensor
                def _(t, kt=kt, hh=hh, pair_vec_tok=pair_vec_tok):
                    t.wait_ge(ctx.sget("vec"), pair_vec_tok)
                    ctx.wait(t, "act")
                    for qc in range(4):
                        i = t.matmul(PS_A[:, 512 * qc:512 * qc + 512],
                                     KA[0:66, hh * S + 128 * kt: hh * S + 128 * kt + 128],
                                     QA[0:66, hh * S + 512 * qc: hh * S + 512 * qc + 512],
                                     start=True, stop=True)
                    ctx.inc(i, "pe")

                @block.scalar
                def _(sc, kt=kt, hh=hh):
                    ctx.wait(sc, "pe")
                    ctx.wait(sc, "act")
                    i = sc.activation(ET[:, hh * S:(hh + 1) * S], PS_A[:], AF.Exp)
                    ctx.inc(i, "act")

                @block.tensor
                def _(t, kt=kt, hh=hh, h=h):
                    ctx.wait(t, "act")
                    for qc in range(4):
                        i = t.matmul(PS_B[64 * hh:64 * hh + 64, 512 * qc:512 * qc + 512],
                                     V[:, kt * D + 64 * h: kt * D + 64 * h + 64],
                                     ET[:, hh * S + 512 * qc: hh * S + 512 * qc + 512],
                                     start=(kt == 0), stop=(kt == NT - 1),
                                     tile_position=(0, 64 * hh),
                                     skip_group_check=True)
                    ctx.inc(i, "pe")

        @block.scalar
        def _(sc, pair=pair):
            ctx.wait(sc, "pe")
            i = sc.activation(QT[:, pair * S:(pair + 1) * S], PS_B[:, 0:S], AF.Identity)
            ctx.inc(i, "act")

    bo_sb, _ = load_row_const(ctx, "l0ab_bo", b_o[None, :], BF16,
                              guards=("pe", "vec", "act"))
    att_act_tok = ctx.tok("act")

    def o_evac(v, psap, t_i, c0, wd):
        dst = x_sb[:, t_i * D + c0: t_i * D + c0 + wd]
        return v.tensor_add(dst, dst, psap)
    gemm_N(ctx, do, D, D, QT, S, PS_A[:, 0:512], o_evac, bias_row=bo_sb,
           pre_wait=[("act", att_act_tok)])





def emit_l1aa(ctx, x_sb, w, PS_A, PS_B):
    block = ctx.block
    H, DH, RD = 16, 64, 32
    EPS = 1e-6
    SCL = float(1.0 / np.sqrt(np.float32(96.0)))

    W_kv = np.asarray(w["L1_attn_a_W_kv"], np.float32)
    g_kv = np.asarray(w["L1_attn_a_g_kv"], np.float32)
    W_kdec = np.asarray(w["L1_attn_a_W_kdec"], np.float32)
    W_vdec = np.asarray(w["L1_attn_a_W_vdec"], np.float32)
    W_q = np.asarray(w["L1_attn_a_W_q"], np.float32)
    W_qdec = np.asarray(w["L1_attn_a_W_qdec"], np.float32)
    W_krope = np.asarray(w["L1_attn_a_W_krope"], np.float32)
    W_qrope = np.asarray(w["L1_attn_a_W_qrope"], np.float32)
    W_o = np.asarray(w["L1_attn_a_W_o"], np.float32)

    def perm_rope(W):
        Wr = np.zeros_like(W)
        for h in range(H):
            blk = W[:, 32 * h:32 * h + 32]
            Wr[:, 32 * h:32 * h + 16] = blk[:, 16:32]
            Wr[:, 32 * h + 16:32 * h + 32] = blk[:, 0:16]
        return np.ascontiguousarray(Wr)

    d_kv = ctx.din("w_l1aa_kv", W_kv)
    d_kdec = ctx.din("w_l1aa_kdec", W_kdec)
    d_vdec = ctx.din("w_l1aa_vdec", W_vdec)
    d_q = ctx.din("w_l1aa_q", W_q)
    d_qdec = ctx.din("w_l1aa_qdec", W_qdec)
    d_krp = ctx.din("w_l1aa_krp", W_krope)
    d_krr = ctx.din("w_l1aa_krr", perm_rope(W_krope))
    d_qrp = ctx.din("w_l1aa_qrp", W_qrope)
    d_qrr = ctx.din("w_l1aa_qrr", perm_rope(W_qrope))
    d_o = ctx.din("w_l1aa_o", W_o)

    inv_freq = 1.0 / 10000.0 ** (np.arange(0, RD, 2, dtype=np.float64) / RD)
    tpos = np.arange(S, dtype=np.float64)
    fr = np.outer(tpos, inv_freq)
    emb = np.concatenate([fr, fr], -1)
    cos32 = np.cos(emb).T
    sin32 = np.sin(emb).T
    cosT = np.zeros((128, S), np.float32)
    sinT = np.zeros((128, S), np.float32)
    for r in range(128):
        d = r % 32
        cosT[r] = cos32[d]
        sinT[r] = sin32[d] * (-1.0 if d < 16 else 1.0)
    cos_sb, _ = load_row_const(ctx, "l1aa_cos", cosT, BF16)
    sin_sb, _ = load_row_const(ctx, "l1aa_sin", sinT, BF16)
    gk_tile, _ = load_row_const(ctx, "l1aa_gkv", g_kv.reshape(2, 128).T)
    onescolb, _ = load_row_const(ctx, "l1aa_onescol", np.ones((128, 1), np.float32), BF16)
    maskTs = []
    for r in range(4):
        m = np.zeros((128, 512), np.float32)
        for qb in range(4):
            if qb < r:
                m[:, qb * 128:(qb + 1) * 128] = -1e30
            elif qb == r:
                for p in range(128):
                    m[p, qb * 128: qb * 128 + p] = -1e30
        maskTs.append(m)
    maskT_sb, _ = load_row_const(ctx, "l1aa_maskT", np.concatenate(maskTs, 1), BF16)
    maskD0 = np.zeros((128, 128), np.float32)
    for p in range(128):
        maskD0[p, p + 1:] = -1e30
    maskD_sb, _ = load_row_const(ctx, "l1aa_maskD", maskD0, BF16)
    eps_sb, _ = load_row_const(ctx, "l1aa_eps", np.array([[EPS]], np.float32))
    onesrowf, _ = load_row_const(ctx, "l1aa_onesrowf", np.ones((1, 128), np.float32), F32)

    QC = ctx.sbuf("ma_QC", [128, 4 * S], BF16)
    KVT = ctx.sbuf("ma_KVT", [128, 2 * S], BF16)
    AOT = ctx.sbuf("ma_AOT", [128, DT * S], BF16)
    KRF = ctx.sbuf("ma_KRF", [128, 4 * S], BF16)
    scr512 = ctx.sbuf("ma_scr", [128, 512], BF16)
    ph1 = ExitStack()
    lnT = ctx.sbuf("ma_lnT", [128, DT * S], BF16, ph1)
    KRR = AOT[:, 4 * S:8 * S]
    rms = ctx.sbuf("ma_rms", [1, S], F32, ph1)
    import os as _os9
    if _os9.environ.get("KLNZ"):
        @block.vector
        def _(v):
            ctx.wait(v, "xin")
            ctx.inc(v.memset(lnT[:], 0.0), "vec")
            ctx.wait(v, "vec")
            ctx.inc(v.memset(QC[:], 0.0), "vec")
            ctx.wait(v, "vec")
            ctx.inc(v.memset(KVT[:], 0.0), "vec")
    else:
        emit_ln_transpose(ctx, x_sb, lnT, PS_A[:, 0:1024], lnrow_buf=AOT[:, 0:D],
                          scope=ph1)

    _sk2 = set((_os9.environ.get("KSKIP2") or "").split(","))
    if "kv" not in _sk2:
        gemm_T(ctx, d_kv, D, 256, lnT, S, KVT, S, PS_A[:, 0:512])
    if "q" not in _sk2:
        gemm_T(ctx, d_q, D, 512, lnT, S, QC, S, PS_A[:, 512:1024])
    if "krg" not in _sk2:
        gemm_T(ctx, d_krp, D, 512, lnT, S, KRF, S, PS_A[:, 0:512])
        gemm_T(ctx, d_krr, D, 512, lnT, S, KRR, S, PS_A[:, 512:1024])
    for j in (() if "krft" in _sk2 else range(4)):
        for scn in range(4):
            @block.vector
            def _(v, j=j, scn=scn):
                ctx.wait(v, "act")
                ctx.wait(v, "vec")
                sl = slice(j * S + 512 * scn, j * S + 512 * scn + 512)
                ssl = slice(512 * scn, 512 * scn + 512)
                i = v.tensor_mul(scr512[:], KRF[:, sl], cos_sb[:, ssl])
                ctx.inc(i, "vec")
                ctx.wait(v, "vec")
                i = v.tensor_mul(KRR[:, sl], KRR[:, sl], sin_sb[:, ssl])
                ctx.inc(i, "vec")
                ctx.wait(v, "vec")
                i = v.tensor_add(KRF[:, sl], scr512[:], KRR[:, sl])
                ctx.inc(i, "vec")

    if "rms" in _sk2:
        _rms_skipped = True
    for scn in (() if "rms" in _sk2 else range(4)):
        for j in range(2):
            @block.scalar
            def _(sc, j=j, scn=scn):
                ctx.wait(sc, "act")
                ctx.wait(sc, "vec")
                ctx.wait(sc, "pe")
                i = sc.activation(scr512[:], KVT[:, j * S + 512 * scn: j * S + 512 * scn + 512],
                                  AF.Square)
                ctx.inc(i, "act")

            @block.tensor
            def _(t, j=j, scn=scn):
                ctx.wait(t, "act")
                i = t.matmul(PS_B[0:1, 0:512], onescolb[:], scr512[:],
                             start=(j == 0), stop=(j == 1))
                ctx.inc(i, "pe")

        @block.scalar
        def _(sc, scn=scn):
            ctx.wait(sc, "pe")
            i = sc.activation(rms[0:1, 512 * scn:512 * scn + 512], PS_B[0:1, 0:512],
                              AF.Sqrt, scale=1.0 / 256.0, bias=eps_sb[0:1, 0:1])
            ctx.inc(i, "act")

        @block.vector
        def _(v, scn=scn):
            ctx.wait(v, "act")
            sl = rms[0:1, 512 * scn:512 * scn + 512]
            ctx.inc(v.reciprocal(sl, sl), "vec")

    sq_act_tok = ctx.tok("act")
    for j in (() if "rms" in _sk2 else range(2)):
        @block.vector
        def _(v, j=j):
            v.wait_ge(ctx.sget("act"), sq_act_tok)
            ctx.wait(v, "vec")
            i = v.tensor_scalar(KVT[:, j * S:(j + 1) * S], KVT[:, j * S:(j + 1) * S],
                                gk_tile[:, j:j + 1], None, ALU.mult)
            ctx.inc(i, "vec")
    for j in range(2):
        for scn in range(4):
            @block.tensor
            def _(t, j=j, scn=scn):
                ctx.wait(t, "vec")
                i = t.matmul(PS_B[:, 512:1024], onesrowf[0:1, :],
                             rms[0:1, 512 * scn:512 * scn + 512],
                             start=True, stop=True)
                ctx.inc(i, "pe")

            @block.vector
            def _(v, j=j, scn=scn):
                ctx.wait(v, "pe")
                sl = KVT[:, j * S + 512 * scn: j * S + 512 * scn + 512]
                ctx.inc(v.tensor_mul(sl, sl, PS_B[:, 512:1024]), "vec")

    if ctx.consts.get("KTR") == "A":
        ph1.close()
        return
    ph1.close()
    QCp = ctx.sbuf("ma_QCp", [128, S], BF16)
    KCp = ctx.sbuf("ma_KCp", [128, S], BF16)
    QRp = ctx.sbuf("ma_QRp", [64, S], BF16)
    QRr = ctx.sbuf("ma_QRr", [64, S], BF16)
    KRp = ctx.sbuf("ma_KRp", [64, S], BF16)
    Vp = ctx.sbuf("ma_Vp", [128, S], BF16)
    KRX = ctx.sbuf("ma_KRX", [128, 2 * S], BF16)
    QRX = ctx.sbuf("ma_QRX", [128, 2 * S], BF16)
    ET = ctx.sbuf("ma_ET", [128, 2 * S], BF16)
    scrE = ET[:, 0:S]
    negm = ctx.sbuf("ma_negm", [128, NT], F32)
    sums = ctx.sbuf("ma_sums", [128, NT], F32)
    mls = ctx.sbuf("ma_mls", [128, NT], F32)
    mhb = ctx.sbuf("ma_mhb", [128, NT], BF16)
    mhl = ctx.sbuf("ma_mhl", [128, 2 * NT], F32)

    @block.vector
    def _(v):
        ctx.wait(v, "vec")
        ctx.inc(v.memset(KRX[32:64, :], 0.0), "vec")
        ctx.wait(v, "vec")
        ctx.inc(v.memset(KRX[32:34, :], -1.0), "vec")
        ctx.wait(v, "vec")
        ctx.inc(v.memset(QRX[32:64, :], 0.0), "vec")

    def pair_gemm(dram, din, actT, c0, dst, dst_c0, ncols, ps):
        nj = din // 128
        slot_i = pair_gemm.flip = (getattr(pair_gemm, "flip", 0) + 1) % 2
        slot = ctx.consts["WSLOTS"][slot_i]
        wide = 2 * ncols
        cb = (c0 // wide) * wide
        off = c0 - cb
        wsem, tok = wload(ctx, dram, din, cb, cb + wide, slot, slot_i,
                          pe_guard=ctx.tok("pe"))
        for scn in range(4):
            @block.tensor
            def _(t, scn=scn, slot=slot, tok=tok, wsem=wsem, nj=nj, ncols=ncols,
                  off=off, wide=wide):
                t.wait_ge(ctx.sget(wsem), tok)
                ctx.wait(t, "act")
                ctx.wait(t, "vec")
                for j in range(nj):
                    i = t.matmul(ps[0:ncols, 0:512],
                                 slot[:, j * wide + off: j * wide + off + ncols],
                                 actT[:, j * S + 512 * scn: j * S + 512 * scn + 512],
                                 start=(j == 0), stop=(j == nj - 1))
                ctx.inc(i, "pe")

            @block.scalar
            def _(sc, scn=scn, ncols=ncols):
                ctx.wait(sc, "pe")
                i = sc.activation(dst[dst_c0:dst_c0 + ncols, 512 * scn:512 * scn + 512],
                                  ps[0:ncols, 0:512], AF.Identity)
                ctx.inc(i, "act")

    def pair_gemm_n(dram, din, actT, c0, ps):
        nj = din // 128
        slot_i = pair_gemm.flip = (getattr(pair_gemm, "flip", 0) + 1) % 2
        slot = ctx.consts["WSLOTS"][slot_i]
        cb = (c0 // 256) * 256
        off = c0 - cb
        wsem, tok = wload(ctx, dram, din, cb, cb + 256, slot, slot_i,
                          pe_guard=ctx.tok("pe"))
        for t_i in range(NT):
            @block.tensor
            def _(t, t_i=t_i, slot=slot, tok=tok, wsem=wsem, nj=nj, off=off):
                t.wait_ge(ctx.sget(wsem), tok)
                ctx.wait(t, "act")
                ctx.wait(t, "vec")
                for j in range(nj):
                    i = t.matmul(ps[:, 0:128],
                                 actT[:, j * S + 128 * t_i: j * S + 128 * t_i + 128],
                                 slot[:, j * 256 + off: j * 256 + off + 128],
                                 start=(j == 0), stop=(j == nj - 1))
                ctx.inc(i, "pe")

            @block.vector
            def _(v, t_i=t_i):
                ctx.wait(v, "pe")
                ctx.inc(v.tensor_copy(Vp[:, t_i * 128: t_i * 128 + 128], ps[:, 0:128]),
                        "vec")

    def rope_apply(dst, rot, n_rows):
        for scn in range(4):
            @block.vector
            def _(v, scn=scn):
                ctx.wait(v, "act")
                ctx.wait(v, "vec")
                sl = slice(512 * scn, 512 * scn + 512)
                i = v.tensor_mul(scr512[0:n_rows, 0:512], dst[:, sl], cos_sb[0:n_rows, sl])
                ctx.inc(i, "vec")
                ctx.wait(v, "vec")
                i = v.tensor_mul(rot[:, sl], rot[:, sl], sin_sb[0:n_rows, sl])
                ctx.inc(i, "vec")
                ctx.wait(v, "vec")
                i = v.tensor_add(dst[:, sl], scr512[0:n_rows, 0:512], rot[:, sl])
                ctx.inc(i, "vec")

    import os as _os
    _skips = set((_os.environ.get("KSKIP") or "").split(","))
    _np_ = int(_os.environ.get("KNPAIR") or 8)
    for pair in range(_np_):
        pb = PS_A[:, 0:512] if pair % 2 == 0 else PS_A[:, 512:1024]
        if "qk" not in _skips:
            pair_gemm(d_qdec, 512, QC, 128 * pair, QCp, 0, 128, pb)
            if "kd" not in _skips:
                pair_gemm(d_kdec, 256, KVT, 128 * pair, KCp, 0, 128, pb)
        if "qr" not in _skips:
            pair_gemm(d_qrp, 512, QC, 64 * pair, QRp, 0, 64, PS_A[:, 512:1024])
            pair_gemm(d_qrr, 512, QC, 64 * pair, QRr, 0, 64, PS_A[:, 512:1024])
            if "rope" not in _skips:
                rope_apply(QRp, QRr, 64)
        if "v" not in _skips:
            pair_gemm_n(d_vdec, 256, KVT, 128 * pair, PS_A[:, 1024:1536])

        @block.vector
        def _(v, pair=pair):
            import os as _os
            if _os.environ.get("KNOKRCOPY"):
                return
            ctx.wait(v, "vec")
            ctx.wait(v, "pe")
            jsrc = pair // 2
            r0 = 64 * (pair % 2)
            i = v.tensor_copy(KRp[:, :], KRF[r0:r0 + 64, jsrc * S:(jsrc + 1) * S])
            ctx.inc(i, "vec")
            ctx.wait(v, "act")
            for hh in range(2):
                ctx.wait(v, "vec")
                i = v.tensor_copy(KRX[0:32, hh * S:(hh + 1) * S],
                                  KRp[32 * hh:32 * hh + 32, :])
                ctx.inc(i, "vec")
                ctx.wait(v, "vec")
                i = v.tensor_copy(QRX[0:32, hh * S:(hh + 1) * S],
                                  QRp[32 * hh:32 * hh + 32, :])
                ctx.inc(i, "vec")
                ctx.wait(v, "vec")
                i = v.tensor_copy(KRX[64:128, hh * S:(hh + 1) * S],
                                  KCp[64 * hh:64 * hh + 64, :])
                ctx.inc(i, "vec")
                ctx.wait(v, "vec")
                i = v.tensor_copy(QRX[64:128, hh * S:(hh + 1) * S],
                                  QCp[64 * hh:64 * hh + 64, :])
                ctx.inc(i, "vec")

        if _os.environ.get("KSTOP") == "proj":
            continue
        for hh in range(2):
            ro = 64 * hh
            rr = 32 * hh
            for qt in range(NT):
                import os as _os2
                nkc = qt // 4 + 1
                if _os2.environ.get("KFULLK"):
                    nkc = 4
                @block.tensor
                def _(t, qt=qt, ro=ro, rr=rr, nkc=nkc):
                    import os as _os
                    ctx.wait(t, "act")
                    ctx.wait(t, "vec")
                    for kc in range(nkc):
                        klen = min(512, 128 * (qt + 1) - 512 * kc)
                        if _os.environ.get("KFULLK"):
                            klen = 512
                        ps = PS_A[:, 512 * kc:512 * kc + klen]
                        if _os.environ.get("KNOROPE"):
                            i = t.matmul(ps, QCp[ro:ro + 64, 128 * qt:128 * qt + 128],
                                         KCp[ro:ro + 64, 512 * kc:512 * kc + klen],
                                         start=True, stop=True)
                        else:
                            t.matmul(ps, QCp[ro:ro + 64, 128 * qt:128 * qt + 128],
                                     KCp[ro:ro + 64, 512 * kc:512 * kc + klen],
                                     start=True, stop=False)
                            i = t.matmul(ps, QRp[rr:rr + 32, 128 * qt:128 * qt + 128],
                                         KRp[rr:rr + 32, 512 * kc:512 * kc + klen],
                                         start=False, stop=True)
                    ctx.inc(i, "pe")

                @block.vector
                def _(v, qt=qt):
                    import os as _os
                    ctx.wait(v, "pe")
                    if not _os.environ.get("KNOMASK"):
                        db = PS_A[:, 128 * qt:128 * qt + 128]
                        ctx.inc(v.tensor_add(db, db, maskD_sb[:]), "vec")
                        ctx.wait(v, "vec")
                    if _os.environ.get("KNORED"):
                        ctx.inc(v.nop(), "vec")
                    else:
                        ctx.inc(v.reduce_max(negm[:, qt:qt + 1], PS_A[:, 0:128 * (qt + 1)],
                                             axis=AX.X, negate=True), "vec")
                        ctx.wait(v, "vec")
                        ctx.inc(v.tensor_scalar_mul(negm[:, qt:qt + 1], negm[:, qt:qt + 1],
                                                    SCL), "vec")

                @block.scalar
                def _(sc, qt=qt):
                    import os as _os
                    ctx.wait(sc, "vec")
                    ctx.wait(sc, "act")
                    if _os.environ.get("KNOEXP"):
                        fd = 2048 if _os.environ.get("KFULLK") else 128 * (qt + 1)
                        i = sc.activation(scrE[:, 0:fd], PS_A[:, 0:fd], AF.Identity)
                    else:
                        i = sc.activation(scrE[:, 0:128 * (qt + 1)], PS_A[:, 0:128 * (qt + 1)],
                                          AF.Exp, bias=negm[:, qt:qt + 1], scale=SCL,
                                          accum_out=sums[:, qt:qt + 1])
                    ctx.inc(i, "act")


            @block.scalar
            def _(sc):
                ctx.wait(sc, "act")
                ctx.inc(sc.activation(mls[:], sums[:], AF.Ln), "act")

            @block.vector
            def _(v):
                ctx.wait(v, "act")
                ctx.inc(v.tensor_sub(mls[:], mls[:], negm[:]), "vec")
                ctx.wait(v, "vec")
                ctx.inc(v.tensor_scalar_mul(mls[:], mls[:], 1.0 / SCL), "vec")
                ctx.wait(v, "vec")
                ctx.inc(v.tensor_copy(mhb[:], mls[:]), "vec")
                ctx.wait(v, "vec")
                ctx.inc(v.tensor_copy(mhl[:].rearrange("p (t two) -> p two t", two=2)[:, 0, :],
                                      mhb[:]), "vec")
                ctx.wait(v, "vec")
                ctx.inc(v.tensor_sub(mhl[:].rearrange("p (t two) -> p two t", two=2)[:, 1, :],
                                     mls[:],
                                     mhl[:].rearrange("p (t two) -> p two t", two=2)[:, 0, :]),
                        "vec")

            for qt in range(NT):
                @block.tensor
                def _(t, qt=qt):
                    ctx.wait(t, "vec")
                    i = t.matmul(PS_B[0:2, 0:128], mhl[:, 2 * qt:2 * qt + 2],
                                 ctx.consts["IDENTF"][:], start=True, stop=True)
                    ctx.inc(i, "pe")

                @block.vector
                def _(v, qt=qt, hh=hh):
                    ctx.wait(v, "pe")
                    dst = QRX[32:34, hh * S + 128 * qt: hh * S + 128 * qt + 128]
                    ctx.inc(v.tensor_copy(dst, PS_B[0:2, 0:128]), "vec")

        if _os.environ.get("KSTOP") == "pass1":
            continue
        _p2s = set((_os.environ.get("KP2SKIP") or "").split(","))
        pair_vec_tok = ctx.tok("vec")
        for kt in range(NT):
            qc0 = kt // 4
            for hh in range(2):
                ro = 64 * hh
                rr = 32 * hh

                @block.tensor
                def _(t, kt=kt, hh=hh, ro=ro, rr=rr, qc0=qc0, pair_vec_tok=pair_vec_tok):
                    t.wait_ge(ctx.sget("vec"), pair_vec_tok)
                    ctx.wait(t, "act")
                    ctx.wait(t, "vec")
                    for qc in range(qc0, 4):
                        ps = PS_A[:, 512 * qc:512 * qc + 512]
                        i = t.matmul(ps,
                                     KRX[0:128, hh * S + 128 * kt: hh * S + 128 * kt + 128],
                                     QRX[0:128, hh * S + 512 * qc: hh * S + 512 * qc + 512],
                                     start=True, stop=True)
                    ctx.inc(i, "pe")

                if "mask" not in _p2s:
                    @block.vector
                    def _(v, kt=kt, qc0=qc0):
                        ctx.wait(v, "pe")
                        ps = PS_A[:, 512 * qc0:512 * qc0 + 512]
                        i = v.tensor_add(ps, ps, maskT_sb[:, 512 * (kt % 4):512 * (kt % 4) + 512])
                        ctx.inc(i, "vec")

                if "exp" not in _p2s:
                    @block.scalar
                    def _(sc, kt=kt, hh=hh, qc0=qc0):
                        ctx.wait(sc, "vec")
                        ctx.wait(sc, "act")
                        i = sc.activation(ET[:, hh * S + 512 * qc0: (hh + 1) * S],
                                          PS_A[:, 512 * qc0: 2048], AF.Exp, scale=SCL)
                        ctx.inc(i, "act")

                if "av" not in _p2s and not (hh == 1 and "avh1" in _p2s):
                    @block.tensor
                    def _(t, kt=kt, hh=hh, qc0=qc0):
                        ctx.wait(t, "act")
                        for qc in range(qc0, 4):
                            i = t.matmul(PS_B[64 * hh:64 * hh + 64, 512 * qc:512 * qc + 512],
                                         Vp[:, kt * 128 + 64 * hh: kt * 128 + 64 * hh + 64],
                                         ET[:, hh * S + 512 * qc: hh * S + 512 * qc + 512],
                                         start=(kt == 0), stop=(kt == NT - 1),
                                         tile_position=(0, 64 * hh),
                                         skip_group_check=True)
                        ctx.inc(i, "pe")

        if _os.environ.get("KSTOP") == "pass2":
            continue
        @block.scalar
        def _(sc, pair=pair):
            ctx.wait(sc, "pe")
            i = sc.activation(AOT[:, pair * S:(pair + 1) * S], PS_B[:, 0:S], AF.Identity)
            ctx.inc(i, "act")

    if _os.environ.get("KNOOUT"):
        return
    att_act_tok = ctx.tok("act")

    def o_evac(v, psap, t_i, c0, wd):
        dst = x_sb[:, t_i * D + c0: t_i * D + c0 + wd]
        return v.tensor_add(dst, dst, psap)
    gemm_N(ctx, d_o, D, D, AOT, S, PS_A[:, 0:512], o_evac,
           pre_wait=[("act", att_act_tok)])





XS = S + 1


def emit_transpose_full(ctx, x_sb, out_T, ps_tr, rowbuf):
    block = ctx.block

    @block.vector
    def _(v):
        ctx.wait(v, "vec")
        ctx.wait(v, "pe")
        z = out_T[:].rearrange("p (j s) -> p j s", s=XS)[:, :, 0:1]
        ctx.inc(v.memset(z, 0.0), "vec")
    for t in range(NT):
        @block.scalar
        def _(sc, t=t):
            ctx.wait(sc, "pe")
            ctx.wait(sc, "vec")
            ctx.wait(sc, "act")
            i = sc.activation(rowbuf[:], x_sb[:, t * D:(t + 1) * D], AF.Identity)
            ctx.inc(i, "act")

        @block.tensor
        def _(tens, t=t):
            ctx.wait(tens, "act")
            ctx.wait(tens, "vec")
            if t == 0:
                ctx.wait(tens, "wdma")
            for half in range(2):
                for j4 in range(4):
                    j = half * 4 + j4
                    i = tens.matmul(ps_tr[:, 512 * half + 128 * j4: 512 * half + 128 * j4 + 128],
                                    rowbuf[:, 128 * j:128 * (j + 1)],
                                    ctx.consts["IDENT"][:], start=True, stop=True)
            ctx.inc(i, "pe")

        @block.vector
        def _(v, t=t):
            ctx.wait(v, "pe")
            srcv = ps_tr[:].rearrange("p (j c) -> p j c", c=128)
            dst = out_T[:].rearrange("p (j s) -> p j s", s=XS)[:, :, 1 + 128 * t:1 + 128 * t + 128]
            ctx.inc(v.tensor_copy(dst, srcv), "vec")


def emit_ln_inplace(ctx, x_sb, PS_unused):
    block = ctx.block
    raw = ctx.sbuf("lni_raw", [128, 12], F32)
    mv = ctx.sbuf("lni_mv", [128, 2 * NT], F32)
    rstd = ctx.sbuf("lni_rstd", [128, NT], F32)
    nmr = ctx.sbuf("lni_nmr", [128, NT], F32)

    @block.vector
    def _(v):
        ctx.wait(v, "xin")
        ctx.wait(v, "vec")
        for t in range(NT):
            xt = x_sb[:, t * D:(t + 1) * D]
            v.bn_stats(raw[:, 0:6], xt[:, 0:512])
            ctx.inc(v.bn_stats(raw[:, 6:12], xt[:, 512:1024]), "vec")
            ctx.wait(v, "vec")
            ctx.inc(v.bn_aggr(mv[:, 2 * t:2 * t + 2], raw[:]), "vec")
            ctx.wait(v, "vec")
        var_v = mv[:].rearrange("p (t two) -> p t two", two=2)[:, :, 1:2]
        ctx.inc(v.tensor_scalar_add(rstd[:].rearrange("p t -> p t ()"), var_v, 1e-5), "vec")

    @block.scalar
    def _(sc):
        ctx.wait(sc, "vec")
        ctx.inc(sc.activation(rstd[:], rstd[:], AF.Sqrt), "act")

    @block.vector
    def _(v):
        ctx.wait(v, "act")
        ctx.inc(v.reciprocal(rstd[:], rstd[:]), "vec")
        ctx.wait(v, "vec")
        mean_v = mv[:].rearrange("p (t two) -> p t two", two=2)[:, :, 0:1]
        i = v.tensor_tensor(nmr[:].rearrange("p t -> p t ()"), mean_v,
                            rstd[:].rearrange("p t -> p t ()"), ALU.mult)
        ctx.inc(i, "vec")
        ctx.wait(v, "vec")
        ctx.inc(v.tensor_scalar_mul(nmr[:], nmr[:], -1.0), "vec")

    for t in range(NT):
        @block.scalar
        def _(sc, t=t):
            ctx.wait(sc, "vec")
            xt = x_sb[:, t * D:(t + 1) * D]
            i = sc.activation(xt, xt, AF.Identity,
                              bias=nmr[:, t:t + 1], scale=rstd[:, t:t + 1])
            ctx.inc(i, "act")


def emit_ffn(ctx, x_sb, w, pfx, PS_A, PS_B, x_in_dram):
    block = ctx.block
    HEADS, DH, W = 8, 64, 512
    HID = 2730
    HIDP = 2816
    NC = 11

    wqkv = np.asarray(w[pfx + "_wqkv"], np.float32)
    q_scale = np.asarray(w[pfx + "_q_scale"], np.float32)
    k_scale = np.asarray(w[pfx + "_k_scale"], np.float32)
    wo = np.asarray(w[pfx + "_wo"], np.float32)
    ln_g = np.asarray(w[pfx + "_ln_g"], np.float32)
    ln_b = np.asarray(w[pfx + "_ln_b"], np.float32)
    w1 = np.asarray(w[pfx + "_w1"], np.float32)
    b1 = np.asarray(w[pfx + "_b1"], np.float32)
    w2 = np.asarray(w[pfx + "_w2"], np.float32)
    b2 = np.asarray(w[pfx + "_b2"], np.float32)

    def perm64(Wm):
        Wr = np.zeros_like(Wm)
        nh = Wm.shape[1] // 64
        for h in range(nh):
            blk = Wm[:, 64 * h:64 * h + 64]
            Wr[:, 64 * h:64 * h + 32] = blk[:, 32:64]
            Wr[:, 64 * h + 32:64 * h + 64] = blk[:, 0:32]
        return np.ascontiguousarray(Wr)

    dL = []
    for l in range(2):
        Wq = np.ascontiguousarray(wqkv[l][:, 0:512])
        Wk = np.ascontiguousarray(wqkv[l][:, 512:1024])
        Wv = np.ascontiguousarray(wqkv[l][:, 1024:1536])
        W1p = np.zeros((D, 2 * HIDP), np.float32)
        b1p = np.zeros((1, 2 * HIDP), np.float32)
        a_part = np.zeros((D, HIDP), np.float32)
        g_part = np.zeros((D, HIDP), np.float32)
        a_part[:, 0:HID] = w1[l][:, 0:HID]
        g_part[:, 0:HID] = w1[l][:, HID:2 * HID]
        ab = np.zeros((1, HIDP), np.float32)
        gb = np.zeros((1, HIDP), np.float32)
        ab[0, 0:HID] = b1[l][0:HID]
        gb[0, 0:HID] = b1[l][HID:2 * HID]
        for c in range(NC):
            W1p[:, 512 * c:512 * c + 256] = a_part[:, 256 * c:256 * c + 256]
            W1p[:, 512 * c + 256:512 * c + 512] = g_part[:, 256 * c:256 * c + 256]
            b1p[0, 512 * c:512 * c + 256] = ab[0, 256 * c:256 * c + 256]
            b1p[0, 512 * c + 256:512 * c + 512] = gb[0, 256 * c:256 * c + 256]
        W2p = np.zeros((HIDP, D), np.float32)
        W2p[0:HID, :] = w2[l]
        dL.append(dict(
            q=ctx.din(f"w_{pfx}_q{l}", Wq), qr=ctx.din(f"w_{pfx}_qr{l}", perm64(Wq)),
            k=ctx.din(f"w_{pfx}_k{l}", Wk), kr=ctx.din(f"w_{pfx}_kr{l}", perm64(Wk)),
            v=ctx.din(f"w_{pfx}_v{l}", Wv),
            wo=ctx.din(f"w_{pfx}_wo{l}", np.ascontiguousarray(wo[l])),
            w1=ctx.din(f"w_{pfx}_w1{l}", W1p),
            b1=ctx.din(f"w_{pfx}_b1{l}", b1p),
            w2=ctx.din(f"w_{pfx}_w2{l}", W2p),
            b2=ctx.din(f"w_{pfx}_b2{l}", b2[l][None, :]),
            qs=ctx.din(f"w_{pfx}_qs{l}", np.tile(q_scale[l], 2)[:, None]),
            qsp=ctx.din(f"w_{pfx}_qsp{l}",
                        np.tile(np.concatenate([q_scale[l][32:], q_scale[l][:32]]), 2)[:, None]),
            ks=ctx.din(f"w_{pfx}_ks{l}", np.tile(k_scale[l], 2)[:, None]),
            ksp=ctx.din(f"w_{pfx}_ksp{l}",
                        np.tile(np.concatenate([k_scale[l][32:], k_scale[l][:32]]), 2)[:, None]),
            lg=ctx.din(f"w_{pfx}_lg{l}", ln_g[l].reshape(8, 128).T.copy()),
            lb=ctx.din(f"w_{pfx}_lb{l}", ln_b[l].reshape(8, 128).T.copy()),
        ))

    half_idx = np.arange(0, DH, 2, dtype=np.float64)
    inv_freq = 1.0 / 10000.0 ** (half_idx / DH)
    tt = np.arange(S, dtype=np.float64)
    freqs = tt[:, None] * inv_freq[None, :]
    freqs = np.concatenate([freqs, freqs], -1)
    base = (half_idx + 0.4 * DH) / (1.4 * DH)
    power = (tt - S // 2) / float(W // 2)
    xsc = base[None, :] ** power[:, None]
    xsc = np.concatenate([xsc, xsc], -1)
    cosv = np.cos(freqs)
    sinv = np.sin(freqs)
    sgn = np.where(np.arange(DH) < 32, -1.0, 1.0)[None, :]
    cq8 = np.tile((cosv * xsc * 8.0).T, (2, 1))
    sq8 = np.tile((sinv * xsc * 8.0 * sgn).T, (2, 1))
    ck = np.tile((cosv / xsc).T, (2, 1))
    sk = np.tile((sinv / xsc * sgn).T, (2, 1))
    d_cq = ctx.din(f"c_{pfx}_cq", np.ascontiguousarray(cq8, np.float32))
    d_sq = ctx.din(f"c_{pfx}_sq", np.ascontiguousarray(sq8, np.float32))
    d_ck = ctx.din(f"c_{pfx}_ck", np.ascontiguousarray(ck, np.float32))
    d_sk = ctx.din(f"c_{pfx}_sk", np.ascontiguousarray(sk, np.float32))

    sel = np.zeros((128, 4 * 8), np.float32)
    selT = np.zeros((8, 4 * 128), np.float32)
    for jo in range(4):
        for p in range(128):
            h = 2 * jo + (1 if p >= 64 else 0)
            sel[p, jo * 8 + h] = 1.0
            selT[h, jo * 128 + p] = 1.0
    maskD = np.zeros((128, 128), np.float32)
    maskU = np.zeros((128, 128), np.float32)
    for p in range(128):
        maskD[p, p + 1:] = -1e30
        maskU[p, 0:p] = -1e30
    onescolb, _ = load_row_const(ctx, pfx + "_onescol",
                                 np.ones((128, 1), np.float32), BF16)
    onesf1, _ = load_row_const(ctx, pfx + "_onesf1", np.ones((1, 1), np.float32))

    xT = ctx.sbuf("ff_xT", [128, DT * XS], BF16)
    rowb = ctx.sbuf("ff_rowb", [128, D], BF16)

    emit_ln_inplace(ctx, x_sb, None)

    for l in range(2):
        dl = dL[l]
        qs_sb, _ = load_row_const(ctx, f"{pfx}_qs{l}v", np.zeros((1, 1), np.float32)) \
            if False else (None, None)
        qs_t = ctx.sbuf(f"ff_qs{l}", [128, 4], F32)

        @block.gpsimd
        def _(g, l=l, qs_t=qs_t):
            dd = dL[l]
            for nm in ("pe", "vec", "act"):
                if ctx.cnt.get(nm):
                    ctx.wait(g, nm)
            ctx.inc(g.dma_start(out=qs_t[:, 0:1], in_=dd["qs"]), "wdma", 16)
            ctx.inc(g.dma_start(out=qs_t[:, 1:2], in_=dd["qsp"]), "wdma", 16)
            ctx.inc(g.dma_start(out=qs_t[:, 2:3], in_=dd["ks"]), "wdma", 16)
            ctx.inc(g.dma_start(out=qs_t[:, 3:4], in_=dd["ksp"]), "wdma", 16)

        atn = ExitStack()
        TWc = ctx.sbuf("ff_TWc", [128, 512], F32, atn)
        TWs = ctx.sbuf("ff_TWs", [128, 512], F32, atn)
        PRJ = ctx.sbuf("ff_PRJ", [128, 4 * 512], F32, atn)
        PRR = ctx.sbuf("ff_PRR", [128, 4 * 512], F32, atn)
        sel_sb, _ = load_row_const(ctx, f"{pfx}_sel{l}", sel, BF16, scope=atn,
                                   guards=("pe", "vec", "act"))
        selT_sb, _ = load_row_const(ctx, f"{pfx}_selT{l}", selT, F32, scope=atn)
        maskD_sb, _ = load_row_const(ctx, f"{pfx}_maskD{l}", maskD, BF16, scope=atn)
        maskU_sb, _ = load_row_const(ctx, f"{pfx}_maskU{l}", maskU, BF16, scope=atn)
        neg2_sb, _ = load_row_const(ctx, f"{pfx}_neg2{l}",
                                    np.full((2, 128), -1.0, np.float32), BF16, scope=atn)

        emit_transpose_full(ctx, x_sb, xT, PS_A[:, 0:1024], rowb)

        QW = ctx.sbuf("ff_QW", [128, 4 * W], BF16, atn)
        K2 = ctx.sbuf("ff_K2", [128, 4 * 2 * W], BF16, atn)
        V2 = ctx.sbuf("ff_V2", [128, 8 * W], BF16, atn)
        OW = ctx.sbuf("ff_OW", [128, 4 * W], BF16, atn)
        WO = ctx.sbuf("ff_WO", [128, 4 * 1024], BF16, atn)
        ET = ctx.sbuf("ff_ET", [128, W], BF16, atn)
        scrE = ctx.sbuf("ff_scrE", [128, 640], BF16, atn)
        nrm = ctx.sbuf("ff_nrm", [8, W], F32, atn)
        MH = ctx.sbuf("ff_MH", [2, 4 * 128], BF16, atn)
        negm = ctx.sbuf("ff_negm", [128, 4], F32, atn)
        sums = ctx.sbuf("ff_sums", [128, 4], F32, atn)
        mls = ctx.sbuf("ff_mls", [128, 4], F32, atn)
        mhb = ctx.sbuf("ff_mhb", [128, 4], BF16, atn)
        mhl = ctx.sbuf("ff_mhl", [128, 8], F32, atn)

        @block.gpsimd
        def _(g, l=l):
            for nm in ("pe", "vec", "act"):
                ctx.wait(g, nm)
            srcw = dL[l]["wo"].rearrange("(j p) c -> p j c", p=128)
            dstw = WO[:].rearrange("p (j c) -> p j c", c=1024)
            ctx.inc(g.dma_start(out=dstw, in_=srcw), "wod", 16)
        wo_tok = ctx.tok("wod")

        def project_win(dram, dramr, wi, dst, d_tabc, d_tabs, qs_col, qsp_col):
            s0 = W * wi

            @block.gpsimd
            def _(g):
                for nm in ("pe", "vec", "act"):
                    ctx.wait(g, nm)
                ctx.inc(g.dma_start(out=TWc[:], in_=d_tabc[:, s0:s0 + 512]), "wdma", 16)
                ctx.inc(g.dma_start(out=TWs[:], in_=d_tabs[:, s0:s0 + 512]), "wdma", 16)

            for which, dr in (("raw", dram), ("rot", dramr)):
                slot_i = pair_flip()
                slot = ctx.consts["WSLOTS"][slot_i]
                wsem, tok = wload(ctx, dr, D, 0, 512, slot, slot_i,
                                  pe_guard=ctx.tok("pe"))
                for jo in range(4):
                    @block.tensor
                    def _(t, jo=jo, slot=slot, tok=tok, wsem=wsem, s0=s0):
                        t.wait_ge(ctx.sget(wsem), tok)
                        ctx.wait(t, "act")
                        ctx.wait(t, "vec")
                        for j in range(DT):
                            off = s0 if j >= 4 else s0 + 1
                            i = t.matmul(PS_A[:, 0:512],
                                         slot[:, j * 512 + 128 * jo: j * 512 + 128 * jo + 128],
                                         xT[:, j * XS + off: j * XS + off + 512],
                                         start=(j == 0), stop=(j == DT - 1))
                        ctx.inc(i, "pe")

                    @block.scalar
                    def _(sc, jo=jo, which=which):
                        ctx.wait(sc, "pe")
                        tgt = PRJ if which == "raw" else PRR
                        i = sc.activation(tgt[:, jo * 512:(jo + 1) * 512], PS_A[:, 0:512],
                                          AF.Identity)
                        ctx.inc(i, "act")

            for jo in range(4):
                @block.scalar
                def _(sc, jo=jo):
                    ctx.wait(sc, "act")
                    ctx.wait(sc, "pe")
                    ctx.wait(sc, "vec")
                    i = sc.activation(scrE[:, 0:512], PRJ[:, jo * 512:(jo + 1) * 512],
                                      AF.Square)
                    ctx.inc(i, "act")

                @block.tensor
                def _(t, jo=jo):
                    ctx.wait(t, "act")
                    if jo == 0:
                        ctx.wait(t, "vec")
                    i = t.matmul(PS_B[0:8, 0:512], sel_sb[:, jo * 8:(jo + 1) * 8],
                                 scrE[:, 0:512], start=(jo == 0), stop=(jo == 3))
                    ctx.inc(i, "pe")

            @block.scalar
            def _(sc):
                ctx.wait(sc, "pe")
                ctx.inc(sc.activation(nrm[:], PS_B[0:8, 0:512], AF.Sqrt), "act")

            @block.vector
            def _(v):
                ctx.wait(v, "act")
                ctx.inc(v.tensor_scalar_max(nrm[:], nrm[:], 1e-12), "vec")
                ctx.wait(v, "vec")
                ctx.inc(v.reciprocal(nrm[:], nrm[:]), "vec")

            for jo in range(4):
                @block.tensor
                def _(t, jo=jo):
                    ctx.wait(t, "vec")
                    i = t.matmul(PS_B[:, 512:1024], selT_sb[0:8, jo * 128:(jo + 1) * 128],
                                 nrm[:], start=True, stop=True)
                    ctx.inc(i, "pe")

                @block.vector
                def _(v, jo=jo, qs_col=qs_col, qsp_col=qsp_col, dst=dst):
                    ctx.wait(v, "pe")
                    ctx.wait(v, "wdma")
                    dsl = slice(jo * 512, (jo + 1) * 512)
                    i = v.tensor_scalar(PRJ[:, dsl], PRJ[:, dsl], qs_col, None, ALU.mult)
                    ctx.inc(i, "vec")
                    ctx.wait(v, "vec")
                    i = v.tensor_scalar(PRR[:, dsl], PRR[:, dsl], qsp_col, None, ALU.mult)
                    ctx.inc(i, "vec")
                    ctx.wait(v, "vec")
                    i = v.tensor_mul(PRJ[:, dsl], PRJ[:, dsl], TWc[:])
                    ctx.inc(i, "vec")
                    ctx.wait(v, "vec")
                    i = v.tensor_mul(PRR[:, dsl], PRR[:, dsl], TWs[:])
                    ctx.inc(i, "vec")
                    ctx.wait(v, "vec")
                    i = v.tensor_add(PRJ[:, dsl], PRJ[:, dsl], PRR[:, dsl])
                    ctx.inc(i, "vec")
                    ctx.wait(v, "vec")
                    i = v.tensor_mul(PRJ[:, dsl], PRJ[:, dsl], PS_B[:, 512:1024])
                    ctx.inc(i, "vec")
                    ctx.wait(v, "vec")
                    i = v.tensor_copy(dst[:, jo * W:(jo + 1) * W], PRJ[:, dsl])
                    ctx.inc(i, "vec")

        def pair_flip():
            pair_flip.f = (getattr(pair_flip, "f", 0) + 1) % 2
            return pair_flip.f

        for wi in range(4):
            project_win(dl["q"], dl["qr"], wi, QW, d_cq, d_sq,
                        qs_t[:, 0:1], qs_t[:, 1:2])
            project_win(dl["k"], dl["kr"], wi, OW, d_ck, d_sk,
                        qs_t[:, 2:3], qs_t[:, 3:4])
            if ctx.consts.get("DBGQ") is not None and l == 0 and wi == 1:
                @block.sync
                def _(sync):
                    for nm in ("pe", "vec", "act"):
                        ctx.wait(sync, nm)
                    ctx.inc(sync.dma_start(out=ctx.consts["DBGQ"][:],
                                           in_=QW[:]), "sdma", 16)
                    ctx.inc(sync.dma_start(out=ctx.consts["DBGK"][:],
                                           in_=OW[:]), "sdma", 16)

                @block.vector
                def _(v):
                    ctx.wait(v, "sdma")
                    ctx.inc(v.nop(), "vec")
            for jo in range(4):
                @block.vector
                def _(v, jo=jo, wi=wi):
                    ctx.wait(v, "vec")
                    ctx.wait(v, "pe")
                    dstk = K2[:, jo * 2 * W + (wi % 2) * W: jo * 2 * W + (wi % 2) * W + W]
                    ctx.inc(v.tensor_copy(dstk, OW[:, jo * W:(jo + 1) * W]), "vec")

            slot_i = pair_flip()
            slot = ctx.consts["WSLOTS"][slot_i]
            wsem, tok = wload(ctx, dl["v"], D, 0, 512, slot, slot_i,
                              pe_guard=ctx.tok("pe"))
            for ti in range(4):
                @block.tensor
                def _(t, ti=ti, wi=wi, slot=slot, tok=tok, wsem=wsem):
                    t.wait_ge(ctx.sget(wsem), tok)
                    ctx.wait(t, "act")
                    ctx.wait(t, "vec")
                    s0 = W * wi + 128 * ti
                    for j in range(DT):
                        off = s0 if j >= 4 else s0 + 1
                        i = t.matmul(PS_A[:, 1024:1536],
                                     xT[:, j * XS + off: j * XS + off + 128],
                                     slot[:, j * 512:(j + 1) * 512],
                                     start=(j == 0), stop=(j == DT - 1))
                    ctx.inc(i, "pe")

                @block.vector
                def _(v, ti=ti, wi=wi):
                    ctx.wait(v, "pe")
                    dstv = V2[:, ((wi % 2) * 4 + ti) * W: ((wi % 2) * 4 + ti) * W + W]
                    ctx.inc(v.tensor_copy(dstv, PS_A[:, 1024:1536]), "vec")

            for h in range(HEADS):
                jo = h // 2
                ro = 64 * (h % 2)
                kjs_all = list(range(8))
                if wi == 0:
                    kjs_all = [kj for kj in kjs_all if kj >= 4]

                def kcol(kj):
                    win = 0 if kj < 4 else 1
                    slot_w = (wi - 1 + win) % 2
                    return jo * 2 * W + slot_w * W + 128 * (kj % 4)

                def vcol(kj, hloc):
                    win = 0 if kj < 4 else 1
                    slot_w = (wi - 1 + win) % 2
                    return (slot_w * 4 + (kj % 4)) * W + 64 * hloc

                for qt in range(4):
                    kjs = [kj for kj in range(qt, qt + 5) if kj in kjs_all]
                    @block.tensor
                    def _(t, qt=qt, kjs=kjs, jo=jo, ro=ro, wi=wi):
                        ctx.wait(t, "act")
                        ctx.wait(t, "vec")
                        for bi, kj in enumerate(kjs):
                            i = t.matmul(PS_A[:, 128 * bi:128 * bi + 128],
                                         QW[ro:ro + 64, jo * W + 128 * qt: jo * W + 128 * qt + 128],
                                         K2[ro:ro + 64, kcol(kj):kcol(kj) + 128],
                                         start=True, stop=True)
                        ctx.inc(i, "pe")

                    @block.vector
                    def _(v, qt=qt, kjs=kjs):
                        ctx.wait(v, "pe")
                        nb = len(kjs)
                        for bi, kj in enumerate(kjs):
                            mk = maskU_sb if kj == qt else (maskD_sb if kj == qt + 4 else None)
                            if mk is not None:
                                db = PS_A[:, 128 * bi:128 * bi + 128]
                                ctx.inc(v.tensor_add(db, db, mk[:]), "vec")
                                ctx.wait(v, "vec")
                        ctx.inc(v.reduce_max(negm[:, qt:qt + 1], PS_A[:, 0:128 * nb],
                                             axis=AX.X, negate=True), "vec")

                    @block.scalar
                    def _(sc, qt=qt, kjs=kjs):
                        ctx.wait(sc, "vec")
                        ctx.wait(sc, "act")
                        i = sc.activation(scrE[:, 0:128 * len(kjs)],
                                          PS_A[:, 0:128 * len(kjs)], AF.Exp,
                                          bias=negm[:, qt:qt + 1],
                                          accum_out=sums[:, qt:qt + 1])
                        ctx.inc(i, "act")

                @block.scalar
                def _(sc):
                    ctx.wait(sc, "act")
                    ctx.inc(sc.activation(mls[:, 0:4], sums[:, 0:4], AF.Ln), "act")

                @block.vector
                def _(v):
                    ctx.wait(v, "act")
                    ctx.inc(v.tensor_sub(mls[:, 0:4], mls[:, 0:4], negm[:, 0:4]), "vec")
                    ctx.wait(v, "vec")
                    ctx.inc(v.tensor_copy(mhb[:, 0:4], mls[:, 0:4]), "vec")
                    ctx.wait(v, "vec")
                    ctx.inc(v.tensor_copy(
                        mhl[:].rearrange("p (t two) -> p two t", two=2)[:, 0, :],
                        mhb[:, 0:4]), "vec")
                    ctx.wait(v, "vec")
                    ctx.inc(v.tensor_sub(
                        mhl[:].rearrange("p (t two) -> p two t", two=2)[:, 1, :],
                        mls[:, 0:4],
                        mhl[:].rearrange("p (t two) -> p two t", two=2)[:, 0, :]), "vec")

                for qt in range(4):
                    @block.tensor
                    def _(t, qt=qt):
                        ctx.wait(t, "vec")
                        i = t.matmul(PS_B[0:2, 0:128], mhl[:, 2 * qt:2 * qt + 2],
                                     ctx.consts["IDENTF"][:], start=True, stop=True)
                        ctx.inc(i, "pe")

                    @block.vector
                    def _(v, qt=qt):
                        ctx.wait(v, "pe")
                        ctx.inc(v.tensor_copy(MH[0:2, 128 * qt:128 * qt + 128],
                                              PS_B[0:2, 0:128]), "vec")

                if (ctx.consts.get("DBGS") is not None and l == 0 and wi == 1
                        and h == 0):
                    @block.sync
                    def _(sync):
                        for nm in ("pe", "vec", "act"):
                            ctx.wait(sync, nm)
                        ctx.inc(sync.dma_start(out=ctx.consts["DBGV"][:],
                                               in_=V2[:]), "sdma", 16)

                    @block.vector
                    def _(v):
                        ctx.wait(v, "sdma")
                        ctx.inc(v.nop(), "vec")
                if (ctx.consts.get("DBGS") is not None and l == 0 and wi == 1
                        and h == 0):
                    @block.sync
                    def _(sync):
                        for nm in ("pe", "vec", "act"):
                            ctx.wait(sync, nm)
                        ctx.inc(sync.dma_start(out=ctx.consts["DBGS"][0:128, 0:4],
                                               in_=negm[:, 0:4]), "sdma", 16)
                        ctx.inc(sync.dma_start(out=ctx.consts["DBGS"][0:128, 4:8],
                                               in_=sums[:, 0:4]), "sdma", 16)
                        ctx.inc(sync.dma_start(out=ctx.consts["DBGS"][0:128, 8:16],
                                               in_=mhl[:, 0:8]), "sdma", 16)

                    @block.vector
                    def _(v):
                        ctx.wait(v, "sdma")
                        ctx.inc(v.nop(), "vec")

                h_vec_tok = ctx.tok("vec")
                for kj in kjs_all:
                    qt0 = max(0, kj - 4)
                    qt1 = min(3, kj)
                    qn = 128 * (qt1 - qt0 + 1)

                    @block.tensor
                    def _(t, kj=kj, qt0=qt0, qn=qn, jo=jo, ro=ro, h_vec_tok=h_vec_tok):
                        t.wait_ge(ctx.sget("vec"), h_vec_tok)
                        ctx.wait(t, "act")
                        ctx.wait(t, "vec")
                        ps = PS_A[:, 0:qn]
                        t.matmul(ps, K2[ro:ro + 64, kcol(kj):kcol(kj) + 128],
                                 QW[ro:ro + 64, jo * W + 128 * qt0: jo * W + 128 * qt0 + qn],
                                 start=True, stop=False)
                        i = t.matmul(ps, neg2_sb[0:2, :],
                                     MH[0:2, 128 * qt0:128 * qt0 + qn],
                                     start=False, stop=True)
                        ctx.inc(i, "pe")

                    @block.vector
                    def _(v, kj=kj, qt0=qt0, qt1=qt1):
                        ctx.wait(v, "pe")
                        if qt0 <= kj <= qt1:
                            bi = kj - qt0
                            db = PS_A[:, 128 * bi:128 * bi + 128]
                            ctx.inc(v.tensor_add(db, db, maskD_sb[:]), "vec")
                        if qt0 <= kj - 4 <= qt1:
                            bi = kj - 4 - qt0
                            db = PS_A[:, 128 * bi:128 * bi + 128]
                            ctx.inc(v.tensor_add(db, db, maskU_sb[:]), "vec")

                    @block.scalar
                    def _(sc, kj=kj, qn=qn):
                        ctx.wait(sc, "vec")
                        ctx.wait(sc, "act")
                        i = sc.activation(ET[:, 0:qn], PS_A[:, 0:qn], AF.Exp)
                        ctx.inc(i, "act")

                    if (ctx.consts.get("DBGE") is not None and l == 0 and wi == 1
                            and h == 0 and kj == 4):
                        @block.sync
                        def _(sync):
                            for nm in ("pe", "vec", "act"):
                                ctx.wait(sync, nm)
                            ctx.inc(sync.dma_start(out=ctx.consts["DBGE"][:],
                                                   in_=ET[:, 0:512]), "sdma", 16)

                        @block.vector
                        def _(v):
                            ctx.wait(v, "sdma")
                            ctx.inc(v.nop(), "vec")

                    @block.tensor
                    def _(t, kj=kj, qt0=qt0, qt1=qt1, h=h, wi=wi):
                        ctx.wait(t, "act")
                        first_kj = kjs_all[0] if True else 0
                        for qt in range(qt0, qt1 + 1):
                            start = (kj == max(qt, first_kj))
                            stop = (kj == qt + 4)
                            i = t.matmul(PS_B[0:64, 512 * qt:512 * qt + 128],
                                         V2[:, vcol(kj, h):vcol(kj, h) + 64],
                                         ET[:, 128 * (qt - qt0):128 * (qt - qt0) + 128],
                                         start=start, stop=stop,
                                         skip_group_check=True)
                        ctx.inc(i, "pe")

                @block.scalar
                def _(sc, h=h, jo=jo, ro=ro):
                    ctx.wait(sc, "pe")
                    for qt in range(4):
                        i = sc.activation(OW[ro:ro + 64, jo * W + 128 * qt: jo * W + 128 * qt + 128],
                                          PS_B[0:64, 512 * qt:512 * qt + 128], AF.Identity)
                    ctx.inc(i, "act")

            if ctx.consts.get("DBGO") is not None and l == 0 and wi in (0, 1):
                @block.sync
                def _(sync, wi=wi):
                    for nm in ("pe", "vec", "act"):
                        ctx.wait(sync, nm)
                    ctx.inc(sync.dma_start(out=ctx.consts["DBGO"][:, 2048 * wi:2048 * wi + 2048],
                                           in_=OW[:]), "sdma", 16)

                @block.vector
                def _(v):
                    ctx.wait(v, "sdma")
                    ctx.inc(v.nop(), "vec")

            @block.tensor
            def _(t, wi=wi):
                t.wait_ge(ctx.sget("wod"), wo_tok)
            for ti in range(4):
                for dc in range(2):
                    @block.tensor
                    def _(t, ti=ti, dc=dc, wi=wi):
                        ctx.wait(t, "act")
                        ctx.wait(t, "vec")
                        for jo in range(4):
                            i = t.matmul(PS_A[:, 1536:2048],
                                         OW[:, jo * W + 128 * ti: jo * W + 128 * ti + 128],
                                         WO[:, jo * 1024 + 512 * dc: jo * 1024 + 512 * dc + 512],
                                         start=(jo == 0), stop=(jo == 3))
                        ctx.inc(i, "pe")

                    @block.vector
                    def _(v, ti=ti, dc=dc, wi=wi):
                        ctx.wait(v, "pe")
                        st_g = 4 * wi + ti
                        dstx = x_sb[:, st_g * D + 512 * dc: st_g * D + 512 * dc + 512]
                        ctx.inc(v.tensor_add(dstx, dstx, PS_A[:, 1536:2048]), "vec")

        atn.close()
        if ctx.consts.get("DBG") is not None and l == 0:
            dbg = ctx.consts["DBG"]

            @block.sync
            def _(sync):
                for nm in ("pe", "vec", "act"):
                    ctx.wait(sync, nm)
                dstd = dbg.rearrange("(t p) d -> p t d", p=128)
                srcd = x_sb[:].rearrange("p (t d) -> p t d", d=D)
                ctx.inc(sync.dma_start(out=dstd, in_=srcd), "sdma", 16)

            @block.vector
            def _(v):
                ctx.wait(v, "sdma")
                ctx.inc(v.nop(), "vec")
        emit_transpose_full(ctx, x_sb, xT, PS_A[:, 0:1024], rowb)

        mlp = ExitStack()
        mnr = ctx.sbuf("ff_mnr", [1, S], F32, mlp)
        rrow = ctx.sbuf("ff_rrow", [1, S], F32, mlp)
        scr1 = ctx.sbuf("ff_scr1", [1, 512], F32, mlp)
        rcol = ctx.sbuf("ff_rcol", [128, NT], F32, mlp)
        sqc = ctx.sbuf("ff_sqc", [128, 512], BF16, mlp)
        lg_t = ctx.sbuf("ff_lgt", [128, 8], F32, mlp)
        lb_t = ctx.sbuf("ff_lbt", [128, 8], F32, mlp)
        lg_b = ctx.sbuf("ff_lgb", [128, 8], BF16, mlp)
        lb_b = ctx.sbuf("ff_lbb", [128, 8], BF16, mlp)
        b1row = ctx.sbuf("ff_b1row", [1, 2 * HIDP], BF16, mlp)
        b2row = ctx.sbuf("ff_b2row", [1, D], BF16, mlp)
        urow = ctx.sbuf("ff_urow", [1, 1024], F32, mlp)
        bxp = ctx.sbuf("ff_bxp", [128, 512], BF16, mlp)
        hT = ctx.sbuf("ff_hT", [128, 22 * 512], BF16, mlp)
        ab_sb = ctx.sbuf("ff_ab", [128, 512], BF16, mlp)
        W2R = ctx.sbuf("ff_W2R", [128, 22 * 512], BF16, mlp)

        @block.gpsimd
        def _(g, l=l):
            for nm in ("pe", "vec", "act"):
                ctx.wait(g, nm)
            ctx.inc(g.dma_start(out=lg_t[:], in_=dL[l]["lg"]), "wdma", 16)
            ctx.inc(g.dma_start(out=lb_t[:], in_=dL[l]["lb"]), "wdma", 16)
            ctx.inc(g.dma_start(out=lg_b[:], in_=dL[l]["lg"]), "wdma", 16)
            ctx.inc(g.dma_start(out=lb_b[:], in_=dL[l]["lb"]), "wdma", 16)
            ctx.inc(g.dma_start(out=b1row[:], in_=dL[l]["b1"]), "wdma", 16)
            ctx.inc(g.dma_start(out=b2row[:], in_=dL[l]["b2"]), "wdma", 16)

        for scn in range(4):
            for j in range(DT):
                @block.scalar
                def _(sc, j=j, scn=scn):
                    ctx.wait(sc, "act")
                    ctx.wait(sc, "pe")
                    ctx.wait(sc, "vec")
                    s0 = 512 * scn
                    off = s0 if j >= 4 else s0 + 1
                    i = sc.activation(sqc[:], xT[:, j * XS + off: j * XS + off + 512],
                                      AF.Square)
                    ctx.inc(i, "act")

                @block.tensor
                def _(t, j=j, scn=scn):
                    ctx.wait(t, "act")
                    if j == 0:
                        ctx.wait(t, "vec")
                    s0 = 512 * scn
                    off = s0 if j >= 4 else s0 + 1
                    t.matmul(PS_B[0:1, 0:512], onescolb[:],
                             xT[:, j * XS + off: j * XS + off + 512],
                             start=(j == 0), stop=(j == DT - 1))
                    i = t.matmul(PS_B[0:1, 512:1024], onescolb[:], sqc[:],
                                 start=(j == 0), stop=(j == DT - 1))
                    ctx.inc(i, "pe")

            @block.vector
            def _(v, scn=scn):
                ctx.wait(v, "pe")
                s0 = 512 * scn
                ctx.inc(v.tensor_scalar_mul(mnr[0:1, s0:s0 + 512], PS_B[0:1, 0:512],
                                            1.0 / float(D)), "vec")
                ctx.wait(v, "vec")
                ctx.inc(v.tensor_scalar_mul(rrow[0:1, s0:s0 + 512], PS_B[0:1, 512:1024],
                                            1.0 / float(D)), "vec")
                ctx.wait(v, "vec")
                i = v.scalar_tensor_tensor(scr1[:], mnr[0:1, s0:s0 + 512],
                                           -1.0, mnr[0:1, s0:s0 + 512],
                                           ALU.mult, ALU.mult)
                ctx.inc(i, "vec")
                ctx.wait(v, "vec")
                i = v.tensor_add(rrow[0:1, s0:s0 + 512], rrow[0:1, s0:s0 + 512],
                                 scr1[:])
                ctx.inc(i, "vec")
                ctx.wait(v, "vec")
                ctx.inc(v.tensor_scalar_add(rrow[0:1, s0:s0 + 512],
                                            rrow[0:1, s0:s0 + 512], 1e-5), "vec")

        @block.scalar
        def _(sc):
            ctx.wait(sc, "vec")
            ctx.inc(sc.activation(rrow[:], rrow[:], AF.Sqrt), "act")

        @block.vector
        def _(v):
            ctx.wait(v, "act")
            ctx.inc(v.reciprocal(rrow[:], rrow[:]), "vec")
            ctx.wait(v, "vec")
            i = v.tensor_mul(mnr[:], mnr[:], rrow[:])
            ctx.inc(i, "vec")
            ctx.wait(v, "vec")
            ctx.inc(v.tensor_scalar_mul(mnr[:], mnr[:], -1.0), "vec")

        for t_i in range(NT):
            @block.tensor
            def _(t, t_i=t_i):
                ctx.wait(t, "vec")
                i = t.matmul(PS_B[:, 1024 + t_i:1024 + t_i + 1],
                             rrow[0:1, 128 * t_i:128 * t_i + 128], onesf1[:],
                             start=True, stop=True)
                ctx.inc(i, "pe")

        @block.vector
        def _(v):
            ctx.wait(v, "pe")
            ctx.inc(v.tensor_copy(rcol[:], PS_B[:, 1024:1024 + NT]), "vec")

        w2half_sem = "wd0"
        for dhalf in range(2):
            @block.gpsimd
            def _(g, l=l, dhalf=dhalf):
                ctx.wait(g, "pe")
                srcw = dL[l]["w2"].rearrange("(j p) c -> p j c", p=128)[:, :, 512 * dhalf:512 * dhalf + 512]
                dstw = W2R[:].rearrange("p (j c) -> p j c", c=512)
                ctx.inc(g.dma_start(out=dstw, in_=srcw), "w2dma", 16)
            w2_tok = ctx.tok("w2dma")

            for sg in range(4):
                for c in range(NC):
                    slot_i = pair_flip()
                    slot = ctx.consts["WSLOTS"][slot_i]
                    wsem, tok = wload(ctx, dL[l]["w1"], D, 512 * c, 512 * c + 512,
                                      slot, slot_i, pe_guard=ctx.tok("pe"))
                    @block.tensor
                    def _(t, slot=slot, tok=tok, wsem=wsem, c=c):
                        t.wait_ge(ctx.sget(wsem), tok)
                        ctx.wait(t, "vec")
                        ctx.wait(t, "wdma")
                        for j in range(DT):
                            t.matmul(PS_B[0:1, 0:512], lg_b[:, j:j + 1],
                                     slot[:, j * 512:(j + 1) * 512],
                                     start=(j == 0), stop=(j == DT - 1))
                        for j in range(DT):
                            i = t.matmul(PS_B[0:1, 512:1024], lb_b[:, j:j + 1],
                                         slot[:, j * 512:(j + 1) * 512],
                                         start=(j == 0), stop=(j == DT - 1))
                        ctx.inc(i, "pe")

                    @block.vector
                    def _(v, c=c):
                        ctx.wait(v, "pe")
                        ctx.inc(v.tensor_copy(urow[0:1, 0:512], PS_B[0:1, 0:512]), "vec")
                        ctx.wait(v, "vec")
                        i = v.tensor_add(urow[0:1, 512:1024], PS_B[0:1, 512:1024],
                                         b1row[0:1, 512 * c:512 * c + 512])
                        ctx.inc(i, "vec")

                    @block.tensor
                    def _(t, c=c):
                        ctx.wait(t, "vec")
                        i = t.matmul(PS_B[:, 0:512], ctx.consts["ONESF128"][0:1, :],
                                     urow[0:1, 512:1024], start=True, stop=True)
                        ctx.inc(i, "pe")

                    @block.vector
                    def _(v, c=c):
                        ctx.wait(v, "pe")
                        ctx.inc(v.tensor_copy(bxp[:, 0:512], PS_B[:, 0:512]), "vec")

                    for j in range(DT):
                        @block.vector
                        def _(v, j=j, slot=slot):
                            ctx.wait(v, "vec")
                            if j == 0:
                                ctx.wait(v, "pe")
                            sl = slot[:, j * 512:(j + 1) * 512]
                            i = v.tensor_scalar(sl, sl, lg_t[:, j:j + 1], None, ALU.mult)
                            ctx.inc(i, "vec")

                    slot_vec_tok = ctx.tok("vec")
                    for ti in range(4):
                        t_i = 4 * sg + ti
                        @block.tensor
                        def _(t, t_i=t_i, c=c, slot=slot):
                            ctx.wait(t, "vec")
                            ctx.wait(t, "act")
                            s0 = 128 * t_i
                            for j in range(DT):
                                off = s0 if j >= 4 else s0 + 1
                                t.matmul(PS_A[:, 1024:1536],
                                         xT[:, j * XS + off: j * XS + off + 128],
                                         slot[:, j * 512:j * 512 + 512],
                                         start=(j == 0), stop=False)
                            i = t.matmul(PS_A[:, 1024:1536],
                                         mnr[0:1, s0:s0 + 128],
                                         urow[0:1, 0:512],
                                         start=False, stop=True)
                            ctx.inc(i, "pe")

                        @block.vector
                        def _(v, t_i=t_i):
                            ctx.wait(v, "pe")
                            i = v.scalar_tensor_tensor(ab_sb[:, 0:256], PS_A[:, 1024:1280],
                                                       rcol[:, t_i:t_i + 1], bxp[:, 0:256],
                                                       ALU.mult, ALU.add)
                            ctx.inc(i, "vec")
                            ctx.wait(v, "vec")
                            i = v.scalar_tensor_tensor(ab_sb[:, 256:512], PS_A[:, 1280:1536],
                                                       rcol[:, t_i:t_i + 1], bxp[:, 256:512],
                                                       ALU.mult, ALU.add)
                            ctx.inc(i, "vec")

                        @block.scalar
                        def _(sc, t_i=t_i):
                            ctx.wait(sc, "vec")
                            ctx.wait(sc, "act")
                            i = sc.activation(ab_sb[:, 256:512], ab_sb[:, 256:512], AF.Gelu)
                            ctx.inc(i, "act")

                        @block.vector
                        def _(v, t_i=t_i):
                            ctx.wait(v, "act")
                            i = v.tensor_mul(ab_sb[:, 0:256], ab_sb[:, 0:256],
                                             ab_sb[:, 256:512])
                            ctx.inc(i, "vec")

                        @block.tensor
                        def _(t, ti=ti, c=c):
                            ctx.wait(t, "vec")
                            for b2i in range(2):
                                i = t.matmul(PS_A[:, 1024 + 128 * b2i:1024 + 128 * b2i + 128],
                                             ab_sb[:, 128 * b2i:128 * b2i + 128],
                                             ctx.consts["IDENT"][:], start=True, stop=True,
                                             skip_group_check=True)
                            ctx.inc(i, "pe")

                        @block.vector
                        def _(v, ti=ti, c=c):
                            ctx.wait(v, "pe")
                            dsth = hT[:, (2 * c) * 512 + 128 * ti: (2 * c) * 512 + 128 * ti + 128]
                            ctx.inc(v.tensor_copy(dsth, PS_A[:, 1024:1152]), "vec")
                            ctx.wait(v, "vec")
                            dsth = hT[:, (2 * c + 1) * 512 + 128 * ti: (2 * c + 1) * 512 + 128 * ti + 128]
                            ctx.inc(v.tensor_copy(dsth, PS_A[:, 1152:1280]), "vec")

                for ti in range(4):
                    t_i = 4 * sg + ti
                    @block.tensor
                    def _(t, t_i=t_i, ti=ti, dhalf=dhalf, w2_tok=w2_tok):
                        t.wait_ge(ctx.sget("w2dma"), w2_tok)
                        ctx.wait(t, "vec")
                        ctx.wait(t, "act")
                        for j in range(22):
                            t.matmul(PS_A[:, 1536:2048],
                                     hT[:, j * 512 + 128 * ti: j * 512 + 128 * ti + 128],
                                     W2R[:, j * 512:(j + 1) * 512],
                                     start=(j == 0), stop=False)
                        i = t.matmul(PS_A[:, 1536:2048], ctx.consts["ONES"][0:1, 0:128],
                                     b2row[0:1, 512 * dhalf:512 * dhalf + 512],
                                     start=False, stop=True)
                        ctx.inc(i, "pe")

                    @block.vector
                    def _(v, t_i=t_i, dhalf=dhalf):
                        ctx.wait(v, "pe")
                        dstx = x_sb[:, t_i * D + 512 * dhalf: t_i * D + 512 * dhalf + 512]
                        ctx.inc(v.tensor_add(dstx, dstx, PS_A[:, 1536:2048]), "vec")
        mlp.close()
        if ctx.consts.get("DBG2") is not None and l == 0:
            dbg = ctx.consts["DBG2"]

            @block.sync
            def _(sync):
                for nm in ("pe", "vec", "act"):
                    ctx.wait(sync, nm)
                dstd = dbg.rearrange("(t p) d -> p t d", p=128)
                srcd = x_sb[:].rearrange("p (t d) -> p t d", d=D)
                ctx.inc(sync.dma_start(out=dstd, in_=srcd), "sdma", 16)

            @block.vector
            def _(v):
                ctx.wait(v, "sdma")
                ctx.inc(v.nop(), "vec")

    xin_b = ctx.sbuf("ff_xinb", [128, D], F32)
    for t_i in range(NT):
        @block.sync
        def _(sync, t_i=t_i):
            if t_i > 0:
                ctx.wait(sync, "vec")
            else:
                for nm in ("pe", "vec", "act"):
                    ctx.wait(sync, nm)
            if "xsp" in ctx.cnt and ctx.cnt["xsp"]:
                ctx.wait(sync, "xsp")
            src = x_in_dram.rearrange("(t p) d -> p t d", p=128)[:, t_i, :]
            ctx.inc(sync.dma_start(out=xin_b[:], in_=src), "xrl", 16)

        @block.vector
        def _(v, t_i=t_i):
            ctx.wait(v, "xrl")
            ctx.wait(v, "vec")
            dstx = x_sb[:, t_i * D:(t_i + 1) * D]
            ctx.inc(v.tensor_add(dstx, dstx, xin_b[:]), "vec")





def emit_l1ab(ctx, x_sb, w, PS_A, PS_B):
    block = ctx.block
    DS = 16
    C2 = 2048
    CT = C2 // 128
    XS2 = S + 3

    W_in = np.asarray(w["L1_attn_b_W_in"], np.float32)
    Wxi = np.ascontiguousarray(W_in[:, 0:C2])
    Wz = np.ascontiguousarray(W_in[:, C2:2 * C2])
    conv_w = np.asarray(w["L1_attn_b_conv_w"], np.float32)
    conv_b = np.asarray(w["L1_attn_b_conv_b"], np.float32)
    W_x = np.asarray(w["L1_attn_b_W_x"], np.float32)
    W_dt = np.asarray(w["L1_attn_b_W_dt"], np.float32)
    b_dt = np.asarray(w["L1_attn_b_b_dt"], np.float32)
    A_log = np.asarray(w["L1_attn_b_A_log"], np.float32)
    Dd = np.asarray(w["L1_attn_b_D"], np.float32)
    W_out = np.asarray(w["L1_attn_b_W_out"], np.float32)

    d_xi = ctx.din("w_l1ab_xi", Wxi)
    d_z = ctx.din("w_l1ab_z", Wz)
    d_out = ctx.din("w_l1ab_out", W_out)
    cw_t, _ = load_row_const(ctx, "l1ab_cw",
                             conv_w[:, 0, :].reshape(CT, 128, 4).transpose(1, 0, 2).reshape(128, CT * 4))
    cb_t, _ = load_row_const(ctx, "l1ab_cb", conv_b.reshape(CT, 128).T.copy())
    alog_r, _ = load_row_const(ctx, "l1ab_alog", A_log[None, :])
    d_col, _ = load_row_const(ctx, "l1ab_d", Dd.reshape(CT, 128).T.copy())
    onesrowb, _ = load_row_const(ctx, "l1ab_ones512", np.ones((1, 512), np.float32), BF16)
    onecolf, _ = load_row_const(ctx, "l1ab_onecol", np.ones((128, 1), np.float32))

    XI = ctx.sbuf("mb_XI", [128, CT * XS2], BF16)
    ln1 = ExitStack()
    lnT = ctx.sbuf("mb_lnT", [128, DT * S], BF16, ln1)

    emit_ln_transpose(ctx, x_sb, lnT, PS_A[:, 0:1024], scope=ln1,
                      lnrow_buf=XI[:, 0:D])

    @block.vector
    def _(v):
        ctx.wait(v, "vec")
        ctx.wait(v, "act")
        z0 = XI[:].rearrange("p (j s) -> p j s", s=XS2)[:, :, 0:3]
        ctx.inc(v.memset(z0, 0.0), "vec")
    gemm_T(ctx, d_xi, D, C2, lnT, S, XI, XS2, PS_A[:, 0:512], dst_off=3)
    ln1.close()

    scan_scope = ExitStack()
    wdt_r, _ = load_row_const(ctx, "l1ab_wdt", W_dt[:, 0][None, :], BF16,
                              scope=scan_scope)
    bdt_r, _ = load_row_const(ctx, "l1ab_bdt", b_dt[None, :], BF16,
                              scope=scan_scope)
    wb_sb, _ = load_row_const(ctx, "l1ab_wb",
                              W_x[:, 1:17].reshape(CT, 128, 16).transpose(1, 0, 2).reshape(128, CT * 16),
                              BF16, scope=scan_scope)
    wc_sb, _ = load_row_const(ctx, "l1ab_wc",
                              W_x[:, 17:33].reshape(CT, 128, 16).transpose(1, 0, 2).reshape(128, CT * 16),
                              BF16, scope=scan_scope)
    wd_sb, _ = load_row_const(ctx, "l1ab_wd",
                              W_x[:, 0:1].reshape(CT, 128).T.copy(), BF16,
                              scope=scan_scope)
    selst_np = np.zeros((16, 16 * 128), np.float32)
    for st in range(16):
        selst_np[st, 128 * st:128 * (st + 1)] = 1.0
    selst, _ = load_row_const(ctx, "l1ab_selst", selst_np, BF16, scope=scan_scope)
    ydl = ctx.sbuf("mb_y", [128, S], F32, scan_scope)
    dlt = ctx.sbuf("mb_dlt", [128, S], BF16, scan_scope)
    wdx = ctx.sbuf("mb_wdx", [128, S], BF16, scan_scope)
    abuf = ctx.sbuf("mb_a", [128, S], BF16, scan_scope)
    bbuf = ctx.sbuf("mb_b", [128, S], BF16, scan_scope)
    hbuf = ctx.sbuf("mb_h", [128, S], BF16, scan_scope)
    tbuf = ctx.sbuf("mb_t", [128, S], BF16, scan_scope)
    cacc = ctx.sbuf("mb_cacc", [128, 512], F32, scan_scope)
    h0 = ctx.sbuf("mb_h0", [128, 1], F32, scan_scope)
    XP0 = ctx.sbuf("mb_xp0", [1, S], BF16, scan_scope)
    XPB = ctx.sbuf("mb_xpb", [16, S], BF16, scan_scope)
    XPC = ctx.sbuf("mb_xpc", [16, S], BF16, scan_scope)
    abc = ctx.sbuf("mb_abc", [128, DS], F32, scan_scope)

    for cch in (3, 2, 1, 0):
        for j in range(CT):
            @block.vector
            def _(v, cch=cch, j=j):
                ctx.wait(v, "act")
                ctx.wait(v, "vec")
                base = j * XS2 + 512 * cch
                i = v.tensor_scalar(cacc[:], XI[:, base:base + 512],
                                    cw_t[:, 4 * j:4 * j + 1], None, ALU.mult)
                ctx.inc(i, "vec")
                for kk in (1, 2, 3):
                    ctx.wait(v, "vec")
                    i = v.scalar_tensor_tensor(cacc[:], XI[:, base + kk:base + kk + 512],
                                               cw_t[:, 4 * j + kk:4 * j + kk + 1],
                                               cacc[:], ALU.mult, ALU.add)
                    ctx.inc(i, "vec")

            @block.scalar
            def _(sc, cch=cch, j=j):
                ctx.wait(sc, "vec")
                ctx.wait(sc, "act")
                i = sc.activation(XI[:, j * XS2 + 3 + 512 * cch: j * XS2 + 3 + 512 * cch + 512],
                                  cacc[:], AF.Silu, bias=cb_t[:, j:j + 1])
                ctx.inc(i, "act")

    for scn in range(4):
        @block.tensor
        def _(t, scn=scn):
            ctx.wait(t, "act")
            ctx.wait(t, "vec")
            ctx.wait(t, "wdma")
            sl = slice(512 * scn, 512 * scn + 512)
            for j in range(CT):
                xi_sl = XI[:, j * XS2 + 3 + 512 * scn: j * XS2 + 3 + 512 * scn + 512]
                t.matmul(PS_A[0:1, 0:512], wd_sb[:, j:j + 1], xi_sl,
                         start=(j == 0), stop=(j == CT - 1))
                t.matmul(PS_A[0:16, 512:1024], wb_sb[:, 16 * j:16 * (j + 1)], xi_sl,
                         start=(j == 0), stop=(j == CT - 1))
                i = t.matmul(PS_A[0:16, 1024:1536], wc_sb[:, 16 * j:16 * (j + 1)], xi_sl,
                             start=(j == 0), stop=(j == CT - 1))
            ctx.inc(i, "pe")

        @block.scalar
        def _(sc, scn=scn):
            ctx.wait(sc, "pe")
            sl = slice(512 * scn, 512 * scn + 512)
            sc.activation(XP0[0:1, sl], PS_A[0:1, 0:512], AF.Identity)
            sc.activation(XPB[0:16, sl], PS_A[0:16, 512:1024], AF.Identity)
            i = sc.activation(XPC[0:16, sl], PS_A[0:16, 1024:1536], AF.Identity)
            ctx.inc(i, "act")

    @block.scalar
    def _(sc):
        ctx.wait(sc, "wdma")
        ctx.wait(sc, "act")
        ctx.inc(sc.activation(alog_r[:], alog_r[:], AF.Exp), "act")

    @block.tensor
    def _(t):
        ctx.wait(t, "act")
        ctx.wait(t, "vec")
        i = t.matmul(PS_B[:, 0:16], ctx.consts["ONESF128"][0:1, :], alog_r[:],
                     start=True, stop=True)
        ctx.inc(i, "pe")

    @block.vector
    def _(v):
        ctx.wait(v, "pe")
        ctx.inc(v.tensor_scalar_mul(abc[:], PS_B[:, 0:16], -1.0), "vec")

    for ct in range(CT):
        for cch in range(4):
            @block.tensor
            def _(t, ct=ct, cch=cch):
                ctx.wait(t, "act")
                ctx.wait(t, "vec")
                t.matmul(PS_A[:, 512 * cch:512 * cch + 512],
                         wdt_r[0:1, 128 * ct:128 * ct + 128],
                         XP0[0:1, 512 * cch:512 * cch + 512],
                         start=True, stop=False)
                i = t.matmul(PS_A[:, 512 * cch:512 * cch + 512],
                             bdt_r[0:1, 128 * ct:128 * ct + 128],
                             onesrowb[0:1, 0:512],
                             start=False, stop=True)
                ctx.inc(i, "pe")

        @block.scalar
        def _(sc, ct=ct):
            ctx.wait(sc, "pe")
            ctx.wait(sc, "act")
            ctx.wait(sc, "vec")
            i = sc.activation(tbuf[:], PS_A[:, 0:S], AF.Abs)
            ctx.inc(i, "act")
            ctx.wait(sc, "act")
            i = sc.activation(tbuf[:], tbuf[:], AF.Exp, scale=-1.0)
            ctx.inc(i, "act")
            ctx.wait(sc, "act")
            i = sc.activation(tbuf[:], tbuf[:], AF.Ln, bias=onecolf[:, 0:1])
            ctx.inc(i, "act")

        @block.vector
        def _(v, ct=ct):
            ctx.wait(v, "act")
            i = v.tensor_scalar_max(dlt[:], PS_A[:, 0:S], 0.0)
            ctx.inc(i, "vec")
            ctx.wait(v, "vec")
            i = v.tensor_add(dlt[:], dlt[:], tbuf[:])
            ctx.inc(i, "vec")
            ctx.wait(v, "vec")
            i = v.tensor_mul(wdx[:], dlt[:], XI[:, ct * XS2 + 3: ct * XS2 + 3 + S])
            ctx.inc(i, "vec")
            ctx.wait(v, "vec")
            ctx.inc(v.memset(ydl[:], 0.0), "vec")
            ctx.wait(v, "vec")
            ctx.inc(v.memset(h0[:], 0.0), "vec")

        for st in range(DS):
            @block.scalar
            def _(sc, ct=ct, st=st):
                ctx.wait(sc, "vec")
                ctx.wait(sc, "act")
                ctx.wait(sc, "pe")
                i = sc.activation(abuf[:], dlt[:], AF.Exp, scale=abc[:, st:st + 1])
                ctx.inc(i, "act")

            for cch in range(4):
                @block.tensor
                def _(t, ct=ct, st=st, cch=cch):
                    ctx.wait(t, "vec")
                    ctx.wait(t, "act")
                    i = t.matmul(PS_B[:, 512 * cch:512 * cch + 512],
                                 selst[0:16, 128 * st:128 * st + 128],
                                 XPB[0:16, 512 * cch:512 * cch + 512],
                                 start=True, stop=True)
                    ctx.inc(i, "pe")

            @block.vector
            def _(v, ct=ct, st=st):
                ctx.wait(v, "pe")
                ctx.wait(v, "vec")
                i = v.tensor_mul(bbuf[:], wdx[:], PS_B[:, 0:S])
                ctx.inc(i, "vec")
                ctx.wait(v, "vec")
                ctx.wait(v, "act")
                i = v.tensor_tensor_scan(hbuf[:], abuf[:], bbuf[:], h0[:],
                                         ALU.mult, ALU.add)
                ctx.inc(i, "vec")

            for cch in range(4):
                @block.tensor
                def _(t, ct=ct, st=st, cch=cch):
                    ctx.wait(t, "vec")
                    i = t.matmul(PS_B[:, 512 * cch:512 * cch + 512],
                                 selst[0:16, 128 * st:128 * st + 128],
                                 XPC[0:16, 512 * cch:512 * cch + 512],
                                 start=True, stop=True)
                    ctx.inc(i, "pe")

            @block.vector
            def _(v, ct=ct, st=st):
                ctx.wait(v, "pe")
                ctx.wait(v, "vec")
                i = v.tensor_mul(tbuf[:], hbuf[:], PS_B[:, 0:S])
                ctx.inc(i, "vec")
                ctx.wait(v, "vec")
                i = v.tensor_add(ydl[:], ydl[:], tbuf[:])
                ctx.inc(i, "vec")

        @block.vector
        def _(v, ct=ct):
            ctx.wait(v, "vec")
            i = v.scalar_tensor_tensor(ydl[:], XI[:, ct * XS2 + 3: ct * XS2 + 3 + S],
                                       d_col[:, ct:ct + 1], ydl[:], ALU.mult, ALU.add)
            ctx.inc(i, "vec")
            ctx.wait(v, "vec")
            ctx.inc(v.tensor_copy(XI[:, ct * XS2 + 3: ct * XS2 + 3 + S], ydl[:]), "vec")

    scan_scope.close()
    zph = ExitStack()
    zbuf = ctx.sbuf("mb_zbuf", [128, CT * 512], BF16, zph)
    lnT2 = lnT
    lnTz = ctx.sbuf("mb_lnTz", [128, DT * S], BF16, zph)
    emit_ln_transpose(ctx, x_sb, lnTz, PS_A[:, 0:1024], scope=zph,
                      lnrow_buf=zbuf[:, 0:D])

    for scn in range(4):
        gemm_T(ctx, d_z, D, C2, lnTz, S, zbuf, 512, PS_A[:, 0:512],
               s_lo=512 * scn, s_hi=512 * scn + 512)
        for j in range(CT):
            @block.scalar
            def _(sc, scn=scn, j=j):
                ctx.wait(sc, "act")
                ctx.wait(sc, "pe")
                sl = zbuf[:, j * 512:(j + 1) * 512]
                i = sc.activation(sl, sl, AF.Silu)
                ctx.inc(i, "act")

            @block.vector
            def _(v, scn=scn, j=j):
                ctx.wait(v, "act")
                ctx.wait(v, "vec")
                sl = zbuf[:, j * 512:(j + 1) * 512]
                i = v.tensor_mul(sl, sl, XI[:, j * XS2 + 3 + 512 * scn: j * XS2 + 3 + 512 * scn + 512])
                ctx.inc(i, "vec")

        WS = ctx.consts["WSLOTS"]
        w_cols = 256
        for li, c0 in enumerate(range(0, D, w_cols)):
            slot = WS[li % 2]
            wsem, tok = wload(ctx, d_out, C2, c0, c0 + w_cols, slot, li % 2,
                              pe_guard=ctx.tok("pe"))
            for ti in range(4):
                t_i = 4 * scn + ti

                @block.tensor
                def _(t, t_i=t_i, ti=ti, c0=c0, slot=slot, tok=tok, wsem=wsem):
                    t.wait_ge(ctx.sget(wsem), tok)
                    ctx.wait(t, "vec")
                    ctx.wait(t, "act")
                    for j in range(CT):
                        i = t.matmul(PS_A[:, 1536:1536 + w_cols],
                                     zbuf[:, j * 512 + 128 * ti: j * 512 + 128 * ti + 128],
                                     slot[:, j * w_cols:(j + 1) * w_cols],
                                     start=(j == 0), stop=(j == CT - 1))
                    ctx.inc(i, "pe")

                @block.vector
                def _(v, t_i=t_i, c0=c0):
                    ctx.wait(v, "pe")
                    dstx = x_sb[:, t_i * D + c0: t_i * D + c0 + w_cols]
                    i = v.tensor_add(dstx, dstx, PS_A[:, 1536:1536 + w_cols])
                    ctx.inc(i, "vec")
    zph.close()



def build_single_sublayer_program(name, weights):
    ctx = Ctx()
    with ctx.stack:
        x_in = ctx.nc.dram_tensor("x_in", [S, D], F32, kind="ExternalInput").ap()
        y_out = ctx.dout("y_out", [S, D], F32)
        x_sb = ctx.sbuf("x_sb", [128, NT * D], F32)
        PS_A = ctx.psum([128, 2048], F32)
        PS_B = ctx.psum([128, 2048], F32)
        wsc = 4096 if (name.endswith("ffn") or name == "l1ab") else WSLOT_COLS
        ctx.consts["WSLOT_COLS"] = wsc
        ws0 = ctx.sbuf("WSLOT0", [128, wsc], BF16)
        ws1 = ctx.sbuf("WSLOT1", [128, wsc], BF16)
        ctx.consts["WSLOTS"] = (ws0, ws1)
        import os
        if os.environ.get("KTR"):
            ctx.consts["KTR"] = os.environ["KTR"]
        if os.environ.get("KDBG"):
            ctx.consts["DBG"] = ctx.dout("dbg1", [S, D], F32)
            ctx.consts["DBG2"] = ctx.dout("dbg2", [S, D], F32)
            ctx.consts["DBGQ"] = ctx.nc.dram_tensor("dbgq", [128, 4 * 512], mybir.dt.bfloat16, kind="ExternalOutput").ap()
            ctx.consts["DBGK"] = ctx.nc.dram_tensor("dbgk", [128, 4 * 512], mybir.dt.bfloat16, kind="ExternalOutput").ap()
            ctx.consts["DBGO"] = ctx.nc.dram_tensor("dbgo", [128, 2 * 4 * 512], mybir.dt.bfloat16, kind="ExternalOutput").ap()
            ctx.consts["DBGS"] = ctx.dout("dbgs", [128, 16], F32)
            ctx.consts["DBGV"] = ctx.nc.dram_tensor("dbgv", [128, 8 * 512], mybir.dt.bfloat16, kind="ExternalOutput").ap()
            ctx.consts["DBGE"] = ctx.nc.dram_tensor("dbge", [128, 512], mybir.dt.bfloat16, kind="ExternalOutput").ap()
        with ctx.nc.Block() as block:
            ctx.block = block
            emit_load_consts(ctx, need_identf=(name not in ("l0aa", "l1ab")))
            emit_load_x(ctx, x_in, x_sb)
            if name.endswith("ffn"):
                emit_ffn(ctx, x_sb, weights, name[:2].upper().replace("L", "L") and
                         ("L0_ffn" if name == "l0ffn" else "L1_ffn"),
                         PS_A, PS_B, x_in)
            else:
                EMITTERS[name](ctx, x_sb, weights, PS_A, PS_B)
            emit_store_x(ctx, x_sb, y_out)
        ctx.nc.finalize()
    return ctx


EMITTERS = {"l0aa": emit_l0aa, "l0ab": emit_l0ab, "l1aa": emit_l1aa,
            "l1ab": emit_l1ab}


def emit_ffn_launch(ctx, x_sb, weights, PS_A, PS_B, pfx, x_in):
    emit_ffn(ctx, x_sb, weights, pfx, PS_A, PS_B, x_in)


def emit_barrier(ctx):
    block = ctx.block

    @block.tensor
    def _(t):
        ctx.wait(t, "vec")
        ctx.wait(t, "act")
        ctx.inc(t.nop(), "pe")

    @block.vector
    def _(v):
        ctx.wait(v, "pe")
        ctx.wait(v, "act")
        ctx.inc(v.nop(), "vec")

    @block.scalar
    def _(sc):
        ctx.wait(sc, "pe")
        ctx.wait(sc, "vec")
        ctx.inc(sc.nop(), "act")


def build_group_program(names, weights):
    ctx = Ctx()
    with ctx.stack:
        x_in = ctx.nc.dram_tensor("x_in", [S, D], F32, kind="ExternalInput").ap()
        y_out = ctx.dout("y_out", [S, D], F32)
        x_sb = ctx.sbuf("x_sb", [128, NT * D], F32)
        PS_A = ctx.psum([128, 2048], F32)
        PS_B = ctx.psum([128, 2048], F32)
        with ctx.nc.Block() as block:
            ctx.block = block
            emit_load_consts(ctx, need_identf=False)
            emit_load_x(ctx, x_in, x_sb)
            for i, name in enumerate(names):
                if i > 0:
                    emit_barrier(ctx)
                scope = ExitStack()
                ctx.cur_scope = scope
                wsc = 4096 if (name.endswith("ffn") or name == "l1ab") else 2048
                ctx.consts["WSLOT_COLS"] = wsc
                ws0 = ctx.sbuf("WSLOT0", [128, wsc], BF16, scope)
                ws1 = ctx.sbuf("WSLOT1", [128, wsc], BF16, scope)
                ctx.consts["WSLOTS"] = (ws0, ws1)
                if name not in ("l0aa", "l1ab"):
                    identf = ctx.sbuf("IDENTF", [128, 128], F32, scope)
                    ctx.consts["IDENTF"] = identf
                    idd = ctx.din("c_ident", np.eye(128, dtype=np.float32))

                    @block.gpsimd
                    def _(g):
                        for nm in ("pe", "vec", "act"):
                            if ctx.cnt.get(nm):
                                ctx.wait(g, nm)
                        ctx.inc(g.dma_start(out=identf[:], in_=idd[:]), "wdma", 16)

                    @block.tensor
                    def _(t):
                        ctx.wait(t, "wdma")
                        ctx.inc(t.nop(), "pe")
                if name.endswith("ffn"):
                    pfx = "L0_ffn" if name == "l0ffn" else "L1_ffn"
                    xsp = ctx.nc.dram_tensor(f"xsp_{i}", [S, D], F32,
                                             kind="Internal").ap()

                    @block.sync
                    def _(sync):
                        for nm in ("pe", "vec", "act"):
                            if ctx.cnt.get(nm):
                                ctx.wait(sync, nm)
                        dst = xsp.rearrange("(t p) d -> p t d", p=128)
                        srcs = x_sb[:].rearrange("p (t d) -> p t d", d=D)
                        ctx.inc(sync.dma_start(out=dst, in_=srcs), "xsp", 16)

                    @block.vector
                    def _(v):
                        ctx.wait(v, "xsp")
                        ctx.inc(v.nop(), "vec")

                    @block.scalar
                    def _(sc):
                        ctx.wait(sc, "xsp")
                        ctx.inc(sc.nop(), "act")
                    emit_ffn(ctx, x_sb, weights, pfx, PS_A, PS_B, xsp)
                else:
                    EMITTERS[name](ctx, x_sb, weights, PS_A, PS_B)
                ctx.cur_scope = None
                scope.close()
            emit_store_x(ctx, x_sb, y_out)
        ctx.nc.finalize()
    return ctx


CHAIN = ["l0aa", "l0ffn", "l0ab", "l0ffn", "l1aa", "l1ffn", "l1ab", "l1ffn"]


def kernel(**inputs):
    import os
    x = np.ascontiguousarray(np.asarray(inputs["L0_attn_a_x"], np.float32))
    B = x.shape[0]
    ng = int(os.environ.get("KGROUPS") or 1)
    if ng >= len(CHAIN):
        groups = [[n] for n in CHAIN]
    else:
        per = (len(CHAIN) + ng - 1) // ng
        groups = [CHAIN[i:i + per] for i in range(0, len(CHAIN), per)]
    x_cur = x
    progs = {}
    for g in groups:
        key = ",".join(g)
        if key not in progs:
            if len(g) == 1:
                progs[key] = build_single_sublayer_program(g[0], inputs)
            else:
                progs[key] = build_group_program(g, inputs)
    for g in groups:
        ctx = progs[",".join(g)]
        in_maps = []
        for b in range(B):
            m = dict(ctx.inputs)
            m["x_in"] = np.ascontiguousarray(x_cur[b])
            in_maps.append(m)
        res = run_bass_kernel_spmd(ctx.nc, in_maps, core_ids=list(range(B)))
        x_cur = np.stack([r["y_out"] for r in res.results])
    return x_cur
